# version 04
# baseline (speedup 1.0000x reference)
import contextlib
import numpy as np

import concourse.bass as bass
import concourse.mybir as mybir
from concourse.bass_utils import run_bass_kernel_spmd

F32 = mybir.dt.float32
BF16 = mybir.dt.bfloat16
AF = mybir.ActivationFunctionType
ALU = mybir.AluOpType
AX = mybir.AxisListType

S = 2048
D = 1024
E = 8
TEMP = 1.5
EPS = 1e-5
SCALE = 1.0 / 32.0
BIG = 30000.0

TC = 16
QS = 4
JC = 16
NG = 4
NPAIR = 2
TS = 4
FBLK = 8
FC = 32
NCV = 128 + 256


def build_core(dbg=0):
    nc = bass.Bass("TRN2", target_bir_lowering=False)

    x_in = nc.dram_tensor("x", [2, S, D], F32, kind="ExternalInput").ap()
    wqkv = nc.dram_tensor("wqkv", [3 * D, D], F32, kind="ExternalInput").ap()
    mqkv = nc.dram_tensor("mqkv", [3 * D, D], F32, kind="ExternalInput").ap()
    wo = nc.dram_tensor("wo", [D, D], F32, kind="ExternalInput").ap()
    mo = nc.dram_tensor("mo", [D, D], F32, kind="ExternalInput").ap()
    bo = nc.dram_tensor("bo", [1, D], F32, kind="ExternalInput").ap()
    mbo = nc.dram_tensor("mbo", [1, D], F32, kind="ExternalInput").ap()
    gamma = nc.dram_tensor("gamma", [1, D], F32, kind="ExternalInput").ap()
    beta = nc.dram_tensor("beta", [1, D], F32, kind="ExternalInput").ap()
    rw = nc.dram_tensor("rw", [E, D], F32, kind="ExternalInput").ap()
    w1 = nc.dram_tensor("w1", [E, D, 4096], F32, kind="ExternalInput").ap()
    w2 = nc.dram_tensor("w2", [E, 4096, D], F32, kind="ExternalInput").ap()
    out = nc.dram_tensor("out", [2, S, D], F32, kind="ExternalOutput").ap()
    if dbg:
        d_r1 = nc.dram_tensor("d_r1", [128, TC, D], F32, kind="ExternalOutput").ap()
        d_attn = nc.dram_tensor("d_attn", [128, 8, S], F32, kind="ExternalOutput").ap()
        d_q = nc.dram_tensor("d_q", [128, 2, S], F32, kind="ExternalOutput").ap()
        d_k = nc.dram_tensor("d_k", [128, 2, S], F32, kind="ExternalOutput").ap()
        d_v = nc.dram_tensor("d_v", [128, 16, 4, 64], F32, kind="ExternalOutput").ap()
        d_mh = nc.dram_tensor("d_mh", [128, 2], F32, kind="ExternalOutput").ap()
        d_xn = nc.dram_tensor("d_xn", [128, 8, S], F32, kind="ExternalOutput").ap()
        d_wd = nc.dram_tensor("d_wd", [128, TC, E], F32, kind="ExternalOutput").ap()
        d_acc2 = nc.dram_tensor("d_acc2", [128, TC, D], F32, kind="ExternalOutput").ap()
        d_bb = nc.dram_tensor("d_bb", [128, 3, D], F32, kind="ExternalOutput").ap()
        d_ln = nc.dram_tensor("d_ln", [128, 8], F32, kind="ExternalOutput").ap()
        d_rt = nc.dram_tensor("d_rt", [128, 2, D], F32, kind="ExternalOutput").ap()
        d_nq = nc.dram_tensor("d_nq", [1, 32], F32, kind="ExternalOutput").ap()
        d_lg = nc.dram_tensor("d_lg", [128, 16, 8], F32, kind="ExternalOutput").ap()

    wqkvT_s = nc.dram_tensor("wqkvT_s", [128, 8, 3 * D], BF16, kind="Internal").ap()
    woT_s = nc.dram_tensor("woT_s", [128, 8, D], BF16, kind="Internal").ap()
    xfm_s = nc.dram_tensor("xfm_s", [128, 8, S], BF16, kind="Internal").ap()
    w1s = nc.dram_tensor("w1s", [E, FBLK, 128, 8, 512], BF16, kind="Internal").ap()
    w2s = nc.dram_tensor("w2s", [E, FC, 128, D], BF16, kind="Internal").ap()

    def nbytes(shape, dtype):
        nb = 1
        for x in shape[1:]:
            nb *= x
        return nb * (4 if dtype == F32 else 2)

    cursor = [26624]

    def al(name, shape, dtype, align=64):
        cursor[0] = -(-cursor[0] // align) * align
        t = nc.alloc_sbuf_tensor_at(name, list(shape), dtype, offset=cursor[0])
        cursor[0] += nbytes(shape, dtype)
        return t

    acc = al("acc", [128, TC, D], F32)
    bias_b = al("bias_b", [128, D], F32)
    gamma_b = al("gamma_b", [128, D], F32)
    beta_b = al("beta_b", [128, D], F32)
    router_T = al("router_T", [128, 8, E], F32)
    ident_f = al("ident_f", [128, 128], F32)
    ones_b = al("ones_b", [128, 128], BF16)
    ones_f = al("ones_f", [33, 128], F32)
    normones = al("normones", [128, 2], BF16)
    normf = al("normf", [128, 2, 128], BF16)
    nqb128 = al("nqb128", [128, 16], F32)
    mt = al("mt", [128, 2], F32)
    epsb = al("epsb", [128, 1], F32)
    scjunk = al("scjunk", [128, 192], F32)
    cl2 = al("cl2", [128, 2], F32)
    rt_s = al("rt_s", [128, 2], F32)
    wdense = al("wdense", [128, TC, E], F32)
    mhatb = al("mhatb", [128, 2], F32)

    rsi = al("rsi", [33, 512], F32)
    bc_sb = al("bc_sb", [128, 512], F32)
    nq = al("nq", [1, 32], F32)
    sm1 = al("sm1", [1, 8], F32)
    smb = al("smb", [1, 8], BF16)
    red = al("red", [128, 24], F32)
    lnm = al("lnm", [128, 2], F32)
    nqb = al("nqb", [1, 16], F32)
    mv = al("mv", [128, 2], F32)
    lnt = al("lnt", [128, 4], F32)
    r_m1 = al("r_m1", [128, 4], F32)
    r_pk = al("r_pk", [128, 4, E], F32)
    assert cursor[0] <= 117248, cursor[0]

    PH = 117248
    TOP = 229344

    def at(name, shape, dtype, off):
        assert PH + off + nbytes(shape, dtype) <= TOP, name
        return nc.alloc_sbuf_tensor_at(name, list(shape), dtype, offset=PH + off)

    a_xslab = at("a_xslab", [128, 2, 8, 512], BF16, 0)
    a_wg = at("a_wg", [128, 2, 8, 768], BF16, 16384)
    a_q = at("a_q", [128, 2, S], BF16, 40960)
    a_k = at("a_k", [128, 2, S], BF16, 49152)
    a_v = at("a_v", [128, 16, 4, 64], BF16, 57344)
    a_attn = at("a_attn", [128, 8, S], BF16, 65664)
    a_xstage = at("a_xstage", [128, 2, D], F32, 98432)
    a_xcast = at("a_xcast", [128, 2, D], BF16, 106624)
    sqbuf = at("sqbuf", [128, 2, 2048], BF16, 98432)
    P_T = at("P_T", [128, 4, 2, 512], BF16, 98432)
    a_fmtile = at("a_fmtile", [128, 2, 8, 128], BF16, 16384)
    woT = at("woT", [128, 8, D], BF16, 40960)
    r_tmp = at("r_tmp", [128, 2, D], F32, 0)
    o_lnscr = at("o_lnscr", [128, D], F32, 8192)
    f_xn2fm = at("f_xn2fm", [128, 8, S], BF16, 0)
    f_h = at("f_h", [128, FC, 512], BF16, 32768)
    f_w1 = at("f_w1", [128, 2, 8, 512], BF16, 65664)
    f_w2 = at("f_w2", [128, 4, D], BF16, 82048)
    f_xn2tm = at("f_xn2tm", [128, 2, D], F32, 90240)
    f_out = at("f_out", [128, 2, D], F32, 90240)
    f_xnf = at("f_xnf", [128, 2, 128], F32, 107520)
    gp_j = at("gp_j", [128, 128], F32, 32768)
    lgd = at("lgd", [128, 16, 8], F32, 33280)
    f_xnlc = at("f_xnlc", [128, 2, D], BF16, 33792)
    f_xnlo = at("f_xnlo", [128, 2, 8, 128], BF16, 37888)
    rwhl = at("rwhl", [128, 2, 8, E], BF16, 42496)
    f_lnscr = at("f_lnscr", [128, D], F32, 98432)
    f_xncast = at("f_xncast", [128, 2, D], BF16, 103424)
    p_w = at("p_w", [128, 2, D], F32, 40960)
    p_m = at("p_m", [128, 2, D], F32, 49152)
    p_gt = at("p_gt", [128, 2, D], BF16, 57344)
    p_gated = at("p_gated", [128, 2, D], BF16, 61440)
    p_t = at("p_t", [128, 2, 8, 128], BF16, 65664)
    c_in = at("c_in", [128, 2, 2048], F32, 69760)
    c_out = at("c_out", [128, 2, 2048], BF16, 86144)
    p_vec = at("p_vec", [8, 6 * 1024], F32, 0)

    ps = nc.alloc_psum_tensor("ps", [128, 8, 512], F32)

    es = contextlib.ExitStack()
    SEMS = {}

    def sem(n):
        SEMS[n] = es.enter_context(nc.semaphore(n))
        return SEMS[n]

    for n in ("sd_misc sd_pw sd_pt sd_cvin sd_cvout sd_x sd_xt sd_xs sd_slab "
              "sd_wg sd_wo sd_res sd_xnt sd_w1 sd_w2 sd_out sv_init sv_gate "
              "sv_cv sv_pb sv_xc st_proj sv_qkv st_s ss_exp st_pv sv_rec "
              "st_bc sv_attn st_nq sv_nq sv_nb sv_sq st_op sv_r1 st_tp sv_xnf "
              "st_lg sv_wd st_h ss_h st_y sv_acc sv_out sv_xn st_rtp sv_ln "
              "ss_ln ss_wd ss_nb sg_wd").split():
        sem(n)
    g = SEMS

    CNT = {}

    def add(name, n):
        CNT[name] = CNT.get(name, 0) + n

    def cur(name):
        return CNT.get(name, 0)

    blk = es.enter_context(nc.Block())

    def p0_sync(sy):
        sy.dma_start(out=p_vec[0:1, 0:D], in_=bo).then_inc(g["sd_misc"], 16)
        sy.dma_start(out=p_vec[0:1, D:2 * D], in_=mbo).then_inc(g["sd_misc"], 16)
        sy.dma_start(out=p_vec[0:1, 2 * D:3 * D], in_=gamma).then_inc(g["sd_misc"], 16)
        sy.dma_start(out=p_vec[0:1, 3 * D:4 * D], in_=beta).then_inc(g["sd_misc"], 16)
        sy.dma_start(out=p_vec[0:8, 4 * D:5 * D], in_=rw).then_inc(g["sd_misc"], 16)
        for oc in range(32):
            b = oc % 2
            if oc >= 2:
                sy.wait_ge(g["sd_pt"], 16 * 9 * (oc - 1))
            wsrc = wqkv[oc * 128:(oc + 1) * 128, :] if oc < 24 else \
                wo[(oc - 24) * 128:(oc - 23) * 128, :]
            msrc = mqkv[oc * 128:(oc + 1) * 128, :] if oc < 24 else \
                mo[(oc - 24) * 128:(oc - 23) * 128, :]
            sy.dma_start(out=p_w[:, b, :], in_=wsrc).then_inc(g["sd_pw"], 16)
            sy.dma_start(out=p_m[:, b, :], in_=msrc).then_inc(g["sd_pw"], 16)
            sy.wait_ge(g["sv_gate"], oc + 1)
            for kc in range(8):
                sy.dma_start_transpose(
                    p_t[:, b, kc, :], p_gated[:, b, kc * 128:(kc + 1) * 128]
                ).then_inc(g["sd_pt"], 16)
            sy.wait_ge(g["sd_pt"], 16 * (9 * oc + 8))
            dst = wqkvT_s[:, :, oc * 128:(oc + 1) * 128] if oc < 24 else \
                woT_s[:, :, (oc - 24) * 128:(oc - 23) * 128]
            sy.dma_start(out=dst, in_=p_t[:, b, :, :]).then_inc(g["sd_pt"], 16)

    def p0_gp(gp):
        gp.memset(ident_f[:], 0.0)
        gp.affine_select(out=ident_f[:], in_=ident_f[:],
                         compare_op=ALU.not_equal, fill=1.0, base=0,
                         pattern=[[-1, 128]], channel_multiplier=1)
        gp.memset(ones_b[:], 1.0)
        gp.memset(ones_f[:], 1.0)
        gp.memset(normones[:], 0.0)
        gp.memset(normones[0:64, 0:1], 1.0)
        gp.memset(normones[64:128, 1:2], 1.0)
        gp.memset(epsb[:], EPS)
        gp.memset(mhatb[:], 0.0)
        gp.memset(red[:], 1.0)
        gp.memset(nqb128[:], 1.0)
        gp.memset(mt[:], 1.0)
        gp.memset(lnm[:], 0.0)
        gp.memset(lnt[:], 1.0)
        gp.memset(r_m1[:], 0.0)
        gp.memset(r_pk[:], 0.0)
        gp.memset(bc_sb[:], 0.0)
        gp.memset(cl2[:, 0:1], 60.0)
        gp.memset(cl2[:, 1:2], -60.0)
        gp.memset(normf[:], 0.0)
        gp.memset(normf[0:64, 0, :], 1.0)
        gp.memset(normf[64:128, 1, :], 1.0)
        gp.engine_nop().then_inc(g["sv_init"], 1)
        u = 0
        for e in range(E):
            for dc in range(8):
                for hf in range(2):
                    b = u % 2
                    if u >= 2:
                        gp.wait_ge(g["sd_cvout"], 16 * (u - 1))
                    gp.dma_start(
                        out=c_in[:, b, :],
                        in_=w1[e, dc * 128:(dc + 1) * 128,
                               hf * 2048:(hf + 1) * 2048]).then_inc(
                        g["sd_cvin"], 16)
                    gp.wait_ge(g["sv_cv"], u + 1)
                    gp.dma_start(
                        out=w1s[e, 4 * hf:4 * hf + 4, :, dc, :].rearrange(
                            "b p f -> p b f"),
                        in_=c_out[:, b, :].rearrange("p (b f) -> p b f", b=4)
                    ).then_inc(g["sd_cvout"], 16)
                    u += 1
        for e in range(E):
            for fc in range(FC):
                b = u % 2
                gp.wait_ge(g["sd_cvout"], 16 * (u - 1))
                gp.dma_start(out=c_in[:, b, 0:D],
                             in_=w2[e, fc * 128:(fc + 1) * 128, :]).then_inc(
                    g["sd_cvin"], 16)
                gp.wait_ge(g["sv_cv"], u + 1)
                gp.dma_start(out=w2s[e, fc], in_=c_out[:, b, 0:D]).then_inc(
                    g["sd_cvout"], 16)
                u += 1

    def p0_vec(ve):
        for oc in range(32):
            b = oc % 2
            if oc >= 2:
                ve.wait_ge(g["sd_pt"], 16 * (9 * (oc - 2) + 8))
            ve.wait_ge(g["sd_pw"], 16 * 2 * (oc + 1))
            ve.tensor_scalar(out=p_gt[:, b, :], in0=p_m[:, b, :], scalar1=0.0,
                             scalar2=None, op0=ALU.is_gt)
            ve.tensor_tensor(out=p_gated[:, b, :], in0=p_w[:, b, :],
                             in1=p_gt[:, b, :], op=ALU.mult).then_inc(
                g["sv_gate"], 1)
        for u in range(NCV):
            b = u % 2
            ve.wait_ge(g["sd_cvin"], 16 * (u + 1))
            src = c_in[:, b, :] if u < 128 else c_in[:, b, 0:D]
            dst = c_out[:, b, :] if u < 128 else c_out[:, b, 0:D]
            ve.tensor_copy(out=dst, in_=src).then_inc(g["sv_cv"], 1)
        ve.wait_ge(g["sd_misc"], 16 * 5)
        ve.tensor_scalar(out=p_vec[0:1, 5 * D:6 * D], in0=p_vec[0:1, D:2 * D],
                         scalar1=0.0, scalar2=None, op0=ALU.is_gt)
        ve.tensor_tensor(out=p_vec[0:1, 0:D], in0=p_vec[0:1, 0:D],
                         in1=p_vec[0:1, 5 * D:6 * D], op=ALU.mult).then_inc(
            g["sv_pb"], 1)
        for i in range(6):
            ve.wait_ge(g["st_rtp"], i + 1)
            dstt = (bias_b, bias_b, gamma_b, gamma_b, beta_b, beta_b)[i]
            ve.tensor_copy(out=dstt[:, (i % 2) * 512:(i % 2 + 1) * 512],
                           in_=ps[:, i % 2, :]).then_inc(g["sv_pb"], 1)
        for dc in range(8):
            ve.wait_ge(g["st_rtp"], 7 + dc)
            ve.tensor_copy(out=router_T[:, dc, :],
                           in_=ps[:, 2 + dc % 2, 0:E]).then_inc(g["sv_pb"], 1)

    def p0_te(te):
        te.wait_ge(g["sv_init"], 1)
        te.wait_ge(g["sv_pb"], 1)
        for i in range(6):
            srcoff = (0, 0, 2 * D, 2 * D, 3 * D, 3 * D)[i] + (i % 2) * 512
            if i >= 2:
                te.wait_ge(g["sv_pb"], i)
            te.matmul(ps[:, i % 2, :], lhsT=ones_f[0:1, :],
                      rhs=p_vec[0:1, srcoff:srcoff + 512],
                      start=True, stop=True).then_inc(g["st_rtp"], 1)
        for dc in range(8):
            if dc >= 2:
                te.wait_ge(g["sv_pb"], 7 + dc)
            te.transpose(ps[:, 2 + dc % 2, 0:E],
                         p_vec[0:8, 4 * D + dc * 128:4 * D + (dc + 1) * 128],
                         ident_f[0:8, 0:8]).then_inc(g["st_rtp"], 1)

    blk.sync(p0_sync)
    blk.gpsimd(p0_gp)
    blk.vector(p0_vec)
    blk.tensor(p0_te)

    add("sd_misc", 16 * 5)
    add("sd_pw", 16 * 64)
    add("sd_pt", 16 * 9 * 32)
    add("sd_cvin", 16 * NCV)
    add("sd_cvout", 16 * NCV)
    add("sv_init", 1)
    add("sv_gate", 32)
    add("sv_cv", NCV)
    add("sv_pb", 15)
    add("st_rtp", 14)

    def ln_stats(ve, src, scr):
        ve.reduce_sum(out=red[:, 0:1],
                      in_=src.rearrange("p (a b) -> p a b", a=1), axis=AX.X)
        ve.tensor_tensor(out=scr[:], in0=src, in1=src, op=ALU.mult)
        ve.reduce_sum(out=red[:, 1:2],
                      in_=scr[:].rearrange("p (a b) -> p a b", a=1),
                      axis=AX.X).then_inc(g["sv_ln"], 1)

    def sguard(sc):
        sc.activation(out=scjunk[:], in_=bias_b[:, 0:192], func=AF.Copy)
        sc.activation(out=scjunk[:], in_=bias_b[:, 192:384], func=AF.Copy)

    def scg(sc):
        sc.activation(out=scjunk[:, 0:96], in_=bias_b[:, 0:96], func=AF.Copy)
        sc.activation(out=scjunk[:, 96:192], in_=bias_b[:, 0:96], func=AF.Copy)

    def ln_sc_apply(sc, n, src, dst):
        sc.wait_ge(g["sv_ln"], n)
        sguard(sc)
        sc.activation(out=lnm[:, 0:1], in_=red[:, 0:1], func=AF.Copy,
                      scale=1.0 / D)
        sc.activation(out=lnm[:, 1:2], in_=lnm[:, 0:1], func=AF.Square)
        sc.activation(out=lnt[:, 3:4], in_=lnm[:, 1:2], func=AF.Identity,
                      scale=-1.0, bias=epsb[:, 0:1])
        scg(sc)
        sc.activation(out=lnt[:, 0:1], in_=red[:, 1:2], func=AF.Identity,
                      scale=1.0 / D, bias=lnt[:, 3:4])
        sc.activation(out=lnt[:, 1:2], in_=lnt[:, 0:1], func=AF.Ln)
        sc.activation(out=lnt[:, 2:3], in_=lnt[:, 1:2], func=AF.Exp,
                      scale=-0.5)
        scg(sc)
        sc.mul(out=lnt[:, 3:4], in_=lnm[:, 0:1], mul=lnt[:, 2:3])
        sc.activation(out=lnt[:, 3:4], in_=lnt[:, 3:4], func=AF.Identity,
                      scale=-1.0)
        scg(sc)
        return sc.activation(out=dst, in_=src, func=AF.Identity,
                             scale=lnt[:, 2:3], bias=lnt[:, 3:4])

    for s in range(1 if dbg else 2):
        X0 = cur("sd_x"); XT0 = cur("sd_xt"); XS0 = cur("sd_xs")
        XC0 = cur("sv_xc")

        def a0_sync(sy, s=s, X0=X0, XT0=XT0, XS0=XS0, XC0=XC0):
            if s == 0:
                sy.wait_ge(g["sd_pt"], 16 * 9 * 32)
                sy.wait_ge(g["sd_cvout"], 16 * NCV)
                sy.wait_ge(g["sv_pb"], 15)
            else:
                sy.wait_ge(g["sv_out"], 16 * s)
                sy.wait_ge(g["sd_out"], 16 * 16 * s)
            for tc in range(TC):
                b = tc % 2
                if tc >= 2:
                    sy.wait_ge(g["sd_xs"], XS0 + 16 * (tc - 1))
                sy.dma_start(out=a_xstage[:, b, :],
                             in_=x_in[s, tc * 128:(tc + 1) * 128, :]).then_inc(
                    g["sd_x"], 16)
                sy.wait_ge(g["sv_xc"], XC0 + tc + 1)
                for kc in range(8):
                    sy.dma_start_transpose(
                        a_fmtile[:, b, kc, :],
                        a_xcast[:, b, kc * 128:(kc + 1) * 128]).then_inc(
                        g["sd_xt"], 16)
                sy.wait_ge(g["sd_xt"], XT0 + 16 * (8 * tc + 8))
                sy.dma_start(out=xfm_s[:, :, tc * 128:(tc + 1) * 128],
                             in_=a_fmtile[:, b, :, :]).then_inc(g["sd_xs"], 16)

        def a0_vec(ve, s=s, X0=X0, XT0=XT0):
            if s == 0:
                ve.wait_ge(g["sd_pt"], 16 * 9 * 32)
                ve.wait_ge(g["sv_pb"], 15)
            else:
                ve.wait_ge(g["sv_out"], 16 * s)
            for tc in range(TC):
                b = tc % 2
                if tc >= 2:
                    ve.wait_ge(g["sd_xt"], XT0 + 16 * 8 * (tc - 1))
                ve.wait_ge(g["sd_x"], X0 + 16 * (tc + 1))
                ve.tensor_copy(out=a_xcast[:, b, :],
                               in_=a_xstage[:, b, :]).then_inc(g["sv_xc"], 1)

        blk.sync(a0_sync)
        blk.vector(a0_vec)
        add("sd_x", 16 * TC)
        add("sd_xt", 16 * 8 * TC)
        add("sd_xs", 16 * TC)
        add("sv_xc", TC)

        WG0 = cur("sd_wg"); SLAB0 = cur("sd_slab"); PROJ0 = cur("st_proj")
        QKV0 = cur("sv_qkv"); SQ0 = cur("sv_sq"); NQ0 = cur("st_nq")
        NQE0 = cur("sv_nq"); NB0 = cur("sv_nb"); S0 = cur("st_s")
        EXP0 = cur("ss_exp"); PV0 = cur("st_pv"); REC0 = cur("sv_rec")
        BC0 = cur("st_bc"); ATT0 = cur("sv_attn"); SSNB0 = cur("ss_nb")

        def a_sync(sy, s=s, WG0=WG0, SLAB0=SLAB0, PROJ0=PROJ0, ATT0=ATT0):
            sy.wait_ge(g["sd_xs"], cur("sd_xs"))
            for gi in range(NG):
                bg = gi % 2
                if gi >= 2:
                    sy.wait_ge(g["st_proj"], PROJ0 + 8 * QS * (gi - 1))
                qo, ko, vo = 256 * gi, D + 256 * gi, 2 * D + 256 * gi
                for i, o0 in enumerate((qo, ko, vo)):
                    sy.dma_start(out=a_wg[:, bg, :, i * 256:(i + 1) * 256],
                                 in_=wqkvT_s[:, :, o0:o0 + 256]).then_inc(
                        g["sd_wg"], 16)
                for qs in range(QS):
                    u = gi * QS + qs
                    if u >= 2:
                        sy.wait_ge(g["st_proj"], PROJ0 + 8 * (u - 1))
                    sy.dma_start(out=a_xslab[:, u % 2, :, :],
                                 in_=xfm_s[:, :, qs * 512:(qs + 1) * 512]
                                 ).then_inc(g["sd_slab"], 16)
            sy.wait_ge(g["sv_attn"], ATT0 + 32)
            sy.dma_start(out=woT[:], in_=woT_s).then_inc(g["sd_wo"], 16)

        def a_te(te, s=s, WG0=WG0, SLAB0=SLAB0, PROJ0=PROJ0, QKV0=QKV0,
                 SQ0=SQ0, NQ0=NQ0, NQE0=NQE0, NB0=NB0, S0=S0, EXP0=EXP0,
                 PV0=PV0, REC0=REC0, BC0=BC0, ATT0=ATT0):
            nproj = 0
            nunit = 0
            for gi in range(NG):
                bg = gi % 2
                te.wait_ge(g["sd_wg"], WG0 + 16 * 3 * (gi + 1))
                for qs in range(QS):
                    u = gi * QS + qs
                    te.wait_ge(g["sd_slab"], SLAB0 + 16 * (u + 1))
                    for which in range(2):
                        for oc in range(2):
                            bank = nproj % 3
                            if nproj >= 3:
                                te.wait_ge(g["sv_qkv"], QKV0 + nproj - 2)
                            toff = which * 256 + oc * 128
                            for kc in range(8):
                                mmi = te.matmul(
                                    ps[:, bank, :],
                                    lhsT=a_wg[:, bg, kc, toff:toff + 128],
                                    rhs=a_xslab[:, u % 2, kc, :],
                                    start=(kc == 0), stop=(kc == 7))
                            mmi.then_inc(g["st_proj"], 1)
                            nproj += 1
                    for tcw in range(4):
                        bank = nproj % 3
                        if nproj >= 3:
                            te.wait_ge(g["sv_qkv"], QKV0 + nproj - 2)
                        for kc in range(8):
                            mmi = te.matmul(
                                ps[:, bank, 0:256],
                                lhsT=a_xslab[:, u % 2, kc,
                                             tcw * 128:(tcw + 1) * 128],
                                rhs=a_wg[:, bg, kc, 512:768],
                                start=(kc == 0), stop=(kc == 7))
                        mmi.then_inc(g["st_proj"], 1)
                        nproj += 1
                for pc in range(NPAIR):
                    pi = 2 * gi + pc
                    te.wait_ge(g["sv_attn"], ATT0 + 4 * pi)
                    te.wait_ge(g["sv_sq"], SQ0 + 2 * pi + 2)
                    for i in range(16):
                        qk, hp, q4 = i // 8, (i // 4) % 2, i % 4
                        if i >= 2:
                            te.wait_ge(g["sv_nq"], NQE0 + 16 * pi + i - 1)
                        te.matmul(ps[:, 4 + i % 2, :],
                                  lhsT=normf[:, hp, :],
                                  rhs=sqbuf[:, qk, q4 * 512:(q4 + 1) * 512],
                                  start=True, stop=True).then_inc(g["st_nq"], 1)
                    te.wait_ge(g["sv_nb"], NB0 + pi + 1)
                    for qs in range(QS):
                        u = nunit
                        up = u % 2
                        hh0 = 2 * pc

                        def s_mm(jc, u=u):
                            jcp = jc % 2
                            sb = 2 * jcp
                            if jc >= 2:
                                te.wait_ge(g["ss_exp"], EXP0 + 16 * u + jc - 1)
                            elif u >= 1:
                                te.wait_ge(g["ss_exp"], EXP0 + 16 * u)
                            te.matmul(ps[:, sb, :],
                                      lhsT=a_k[0:64, pc, jc * 128:(jc + 1) * 128],
                                      rhs=a_q[0:64, pc, qs * 512:(qs + 1) * 512],
                                      start=True, stop=True, tile_position=(0, 0))
                            te.matmul(ps[:, sb + 1, :],
                                      lhsT=a_k[64:128, pc,
                                               jc * 128:(jc + 1) * 128],
                                      rhs=a_q[64:128, pc,
                                              qs * 512:(qs + 1) * 512],
                                      start=True, stop=True,
                                      tile_position=(64, 0)).then_inc(g["st_s"], 1)

                        def pv_mm(jc, u=u, up=up, hh0=hh0):
                            jcp = jc % 4
                            te.wait_ge(g["ss_exp"], EXP0 + 16 * u + jc + 1)
                            if jc == 0:
                                if u >= 2:
                                    te.wait_ge(g["sv_attn"], ATT0 + u - 1)
                            te.matmul(ps[0:64, 4 + up, :],
                                      lhsT=a_v[:, jc, hh0, :],
                                      rhs=P_T[:, jcp, 0, :],
                                      start=(jc == 0), stop=(jc == JC - 1),
                                      tile_position=(0, 0))
                            te.matmul(ps[64:128, 4 + up, :],
                                      lhsT=a_v[:, jc, hh0 + 1, :],
                                      rhs=P_T[:, jcp, 1, :],
                                      start=(jc == 0), stop=(jc == JC - 1),
                                      tile_position=(0, 64))
                            te.matmul(ps[0:64, 6 + up, :],
                                      lhsT=ones_b[:, 0:64],
                                      rhs=P_T[:, jcp, 0, :],
                                      start=(jc == 0), stop=(jc == JC - 1),
                                      tile_position=(0, 0))
                            mmi = te.matmul(ps[64:128, 6 + up, :],
                                            lhsT=ones_b[:, 0:64],
                                            rhs=P_T[:, jcp, 1, :],
                                            start=(jc == 0), stop=(jc == JC - 1),
                                            tile_position=(0, 64))
                            mmi.then_inc(g["st_pv"], 1)

                        s_mm(0)
                        s_mm(1)
                        for jc in range(2, JC):
                            s_mm(jc)
                            pv_mm(jc - 2)
                        pv_mm(JC - 2)
                        pv_mm(JC - 1)
                        nunit += 1

        def a_vec(ve, s=s, PROJ0=PROJ0, QKV0=QKV0, SQ0=SQ0, NQ0=NQ0,
                  NQE0=NQE0, NB0=NB0, EXP0=EXP0, PV0=PV0, REC0=REC0,
                  BC0=BC0, ATT0=ATT0, SSNB0=SSNB0):
            nproj = 0
            nunit = 0
            for gi in range(NG):
                for qs in range(QS):
                    for which in range(2):
                        for oc in range(2):
                            ve.wait_ge(g["st_proj"], PROJ0 + nproj + 1)
                            dst = a_q if which == 0 else a_k
                            ve.tensor_copy(
                                out=dst[:, oc, qs * 512:(qs + 1) * 512],
                                in_=ps[:, nproj % 3, :]).then_inc(g["sv_qkv"], 1)
                            nproj += 1
                    for tcw in range(4):
                        ve.wait_ge(g["st_proj"], PROJ0 + nproj + 1)
                        ve.tensor_copy(
                            out=a_v[:, qs * 4 + tcw, :, :],
                            in_=ps[:, nproj % 3, 0:256].rearrange(
                                "p (h d) -> p h d", h=4)).then_inc(g["sv_qkv"], 1)
                        nproj += 1
                for pc in range(NPAIR):
                    pi = 2 * gi + pc
                    if pi >= 1:
                        ve.wait_ge(g["st_nq"], NQ0 + 16 * pi)
                        ve.wait_ge(g["ss_exp"], EXP0 + 64 * pi)
                        ve.wait_ge(g["st_pv"], PV0 + 16 * 4 * pi)
                    ve.tensor_tensor(out=sqbuf[:, 0, :], in0=a_q[:, pc, :],
                                     in1=a_q[:, pc, :], op=ALU.mult)
                    ve.tensor_tensor(out=sqbuf[:, 1, :], in0=a_k[:, pc, :],
                                     in1=a_k[:, pc, :],
                                     op=ALU.mult).then_inc(g["sv_sq"], 2)
                    for i in range(16):
                        ve.wait_ge(g["st_nq"], NQ0 + 16 * pi + i + 1)
                        ve.reduce_max(out=red[:, i:i + 1],
                                      in_=ps[:, 4 + i % 2, :],
                                      axis=AX.X).then_inc(g["sv_nq"], 1)
                    ve.wait_ge(g["ss_nb"], SSNB0 + 2 * pi + 1)
                    ve.tensor_copy(out=a_xcast[:, 0, :].bitcast(F32)
                                   [:, 0:256], in_=bias_b[:, 0:256])
                    ve.reduce_max(out=red[:, 16:17], in_=nqb128[:, 0:8],
                                  axis=AX.X)
                    ve.reduce_max(out=red[:, 17:18], in_=nqb128[:, 8:16],
                                  axis=AX.X).then_inc(g["sv_nb"], 1)
                    if dbg:
                        ve.wait_ge(g["ss_nb"], SSNB0 + 2 * pi + 2)
                        ve.tensor_copy(out=nq[0:1, 16 + pi:17 + pi],
                                       in_=mhatb[0:1, pc:pc + 1])
                    for qs in range(QS):
                        u = nunit
                        up = u % 2
                        ve.wait_ge(g["st_pv"], PV0 + 16 * u + 16)
                        ve.reciprocal(out=bc_sb[:], in_=ps[:, 6 + up, :])
                        ve.tensor_copy(out=a_xcast[:, 0, :].bitcast(F32)
                                       [:, 0:256], in_=bias_b[:, 0:256])
                        ve.tensor_copy(out=a_xcast[:, 1, :].bitcast(F32)
                                       [:, 0:256], in_=bias_b[:, 0:256])
                        ve.tensor_tensor(
                            out=a_attn[:, pi, qs * 512:(qs + 1) * 512],
                            in0=ps[:, 4 + up, :], in1=bc_sb[:],
                            op=ALU.mult).then_inc(g["sv_attn"], 1)
                        nunit += 1

        def a_sc(sc, s=s, S0=S0, EXP0=EXP0, PV0=PV0, NB0=NB0, NQE0=NQE0,
                 SSNB0=SSNB0):
            nunit = 0
            for gi in range(NG):
                for pc in range(NPAIR):
                    pi = 2 * gi + pc
                    sc.wait_ge(g["sv_nq"], NQE0 + 16 * pi + 16)
                    sguard(sc)
                    sc.activation(out=nqb128[:], in_=red[:, 0:16],
                                  func=AF.Copy).then_inc(g["ss_nb"], 1)
                    sc.wait_ge(g["sv_nb"], NB0 + pi + 1)
                    sguard(sc)
                    sc.activation(out=mt[:, 0:1], in_=red[:, 16:17],
                                  func=AF.Copy)
                    scg(sc)
                    sc.activation(out=mt[:, 1:2], in_=red[:, 17:18],
                                  func=AF.Identity, bias=mt[:, 0:1])
                    sc.mul(out=mt[:, 1:2], in_=mt[:, 1:2],
                           mul=SCALE * 1.01 * 0.5)
                    sc.activation(out=mt[:, 0:1], in_=mt[:, 1:2],
                                  func=AF.Relu)
                    sc.activation(out=mt[:, 1:2], in_=mt[:, 0:1],
                                  func=AF.Identity, scale=-1.0, bias=cl2[:, 0:1])
                    sc.activation(out=mt[:, 1:2], in_=mt[:, 1:2], func=AF.Relu)
                    sc.activation(out=mhatb[:, pc:pc + 1], in_=mt[:, 1:2],
                                  func=AF.Identity, scale=1.0,
                                  bias=cl2[:, 1:2])
                    scg(sc)
                    sc.engine_nop().then_inc(g["ss_nb"], 1) if False else \
                        sc.activation(out=scjunk[:, 0:64],
                                      in_=bias_b[:, 0:64],
                                      func=AF.Copy).then_inc(g["ss_nb"], 1)
                    for qs in range(QS):
                        u = nunit
                        for jc in range(JC):
                            jcp = jc % 2
                            sc.wait_ge(g["st_s"], S0 + 16 * u + jc + 1)
                            if jc >= 4:
                                sc.wait_ge(g["st_pv"], PV0 + 16 * u + jc - 3)
                            elif u >= 1:
                                sc.wait_ge(g["st_pv"], PV0 + 16 * u)
                            sc.activation(
                                out=P_T[:, jc % 4, :, :],
                                in_=ps[:, 2 * jcp:2 * jcp + 2, :],
                                func=AF.Exp, bias=mhatb[:, pc:pc + 1],
                                scale=SCALE).then_inc(g["ss_exp"], 1)
                        nunit += 1

        blk.sync(a_sync)
        blk.tensor(a_te)
        blk.vector(a_vec)
        blk.scalar(a_sc)

        add("sd_wg", 16 * 3 * NG)
        add("sd_slab", 16 * NG * QS)
        add("sd_wo", 16)
        add("st_proj", 8 * NG * QS)
        add("sv_qkv", 8 * NG * QS)
        add("sv_sq", 2 * NG * NPAIR)
        add("st_nq", 16 * NG * NPAIR)
        add("sv_nq", 16 * NG * NPAIR)
        add("sv_nb", NG * NPAIR)
        add("ss_nb", 2 * NG * NPAIR)
        add("st_s", 16 * 32)
        add("ss_exp", 16 * 32)
        add("st_pv", 16 * 32)
        add("sv_attn", 32)

        OP0 = cur("st_op"); R10 = cur("sv_r1"); RES0 = cur("sd_res")
        LNV0 = cur("sv_ln"); LNS0 = cur("ss_ln"); ATT1 = cur("sv_attn")

        def o_gp(gp, s=s, ATT1=ATT1, R10=R10):
            gp.wait_ge(g["sv_attn"], ATT1)
            for tc in range(TC):
                if tc >= 2:
                    gp.wait_ge(g["sv_r1"], R10 + tc - 1)
                gp.dma_start(out=a_xstage[:, tc % 2, :],
                             in_=x_in[s, tc * 128:(tc + 1) * 128, :]).then_inc(
                    g["sd_res"], 16)

        def o_te(te, s=s, R10=R10):
            te.wait_ge(g["sd_wo"], 16 * (s + 1))
            for tc in range(TC):
                bs = (tc % 2) * 4
                if tc >= 2:
                    te.wait_ge(g["sv_r1"], R10 + tc - 1)
                for p in range(8):
                    for dh in range(2):
                        te.matmul(ps[:, bs + dh, :],
                                  lhsT=a_attn[0:64, p, tc * 128:(tc + 1) * 128],
                                  rhs=woT[0:64, p, dh * 512:(dh + 1) * 512],
                                  start=(p == 0), stop=(p == 7),
                                  tile_position=(0, 0))
                        mmi = te.matmul(
                            ps[:, bs + 2 + dh, :],
                            lhsT=a_attn[64:128, p, tc * 128:(tc + 1) * 128],
                            rhs=woT[64:128, p, dh * 512:(dh + 1) * 512],
                            start=(p == 0), stop=(p == 7),
                            tile_position=(64, 0))
                mmi.then_inc(g["st_op"], 1)

        def o_vec(ve, s=s, OP0=OP0, R10=R10, RES0=RES0, LNS0=LNS0):
            for tc in range(TC):
                b = tc % 2
                bs = (tc % 2) * 4
                ve.wait_ge(g["st_op"], OP0 + tc + 1)
                ve.wait_ge(g["sd_res"], RES0 + 16 * (tc + 1))
                ve.scalar_tensor_tensor(
                    out=r_tmp[:, b, :],
                    in0=ps[:, bs:bs + 2, :].rearrange("p a b -> p (a b)"),
                    scalar=0.0, in1=bias_b[:], op0=ALU.bypass, op1=ALU.add)
                ve.scalar_tensor_tensor(
                    out=r_tmp[:, b, :],
                    in0=ps[:, bs + 2:bs + 4, :].rearrange("p a b -> p (a b)"),
                    scalar=0.0, in1=r_tmp[:, b, :], op0=ALU.bypass, op1=ALU.add)
                ve.tensor_tensor(out=r_tmp[:, b, :], in0=r_tmp[:, b, :],
                                 in1=a_xstage[:, b, :], op=ALU.add)
                if dbg == 4:
                    ve.tensor_copy(out=acc[:, tc, :], in_=r_tmp[:, b, :])
                    ve.engine_nop().then_inc(g["sv_r1"], 1)
                else:
                    ve.engine_nop().then_inc(g["sv_r1"], 1)
                    if tc >= 2:
                        ve.wait_ge(g["ss_ln"], LNS0 + tc - 1)
                    ln_stats(ve, r_tmp[:, b, :], o_lnscr)

        def o_sc(sc, s=s, LNV0=LNV0):
            if dbg == 4:
                return
            for tc in range(TC):
                ln_sc_apply(sc, LNV0 + tc + 1, r_tmp[:, tc % 2, :],
                            acc[:, tc, :]).then_inc(g["ss_ln"], 1)

        blk.gpsimd(o_gp)
        blk.tensor(o_te)
        blk.vector(o_vec)
        blk.scalar(o_sc)

        add("sd_res", 16 * TC)
        add("st_op", TC)
        add("sv_r1", TC)
        add("sv_ln", TC)
        add("ss_ln", TC)

        if dbg in (1, 4):
            def dbg_sync(sy, LNS0=LNS0):
                sy.wait_ge(g["sv_r1"], cur("sv_r1"))
                if dbg == 1:
                    sy.wait_ge(g["ss_ln"], LNS0 + TC)
                for nm, src_t in ((d_r1, acc), (d_attn, None), (d_q, None)):
                    pass
                sy.dma_start(out=d_r1, in_=acc[:]).then_inc(g["sd_out"], 16)
                sy.dma_start(out=d_mh, in_=mhatb[:]).then_inc(g["sd_out"], 16)
                sy.dma_start(out=d_bb[:, 0, :], in_=bias_b[:]).then_inc(g["sd_out"], 16)
                sy.dma_start(out=d_bb[:, 1, :], in_=gamma_b[:]).then_inc(g["sd_out"], 16)
                sy.dma_start(out=d_bb[:, 2, :], in_=beta_b[:]).then_inc(g["sd_out"], 16)
                sy.dma_start(out=d_ln[:, 0:2], in_=mv[:]).then_inc(g["sd_out"], 16)
                sy.dma_start(out=d_ln[:, 2:6], in_=lnt[:]).then_inc(g["sd_out"], 16)
                sy.dma_start(out=d_rt[:, 0, :], in_=r_tmp[:, 1, :]).then_inc(g["sd_out"], 16)
                sy.dma_start(out=d_rt[:, 1, :], in_=o_lnscr[:]).then_inc(g["sd_out"], 16)
                sy.dma_start(out=d_nq, in_=nq[0:1, :]).then_inc(g["sd_out"], 16)
                sy.wait_ge(g["sd_out"], cur("sd_out") + 16 * 10)
            def dbg_vec(ve):
                ve.wait_ge(g["sv_attn"], cur("sv_attn"))
                ve.wait_ge(g["sd_out"], cur("sd_out") + 16 * 9)
                ve.tensor_copy(out=bcf32[:], in_=a_attn[:, :, :]).then_inc(g["sv_out"], 1)
                ve.tensor_copy(out=qf32[:], in_=a_q[:]).then_inc(g["sv_out"], 1)
                ve.tensor_copy(out=vf32[:], in_=a_v[:]).then_inc(g["sv_out"], 1)
                ve.wait_ge(g["sd_out"], cur("sd_out") + 16 * 7)
                ve.tensor_copy(out=qf32[:], in_=a_k[:]).then_inc(g["sv_out"], 1)
            def dbg_sync2(sy):
                sy.wait_ge(g["sv_out"], 3)
                sy.dma_start(out=d_attn, in_=bcf32[:]).then_inc(g["sd_out"], 16)
                sy.dma_start(out=d_q, in_=qf32[:]).then_inc(g["sd_out"], 16)
                sy.dma_start(out=d_v, in_=vf32[:]).then_inc(g["sd_out"], 16)
                sy.wait_ge(g["sv_out"], 4)
                sy.dma_start(out=d_k, in_=qf32[:]).then_inc(g["sd_out"], 16)
                sy.wait_ge(g["sd_out"], cur("sd_out") + 16 * 9)
            bcf32 = at("bcf32", [128, 8, S], F32, 0)
            qf32 = at("qf32", [128, 2, S], F32, 65664)
            vf32 = at("vf32", [128, 16, 4, 64], F32, 82048)
            blk.vector(dbg_vec)
            blk.sync(dbg_sync)
            blk.sync(dbg_sync2)
            es.close()
            return nc

        XN0 = cur("sv_xn"); XNT0 = cur("sd_xnt"); TP0 = cur("st_tp")
        XNF0 = cur("sv_xnf"); LG0 = cur("st_lg"); WD0 = cur("sv_wd")
        SSWD0 = cur("ss_wd"); SG0 = cur("sg_wd")
        LNV1 = cur("sv_ln"); LNS1 = cur("ss_ln"); R11 = cur("sv_r1")

        def f1_vec(ve, s=s, XN0=XN0, XNT0=XNT0, TP0=TP0, XNF0=XNF0, LG0=LG0,
                   WD0=WD0, LNS1=LNS1, R11=R11, SSWD0=SSWD0, SG0=SG0,
                   LNSO=LNS0):
            ve.tensor_copy(out=rwhl[:, 0, :, :],
                           in_=router_T[:, :, :])
            ve.tensor_tensor(out=rwhl[:, 1, :, :], in0=router_T[:, :, :],
                             in1=rwhl[:, 0, :, :],
                             op=ALU.subtract).then_inc(g["st_rtp"], 1)
            for tc in range(TC):
                b = tc % 2
                ve.wait_ge(g["ss_ln"], LNSO + tc + 1)
                if tc >= 2:
                    ve.wait_ge(g["sd_xnt"], XNT0 + 16 * 16 * (tc - 1))
                ln_stats(ve, acc[:, tc, :], f_lnscr)
                ve.wait_ge(g["ss_ln"], LNS1 + tc + 1)
                ve.tensor_copy(out=f_lnscr[:, 0:256], in_=bias_b[:, 0:256])
                ve.tensor_copy(out=f_lnscr[:, 256:512], in_=bias_b[:, 0:256])
                ve.tensor_tensor(out=f_xn2tm[:, b, :], in0=f_xn2tm[:, b, :],
                                 in1=gamma_b[:], op=ALU.mult)
                ve.tensor_tensor(out=f_xn2tm[:, b, :], in0=f_xn2tm[:, b, :],
                                 in1=beta_b[:], op=ALU.add)
                ve.tensor_copy(out=f_xncast[:, b, :], in_=f_xn2tm[:, b, :])
                ve.tensor_tensor(out=f_xnlc[:, b, :], in0=f_xn2tm[:, b, :],
                                 in1=f_xncast[:, b, :],
                                 op=ALU.subtract).then_inc(g["sv_xn"], 1)
                ve.tensor_scalar(out=acc[:, tc, :], in0=acc[:, tc, :],
                                 scalar1=2.0, scalar2=None, op0=ALU.mult)
                ve.wait_ge(g["st_lg"], LG0 + tc + 1)
                lg = ps[:, 2, 0:E]
                if dbg:
                    ve.tensor_copy(out=lgd[:, tc, :], in_=lg)
                ve.reduce_max(out=red[:, 2:3], in_=lg,
                              axis=AX.X).then_inc(g["sv_wd"], 1)
                ve.wait_ge(g["ss_wd"], SSWD0 + 3 * tc + 1)
                ve.tensor_copy(out=f_lnscr[:, 0:256], in_=bias_b[:, 0:256])
                ve.tensor_copy(out=f_lnscr[:, 256:512], in_=bias_b[:, 0:256])
                ve.tensor_scalar(out=r_pk[:, 1, :], in0=lg, scalar1=r_m1[:, 0:1],
                                 scalar2=None, op0=ALU.is_lt)
                ve.tensor_scalar(out=r_pk[:, 2, :], in0=lg, scalar1=BIG,
                                 scalar2=None, op0=ALU.add).then_inc(
                    g["sv_wd"], 1)
                ve.wait_ge(g["sg_wd"], SG0 + 2 * tc + 1)
                ve.wait_ge(g["ss_wd"], SSWD0 + 3 * tc + 2)
                ve.tensor_copy(out=f_lnscr[:, 0:256], in_=bias_b[:, 0:256])
                ve.tensor_copy(out=f_lnscr[:, 256:512], in_=bias_b[:, 0:256])
                ve.reduce_max(out=red[:, 3:4], in_=r_pk[:, 3, :], axis=AX.X)
                ve.reduce_sum(out=red[:, 4:5],
                              in_=r_pk[:, 0, :].rearrange(
                                  "p (a b) -> p a b", a=1),
                              axis=AX.X).then_inc(g["sv_wd"], 1)
                ve.wait_ge(g["sg_wd"], SG0 + 2 * tc + 2)
                ve.wait_ge(g["ss_wd"], SSWD0 + 3 * tc + 3)
                ve.tensor_copy(out=f_lnscr[:, 0:256], in_=bias_b[:, 0:256])
                ve.tensor_copy(out=f_lnscr[:, 256:512], in_=bias_b[:, 0:256])
                ve.scalar_tensor_tensor(out=wdense[:, tc, :],
                                        in0=r_pk[:, 0, :],
                                        scalar=r_m1[:, 3:4],
                                        in1=r_pk[:, 1, :],
                                        op0=ALU.mult,
                                        op1=ALU.mult).then_inc(g["sv_wd"], 1)

        def f1_gp(gp, s=s, WD0=WD0, SSWD0=SSWD0, SG0=SG0):
            for tc in range(TC):
                gp.wait_ge(g["sv_wd"], WD0 + 4 * tc + 2)
                gp.tensor_copy(out=gp_j[:], in_=bias_b[:, 0:128])
                gp.tensor_copy(out=gp_j[:], in_=bias_b[:, 128:256])
                gp.tensor_tensor(out=r_pk[:, 3, :], in0=r_pk[:, 1, :],
                                 in1=r_pk[:, 2, :], op=ALU.mult).then_inc(
                    g["sg_wd"], 1)
                gp.wait_ge(g["ss_wd"], SSWD0 + 3 * tc + 3)
                gp.tensor_copy(out=gp_j[:], in_=bias_b[:, 0:128])
                gp.tensor_copy(out=gp_j[:], in_=bias_b[:, 128:256])
                gp.tensor_scalar(out=r_pk[:, 1, :], in0=r_pk[:, 2, :],
                                 scalar1=r_m1[:, 2:3], scalar2=None,
                                 op0=ALU.is_ge).then_inc(g["sg_wd"], 1)

        def f1_sync(sy, s=s, XN0=XN0, LG0=LG0):
            for tc in range(TC):
                b = tc % 2
                sy.wait_ge(g["sv_xn"], XN0 + tc + 1)
                if tc >= 2:
                    sy.wait_ge(g["st_lg"], LG0 + tc - 1)
                for kc in range(8):
                    sy.dma_start_transpose(
                        f_xn2fm[:, kc, tc * 128:(tc + 1) * 128],
                        f_xncast[:, b, kc * 128:(kc + 1) * 128]).then_inc(
                        g["sd_xnt"], 16)
                for kc in range(8):
                    sy.dma_start_transpose(
                        f_xnlo[:, b, kc, :],
                        f_xnlc[:, b, kc * 128:(kc + 1) * 128]).then_inc(
                        g["sd_xnt"], 16)

        def f1_te(te, s=s, XN0=XN0, XNT0=XNT0, WD0=WD0, R11=R11, LNS1=LNS1):
            te.wait_ge(g["sv_r1"], R11)
            te.wait_ge(g["st_rtp"], 14 + s + 1)
            for tc in range(TC):
                b = tc % 2
                te.wait_ge(g["sd_xnt"], XNT0 + 16 * 16 * (tc + 1))
                if tc >= 1:
                    te.wait_ge(g["sv_wd"], WD0 + 4 * tc)
                nmm = 0
                for dc in range(8):
                    for which in range(3):
                        lhs = f_xn2fm[:, dc, tc * 128:(tc + 1) * 128] \
                            if which != 1 else f_xnlo[:, b, dc, :]
                        rhs = rwhl[:, 0, dc, :] if which != 2 \
                            else rwhl[:, 1, dc, :]
                        nmm += 1
                        mmi = te.matmul(ps[:, 2, 0:E], lhsT=lhs, rhs=rhs,
                                        start=(nmm == 1), stop=(nmm == 24))
                mmi.then_inc(g["st_lg"], 1)

        def f1_sc(sc, s=s, WD0=WD0, LNV1=LNV1, SSWD0=SSWD0, XN0=XN0):
            for tc in range(TC):
                if tc >= 2:
                    sc.wait_ge(g["sv_xn"], XN0 + tc - 1)
                ln_sc_apply(sc, LNV1 + tc + 1, acc[:, tc, :],
                            f_xn2tm[:, tc % 2, :]).then_inc(g["ss_ln"], 1)
                sc.wait_ge(g["sv_wd"], WD0 + 4 * tc + 1)
                sguard(sc)
                sc.activation(out=r_m1[:, 0:1], in_=red[:, 2:3], func=AF.Copy)
                sc.activation(out=r_m1[:, 1:2], in_=red[:, 2:3], func=AF.Copy,
                              scale=-1.0 / TEMP).then_inc(g["ss_wd"], 1)
                sc.wait_ge(g["sv_wd"], WD0 + 4 * tc + 2)
                scg(sc)
                sc.activation(out=r_pk[:, 0, :], in_=ps[:, 2, 0:E], func=AF.Exp,
                              bias=r_m1[:, 1:2],
                              scale=1.0 / TEMP).then_inc(g["ss_wd"], 1)
                sc.wait_ge(g["sv_wd"], WD0 + 4 * tc + 3)
                sguard(sc)
                sc.activation(out=r_m1[:, 2:3], in_=red[:, 3:4], func=AF.Copy)
                sc.activation(out=rt_s[:, 0:1], in_=red[:, 4:5], func=AF.Ln)
                sc.activation(out=r_m1[:, 3:4], in_=rt_s[:, 0:1], func=AF.Exp,
                              scale=-1.0).then_inc(g["ss_wd"], 1)

        blk.vector(f1_vec)
        blk.sync(f1_sync)
        blk.tensor(f1_te)
        blk.scalar(f1_sc)
        blk.gpsimd(f1_gp)

        add("sv_xn", TC)
        add("sd_xnt", 16 * 16 * TC)
        add("st_lg", TC)
        add("st_rtp", 1)
        add("sv_wd", 4 * TC)
        add("ss_wd", 3 * TC)
        add("sg_wd", 2 * TC)
        add("sv_ln", TC)
        add("ss_ln", TC)

        if dbg == 2:
            def dbg_vec(ve):
                ve.wait_ge(g["sd_xnt"], cur("sd_xnt"))
                ve.tensor_copy(out=xnf32[:], in_=f_xn2fm[:]).then_inc(g["sv_out"], 1)
            def dbg_sync(sy):
                sy.wait_ge(g["sv_wd"], cur("sv_wd"))
                sy.wait_ge(g["sv_out"], 1)
                sy.dma_start(out=d_r1, in_=acc[:]).then_inc(g["sd_out"], 16)
                sy.dma_start(out=d_wd, in_=wdense[:]).then_inc(g["sd_out"], 16)
                sy.dma_start(out=d_xn, in_=xnf32[:]).then_inc(g["sd_out"], 16)
                sy.dma_start(out=d_lg, in_=lgd[:]).then_inc(g["sd_out"], 16)
                sy.wait_ge(g["sd_out"], cur("sd_out") + 16 * 4)
            xnf32 = at("xnf32", [128, 8, S], F32, 32768)
            blk.vector(dbg_vec)
            blk.sync(dbg_sync)
            es.close()
            return nc

        W10 = cur("sd_w1"); W20 = cur("sd_w2"); H0 = cur("st_h")
        SH0 = cur("ss_h"); Y0 = cur("st_y"); ACC0 = cur("sv_acc")
        WD1 = cur("sv_wd"); XNT1 = cur("sd_xnt")

        def f2_sync(sy, s=s, W10=W10, W20=W20, H0=H0, Y0=Y0, WD1=WD1,
                    XNT1=XNT1):
            sy.wait_ge(g["sv_wd"], WD1)
            sy.wait_ge(g["sd_xnt"], XNT1)
            nblk = 0
            nfc = 0
            for e in range(E):
                for ts in range(TS):
                    for fb in range(FBLK):
                        if nblk >= 2:
                            sy.wait_ge(g["st_h"], H0 + 4 * nblk - 4)
                        sy.dma_start(out=f_w1[:, nblk % 2, :, :],
                                     in_=w1s[e, fb]).then_inc(g["sd_w1"], 16)
                        nblk += 1
                    for fc in range(FC):
                        if nfc >= 4:
                            sy.wait_ge(g["st_y"], Y0 + nfc - 3)
                        sy.dma_start(out=f_w2[:, nfc % 4, :],
                                     in_=w2s[e, fc]).then_inc(g["sd_w2"], 16)
                        nfc += 1

        def f2_te(te, s=s, W10=W10, W20=W20, H0=H0, SH0=SH0, Y0=Y0,
                  ACC0=ACC0, WD1=WD1):
            te.wait_ge(g["sv_wd"], WD1)
            nblk = 0
            nfc = 0
            nsl = 0
            for e in range(E):
                for ts in range(TS):
                    for fb in range(FBLK):
                        te.wait_ge(g["sd_w1"], W10 + 16 * (nblk + 1))
                        bw = nblk % 2
                        for fc4 in range(4):
                            fc = fb * 4 + fc4
                            hb = fc % 2
                            nh = nsl * FC + fc
                            if nh >= 2:
                                te.wait_ge(g["ss_h"], SH0 + (nh - 2) // 2 + 1)
                            if fc < 2 and nsl >= 1:
                                te.wait_ge(g["sv_acc"], ACC0 + 4 * (nsl - 1) + 1)
                            for dc in range(8):
                                mmi = te.matmul(
                                    ps[:, hb, :],
                                    lhsT=f_w1[:, bw, dc,
                                              fc4 * 128:(fc4 + 1) * 128],
                                    rhs=f_xn2fm[:, dc, ts * 512:(ts + 1) * 512],
                                    start=(dc == 0), stop=(dc == 7))
                            mmi.then_inc(g["st_h"], 1)
                        nblk += 1
                    te.wait_ge(g["ss_h"], SH0 + 16 * (nsl + 1))
                    if nsl >= 1:
                        te.wait_ge(g["sv_acc"], ACC0 + 4 * nsl)
                    for fc in range(FC):
                        te.wait_ge(g["sd_w2"], W20 + 16 * (nfc + 1))
                        b4 = nfc % 4
                        for tc4 in range(4):
                            for dh in range(2):
                                mmi = te.matmul(
                                    ps[:, 2 * tc4 + dh, :],
                                    lhsT=f_h[:, fc, tc4 * 128:(tc4 + 1) * 128],
                                    rhs=f_w2[:, b4, dh * 512:(dh + 1) * 512],
                                    start=(fc == 0), stop=(fc == FC - 1))
                        mmi.then_inc(g["st_y"], 1)
                        nfc += 1
                    nsl += 1

        def f2_sc(sc, s=s, H0=H0, SH0=SH0):
            nsl = 0
            for e in range(E):
                for ts in range(TS):
                    for fcp in range(FC // 2):
                        sc.wait_ge(g["st_h"], H0 + nsl * FC + 2 * fcp + 2)
                        sc.activation(out=f_h[:, 2 * fcp:2 * fcp + 2, :],
                                      in_=ps[:, 0:2, :],
                                      func=AF.Silu).then_inc(g["ss_h"], 1)
                    nsl += 1

        def f2_vec(ve, s=s, Y0=Y0, ACC0=ACC0):
            nsl = 0
            for e in range(E):
                for ts in range(TS):
                    ve.wait_ge(g["st_y"], Y0 + FC * (nsl + 1))
                    for tc4 in range(4):
                        tc = ts * 4 + tc4
                        ve.scalar_tensor_tensor(
                            out=acc[:, tc, :],
                            in0=ps[:, 2 * tc4:2 * tc4 + 2, :].rearrange(
                                "p a b -> p (a b)"),
                            scalar=wdense[:, tc, e:e + 1], in1=acc[:, tc, :],
                            op0=ALU.mult, op1=ALU.add).then_inc(g["sv_acc"], 1)
                    nsl += 1

        blk.sync(f2_sync)
        blk.tensor(f2_te)
        blk.scalar(f2_sc)
        blk.vector(f2_vec)

        add("sd_w1", 16 * E * TS * FBLK)
        add("sd_w2", 16 * E * TS * FC)
        add("st_h", E * TS * FC)
        add("ss_h", E * TS * FC // 2)
        add("st_y", E * TS * FC)
        add("sv_acc", E * TS * 4)

        if dbg == 3:
            def dbg_sync(sy):
                sy.wait_ge(g["sv_acc"], cur("sv_acc"))
                sy.dma_start(out=d_acc2, in_=acc[:]).then_inc(g["sd_out"], 16)
                sy.wait_ge(g["sd_out"], cur("sd_out") + 16)
            blk.sync(dbg_sync)
            es.close()
            return nc

        OUT0 = cur("sv_out"); DOUT0 = cur("sd_out")
        LNV2 = cur("sv_ln"); LNS2 = cur("ss_ln"); ACC1 = cur("sv_acc")

        def f3_vec(ve, s=s, OUT0=OUT0, DOUT0=DOUT0, LNS2=LNS2, ACC1=ACC1):
            ve.wait_ge(g["sv_acc"], ACC1)
            for tc in range(TC):
                ln_stats(ve, acc[:, tc, :], f_lnscr)

        def f3_sc(sc, s=s, LNV2=LNV2, DOUT0=DOUT0):
            for tc in range(TC):
                if tc >= 2:
                    sc.wait_ge(g["sd_out"], DOUT0 + 16 * (tc - 1))
                ln_sc_apply(sc, LNV2 + tc + 1, acc[:, tc, :],
                            f_out[:, tc % 2, :]).then_inc(g["sv_out"], 1)

        def f3_sync(sy, s=s, OUT0=OUT0):
            for tc in range(TC):
                sy.wait_ge(g["sv_out"], OUT0 + tc + 1)
                sy.dma_start(out=out[s, tc * 128:(tc + 1) * 128, :],
                             in_=f_out[:, tc % 2, :]).then_inc(g["sd_out"], 16)

        blk.vector(f3_vec)
        blk.scalar(f3_sc)
        blk.sync(f3_sync)

        add("sv_out", TC)
        add("sd_out", 16 * TC)
        add("sv_ln", TC)

    def fin_sync(sy):
        sy.wait_ge(g["sd_out"], cur("sd_out"))

    blk.sync(fin_sync)
    es.close()
    return nc


_NC_CACHE = []


def kernel(L0_attn_x, L0_attn_qkv_weight, L0_attn_qkv_weight_mask,
           L0_attn_out_weight, L0_attn_out_weight_mask, L0_attn_out_bias,
           L0_attn_out_bias_mask, L0_ffn_gamma, L0_ffn_beta, L0_ffn_router_w,
           L0_ffn_W1, L0_ffn_W2):
    x = np.ascontiguousarray(np.asarray(L0_attn_x, dtype=np.float32))
    com = {
        "wqkv": np.ascontiguousarray(np.asarray(L0_attn_qkv_weight, np.float32)),
        "mqkv": np.ascontiguousarray(np.asarray(L0_attn_qkv_weight_mask, np.float32)),
        "wo": np.ascontiguousarray(np.asarray(L0_attn_out_weight, np.float32)),
        "mo": np.ascontiguousarray(np.asarray(L0_attn_out_weight_mask, np.float32)),
        "bo": np.ascontiguousarray(np.asarray(L0_attn_out_bias, np.float32).reshape(1, D)),
        "mbo": np.ascontiguousarray(np.asarray(L0_attn_out_bias_mask, np.float32).reshape(1, D)),
        "gamma": np.ascontiguousarray(np.asarray(L0_ffn_gamma, np.float32).reshape(1, D)),
        "beta": np.ascontiguousarray(np.asarray(L0_ffn_beta, np.float32).reshape(1, D)),
        "rw": np.ascontiguousarray(np.asarray(L0_ffn_router_w, np.float32)),
        "w1": np.ascontiguousarray(np.asarray(L0_ffn_W1, np.float32)),
        "w2": np.ascontiguousarray(np.asarray(L0_ffn_W2, np.float32)),
    }
    if not _NC_CACHE:
        _NC_CACHE.append(build_core())
    nc = _NC_CACHE[0]
    in_maps = [dict(com, x=np.ascontiguousarray(x[2 * c:2 * c + 2]))
               for c in range(8)]
    res = run_bass_kernel_spmd(nc, in_maps, core_ids=list(range(8)))
    return np.concatenate([np.asarray(r["out"]) for r in res.results], axis=0)


# version 05
# speedup vs baseline: 1.0708x; 1.0708x over previous
import contextlib
import numpy as np

import concourse.bass as bass
import concourse.mybir as mybir
from concourse.bass_utils import run_bass_kernel_spmd

F32 = mybir.dt.float32
BF16 = mybir.dt.bfloat16
AF = mybir.ActivationFunctionType
ALU = mybir.AluOpType
AX = mybir.AxisListType

S = 2048
D = 1024
E = 8
TEMP = 1.5
EPS = 1e-5
SCALE = 1.0 / 32.0
BIG = 30000.0

TC = 16
QS = 4
JC = 16
NG = 4
NPAIR = 2
TS = 4
FBLK = 8
FC = 32
NCV = 128 + 256


def build_core(dbg=0):
    nc = bass.Bass("TRN2", target_bir_lowering=False)

    x_in = nc.dram_tensor("x", [2, S, D], F32, kind="ExternalInput").ap()
    wqkv = nc.dram_tensor("wqkv", [3 * D, D], F32, kind="ExternalInput").ap()
    mqkv = nc.dram_tensor("mqkv", [3 * D, D], F32, kind="ExternalInput").ap()
    wo = nc.dram_tensor("wo", [D, D], F32, kind="ExternalInput").ap()
    mo = nc.dram_tensor("mo", [D, D], F32, kind="ExternalInput").ap()
    bo = nc.dram_tensor("bo", [1, D], F32, kind="ExternalInput").ap()
    mbo = nc.dram_tensor("mbo", [1, D], F32, kind="ExternalInput").ap()
    gamma = nc.dram_tensor("gamma", [1, D], F32, kind="ExternalInput").ap()
    beta = nc.dram_tensor("beta", [1, D], F32, kind="ExternalInput").ap()
    rw = nc.dram_tensor("rw", [E, D], F32, kind="ExternalInput").ap()
    w1 = nc.dram_tensor("w1", [E, D, 4096], F32, kind="ExternalInput").ap()
    w2 = nc.dram_tensor("w2", [E, 4096, D], F32, kind="ExternalInput").ap()
    out = nc.dram_tensor("out", [2, S, D], F32, kind="ExternalOutput").ap()
    if dbg:
        d_r1 = nc.dram_tensor("d_r1", [128, TC, D], F32, kind="ExternalOutput").ap()
        d_attn = nc.dram_tensor("d_attn", [128, 8, S], F32, kind="ExternalOutput").ap()
        d_q = nc.dram_tensor("d_q", [128, 2, S], F32, kind="ExternalOutput").ap()
        d_k = nc.dram_tensor("d_k", [128, 2, S], F32, kind="ExternalOutput").ap()
        d_v = nc.dram_tensor("d_v", [128, 16, 4, 64], F32, kind="ExternalOutput").ap()
        d_mh = nc.dram_tensor("d_mh", [128, 2], F32, kind="ExternalOutput").ap()
        d_xn = nc.dram_tensor("d_xn", [128, 8, S], F32, kind="ExternalOutput").ap()
        d_wd = nc.dram_tensor("d_wd", [128, TC, E], F32, kind="ExternalOutput").ap()
        d_acc2 = nc.dram_tensor("d_acc2", [128, TC, D], F32, kind="ExternalOutput").ap()
        d_bb = nc.dram_tensor("d_bb", [128, 3, D], F32, kind="ExternalOutput").ap()
        d_ln = nc.dram_tensor("d_ln", [128, 8], F32, kind="ExternalOutput").ap()
        d_rt = nc.dram_tensor("d_rt", [128, 2, D], F32, kind="ExternalOutput").ap()
        d_nq = nc.dram_tensor("d_nq", [1, 32], F32, kind="ExternalOutput").ap()
        d_lg = nc.dram_tensor("d_lg", [128, 16, 8], F32, kind="ExternalOutput").ap()

    wqkvT_s = nc.dram_tensor("wqkvT_s", [128, 8, 3 * D], BF16, kind="Internal").ap()
    woT_s = nc.dram_tensor("woT_s", [128, 8, D], BF16, kind="Internal").ap()
    xfm_s = nc.dram_tensor("xfm_s", [128, 8, S], BF16, kind="Internal").ap()
    w1s = nc.dram_tensor("w1s", [E, FBLK, 128, 8, 512], BF16, kind="Internal").ap()
    w2s = nc.dram_tensor("w2s", [E, FC, 128, D], BF16, kind="Internal").ap()

    def nbytes(shape, dtype):
        nb = 1
        for x in shape[1:]:
            nb *= x
        return nb * (4 if dtype == F32 else 2)

    cursor = [26624]

    def al(name, shape, dtype, align=64):
        cursor[0] = -(-cursor[0] // align) * align
        t = nc.alloc_sbuf_tensor_at(name, list(shape), dtype, offset=cursor[0])
        cursor[0] += nbytes(shape, dtype)
        return t

    acc = al("acc", [128, TC, D], F32)
    bias_b = al("bias_b", [128, D], F32)
    gamma_b = al("gamma_b", [128, D], F32)
    beta_b = al("beta_b", [128, D], F32)
    router_T = al("router_T", [128, 8, E], F32)
    ident_f = al("ident_f", [128, 128], F32)
    ones_b = al("ones_b", [128, 128], BF16)
    ones_f = al("ones_f", [33, 128], F32)
    normones = al("normones", [128, 2], BF16)
    normf = al("normf", [128, 2, 128], BF16)
    nqb128 = al("nqb128", [128, 16], F32)
    mt = al("mt", [128, 2], F32)
    epsb = al("epsb", [128, 1], F32)
    scjunk = al("scjunk", [128, 192], F32)
    cl2 = al("cl2", [128, 2], F32)
    rt_s = al("rt_s", [128, 2], F32)
    wdense = al("wdense", [128, TC, E], F32)
    mhatb = al("mhatb", [128, 2], F32)

    rsi = al("rsi", [33, 512], F32)
    bc_sb = al("bc_sb", [128, 512], F32)
    nq = al("nq", [1, 32], F32)
    sm1 = al("sm1", [1, 8], F32)
    smb = al("smb", [1, 8], BF16)
    red = al("red", [128, 24], F32)
    lnm = al("lnm", [128, 2], F32)
    nqb = al("nqb", [1, 16], F32)
    mv = al("mv", [128, 2], F32)
    lnt = al("lnt", [128, 4], F32)
    r_m1 = al("r_m1", [128, 4], F32)
    r_pk = al("r_pk", [128, 4, E], F32)
    assert cursor[0] <= 117248, cursor[0]

    PH = 117248
    TOP = 229344

    def at(name, shape, dtype, off):
        assert PH + off + nbytes(shape, dtype) <= TOP, name
        return nc.alloc_sbuf_tensor_at(name, list(shape), dtype, offset=PH + off)

    a_xslab = at("a_xslab", [128, 2, 8, 512], BF16, 0)
    a_wg = at("a_wg", [128, 2, 8, 768], BF16, 16384)
    a_q = at("a_q", [128, 2, S], BF16, 40960)
    a_k = at("a_k", [128, 2, S], BF16, 49152)
    a_v = at("a_v", [128, 16, 4, 64], BF16, 57344)
    a_attn = at("a_attn", [128, 8, S], BF16, 65664)
    a_xstage = at("a_xstage", [128, 2, D], F32, 98432)
    a_xcast = at("a_xcast", [128, 2, D], BF16, 106624)
    sqbuf = at("sqbuf", [128, 2, 2048], BF16, 98432)
    P_T = at("P_T", [128, 4, 2, 512], BF16, 98432)
    a_fmtile = at("a_fmtile", [128, 2, 8, 128], BF16, 16384)
    woT = at("woT", [128, 8, D], BF16, 40960)
    r_tmp = at("r_tmp", [128, 2, D], F32, 0)
    o_lnscr = at("o_lnscr", [128, D], F32, 8192)
    f_xn2fm = at("f_xn2fm", [128, 8, S], BF16, 0)
    f_h = at("f_h", [128, FC, 512], BF16, 32768)
    f_w1 = at("f_w1", [128, 2, 8, 512], BF16, 65664)
    f_w2 = at("f_w2", [128, 4, D], BF16, 82048)
    f_xn2tm = at("f_xn2tm", [128, 2, D], F32, 90240)
    f_out = at("f_out", [128, 2, D], F32, 90240)
    f_xnf = at("f_xnf", [128, 2, 128], F32, 107520)
    gp_j = at("gp_j", [128, 128], F32, 32768)
    lgd = at("lgd", [128, 16, 8], F32, 33280)
    f_xnlc = at("f_xnlc", [128, 2, D], BF16, 33792)
    f_xnlo = at("f_xnlo", [128, 2, 8, 128], BF16, 37888)
    rwhl = at("rwhl", [128, 2, 8, E], BF16, 42496)
    f_lnscr = at("f_lnscr", [128, D], F32, 98432)
    f_xncast = at("f_xncast", [128, 2, D], BF16, 103424)
    p_w = at("p_w", [128, 2, D], F32, 40960)
    p_m = at("p_m", [128, 2, D], F32, 49152)
    p_gt = at("p_gt", [128, 2, D], BF16, 57344)
    p_gated = at("p_gated", [128, 2, D], BF16, 61440)
    p_t = at("p_t", [128, 2, 8, 128], BF16, 65664)
    c_in = at("c_in", [128, 2, 2048], F32, 69760)
    c_out = at("c_out", [128, 2, 2048], BF16, 86144)
    p_vec = at("p_vec", [8, 6 * 1024], F32, 0)

    ps = nc.alloc_psum_tensor("ps", [128, 8, 512], F32)

    es = contextlib.ExitStack()
    SEMS = {}

    def sem(n):
        SEMS[n] = es.enter_context(nc.semaphore(n))
        return SEMS[n]

    for n in ("sd_misc sd_pw sd_pt sd_cvin sd_cvout sd_x sd_xt sd_xs sd_slab "
              "sd_wg sd_wo sd_res sd_xnt sd_w1 sd_w2 sd_out sv_init sv_gate "
              "sv_cv sv_pb sv_xc st_proj sv_qkv st_s ss_exp st_pv sv_rec "
              "st_bc sv_attn st_nq sv_nq sv_nb sv_sq st_op sv_r1 st_tp sv_xnf "
              "st_lg sv_wd st_h ss_h st_y sv_acc sv_out sv_xn st_rtp sv_ln "
              "ss_ln ss_wd ss_nb sg_wd sv_w2c").split():
        sem(n)
    g = SEMS

    CNT = {}

    def add(name, n):
        CNT[name] = CNT.get(name, 0) + n

    def cur(name):
        return CNT.get(name, 0)

    blk = es.enter_context(nc.Block())

    def p0_sync(sy):
        sy.dma_start(out=p_vec[0:1, 0:D], in_=bo).then_inc(g["sd_misc"], 16)
        sy.dma_start(out=p_vec[0:1, D:2 * D], in_=mbo).then_inc(g["sd_misc"], 16)
        sy.dma_start(out=p_vec[0:1, 2 * D:3 * D], in_=gamma).then_inc(g["sd_misc"], 16)
        sy.dma_start(out=p_vec[0:1, 3 * D:4 * D], in_=beta).then_inc(g["sd_misc"], 16)
        sy.dma_start(out=p_vec[0:8, 4 * D:5 * D], in_=rw).then_inc(g["sd_misc"], 16)
        for oc in range(32):
            b = oc % 2
            if oc >= 2:
                sy.wait_ge(g["sd_pt"], 16 * 9 * (oc - 1))
            wsrc = wqkv[oc * 128:(oc + 1) * 128, :] if oc < 24 else \
                wo[(oc - 24) * 128:(oc - 23) * 128, :]
            msrc = mqkv[oc * 128:(oc + 1) * 128, :] if oc < 24 else \
                mo[(oc - 24) * 128:(oc - 23) * 128, :]
            sy.dma_start(out=p_w[:, b, :], in_=wsrc).then_inc(g["sd_pw"], 16)
            sy.dma_start(out=p_m[:, b, :], in_=msrc).then_inc(g["sd_pw"], 16)
            sy.wait_ge(g["sv_gate"], oc + 1)
            for kc in range(8):
                sy.dma_start_transpose(
                    p_t[:, b, kc, :], p_gated[:, b, kc * 128:(kc + 1) * 128]
                ).then_inc(g["sd_pt"], 16)
            sy.wait_ge(g["sd_pt"], 16 * (9 * oc + 8))
            dst = wqkvT_s[:, :, oc * 128:(oc + 1) * 128] if oc < 24 else \
                woT_s[:, :, (oc - 24) * 128:(oc - 23) * 128]
            sy.dma_start(out=dst, in_=p_t[:, b, :, :]).then_inc(g["sd_pt"], 16)
        u2 = 0
        for e in range(E):
            for fc in range(FC):
                b2 = u2 % 2
                if u2 >= 2:
                    sy.wait_ge(g["sv_w2c"], u2 - 1)
                sy.dma_start(out=c_in[:, 1, b2 * 1024:(b2 + 1) * 1024],
                             in_=w2[e, fc * 128:(fc + 1) * 128, :]).then_inc(
                    g["sd_w1"], 16)
                sy.wait_ge(g["sv_w2c"], u2 + 1)
                sy.dma_start(out=w2s[e, fc],
                             in_=c_out[:, 1, b2 * 1024:(b2 + 1) * 1024]
                             ).then_inc(g["sd_w2"], 16)
                u2 += 1

    def p0_gp(gp):
        gp.memset(ident_f[:], 0.0)
        gp.affine_select(out=ident_f[:], in_=ident_f[:],
                         compare_op=ALU.not_equal, fill=1.0, base=0,
                         pattern=[[-1, 128]], channel_multiplier=1)
        gp.memset(ones_b[:], 1.0)
        gp.memset(ones_f[:], 1.0)
        gp.memset(normones[:], 0.0)
        gp.memset(normones[0:64, 0:1], 1.0)
        gp.memset(normones[64:128, 1:2], 1.0)
        gp.memset(epsb[:], EPS)
        gp.memset(mhatb[:], 0.0)
        gp.memset(red[:], 1.0)
        gp.memset(nqb128[:], 1.0)
        gp.memset(mt[:], 1.0)
        gp.memset(lnm[:], 0.0)
        gp.memset(lnt[:], 1.0)
        gp.memset(r_m1[:], 0.0)
        gp.memset(r_pk[:], 0.0)
        gp.memset(bc_sb[:], 0.0)
        gp.memset(cl2[:, 0:1], 60.0)
        gp.memset(cl2[:, 1:2], -60.0)
        gp.memset(normf[:], 0.0)
        gp.memset(normf[0:64, 0, :], 1.0)
        gp.memset(normf[64:128, 1, :], 1.0)
        gp.engine_nop().then_inc(g["sv_init"], 1)
        u = 0
        for e in range(E):
            for dc in range(8):
                for hf in range(2):
                    if u >= 1:
                        gp.wait_ge(g["sd_cvout"], 16 * u)
                    gp.dma_start(
                        out=c_in[:, 0, :],
                        in_=w1[e, dc * 128:(dc + 1) * 128,
                               hf * 2048:(hf + 1) * 2048]).then_inc(
                        g["sd_cvin"], 16)
                    gp.wait_ge(g["sv_cv"], u + 1)
                    gp.dma_start(
                        out=w1s[e, 4 * hf:4 * hf + 4, :, dc, :].rearrange(
                            "b p f -> p b f"),
                        in_=c_out[:, 0, :].rearrange("p (b f) -> p b f", b=4)
                    ).then_inc(g["sd_cvout"], 16)
                    u += 1

    def p0_vec(ve):
        for oc in range(32):
            b = oc % 2
            if oc >= 2:
                ve.wait_ge(g["sd_pt"], 16 * (9 * (oc - 2) + 8))
            ve.wait_ge(g["sd_pw"], 16 * 2 * (oc + 1))
            ve.tensor_scalar(out=p_gt[:, b, :], in0=p_m[:, b, :], scalar1=0.0,
                             scalar2=None, op0=ALU.is_gt)
            ve.tensor_tensor(out=p_gated[:, b, :], in0=p_w[:, b, :],
                             in1=p_gt[:, b, :], op=ALU.mult).then_inc(
                g["sv_gate"], 1)
        u2 = 0
        for u in range(128):
            ve.wait_ge(g["sd_cvin"], 16 * (u + 1))
            if u >= 1:
                ve.wait_ge(g["sd_cvout"], 16 * u)
            ve.tensor_copy(out=c_out[:, 0, :],
                           in_=c_in[:, 0, :]).then_inc(g["sv_cv"], 1)
            for _ in range(2):
                if u2 < 256:
                    b2 = u2 % 2
                    ve.wait_ge(g["sd_w1"], 16 * (u2 + 1))
                    if u2 >= 2:
                        ve.wait_ge(g["sd_w2"], 16 * (u2 - 1))
                    ve.tensor_copy(
                        out=c_out[:, 1, b2 * 1024:(b2 + 1) * 1024],
                        in_=c_in[:, 1, b2 * 1024:(b2 + 1) * 1024]).then_inc(
                        g["sv_w2c"], 1)
                    u2 += 1
        ve.wait_ge(g["sd_misc"], 16 * 5)
        ve.tensor_scalar(out=p_vec[0:1, 5 * D:6 * D], in0=p_vec[0:1, D:2 * D],
                         scalar1=0.0, scalar2=None, op0=ALU.is_gt)
        ve.tensor_tensor(out=p_vec[0:1, 0:D], in0=p_vec[0:1, 0:D],
                         in1=p_vec[0:1, 5 * D:6 * D], op=ALU.mult).then_inc(
            g["sv_pb"], 1)
        for i in range(6):
            ve.wait_ge(g["st_rtp"], i + 1)
            dstt = (bias_b, bias_b, gamma_b, gamma_b, beta_b, beta_b)[i]
            ve.tensor_copy(out=dstt[:, (i % 2) * 512:(i % 2 + 1) * 512],
                           in_=ps[:, i % 2, :]).then_inc(g["sv_pb"], 1)
        for dc in range(8):
            ve.wait_ge(g["st_rtp"], 7 + dc)
            ve.tensor_copy(out=router_T[:, dc, :],
                           in_=ps[:, 2 + dc % 2, 0:E]).then_inc(g["sv_pb"], 1)

    def p0_te(te):
        te.wait_ge(g["sv_init"], 1)
        te.wait_ge(g["sv_pb"], 1)
        for i in range(6):
            srcoff = (0, 0, 2 * D, 2 * D, 3 * D, 3 * D)[i] + (i % 2) * 512
            if i >= 2:
                te.wait_ge(g["sv_pb"], i)
            te.matmul(ps[:, i % 2, :], lhsT=ones_f[0:1, :],
                      rhs=p_vec[0:1, srcoff:srcoff + 512],
                      start=True, stop=True).then_inc(g["st_rtp"], 1)
        for dc in range(8):
            if dc >= 2:
                te.wait_ge(g["sv_pb"], 7 + dc)
            te.transpose(ps[:, 2 + dc % 2, 0:E],
                         p_vec[0:8, 4 * D + dc * 128:4 * D + (dc + 1) * 128],
                         ident_f[0:8, 0:8]).then_inc(g["st_rtp"], 1)

    blk.sync(p0_sync)
    blk.gpsimd(p0_gp)
    blk.vector(p0_vec)
    blk.tensor(p0_te)

    add("sd_misc", 16 * 5)
    add("sd_pw", 16 * 64)
    add("sd_pt", 16 * 9 * 32)
    add("sd_cvin", 16 * 128)
    add("sd_cvout", 16 * 128)
    add("sd_w1", 16 * 256)
    add("sd_w2", 16 * 256)
    add("sv_init", 1)
    add("sv_gate", 32)
    add("sv_cv", 128)
    add("sv_pb", 15)
    add("sv_w2c", 256)
    add("st_rtp", 14)

    def ln_stats(ve, src, scr):
        ve.reduce_sum(out=red[:, 0:1],
                      in_=src.rearrange("p (a b) -> p a b", a=1), axis=AX.X)
        ve.tensor_tensor(out=scr[:], in0=src, in1=src, op=ALU.mult)
        ve.reduce_sum(out=red[:, 1:2],
                      in_=scr[:].rearrange("p (a b) -> p a b", a=1),
                      axis=AX.X).then_inc(g["sv_ln"], 1)

    def sguard(sc):
        sc.activation(out=scjunk[:], in_=bias_b[:, 0:192], func=AF.Copy)
        sc.activation(out=scjunk[:], in_=bias_b[:, 192:384], func=AF.Copy)

    def scg(sc):
        sc.activation(out=scjunk[:, 0:96], in_=bias_b[:, 0:96], func=AF.Copy)
        sc.activation(out=scjunk[:, 96:192], in_=bias_b[:, 0:96], func=AF.Copy)

    def ln_sc_apply(sc, n, src, dst):
        sc.wait_ge(g["sv_ln"], n)
        sguard(sc)
        sc.activation(out=lnm[:, 0:1], in_=red[:, 0:1], func=AF.Copy,
                      scale=1.0 / D)
        sc.activation(out=lnm[:, 1:2], in_=lnm[:, 0:1], func=AF.Square)
        sc.activation(out=lnt[:, 3:4], in_=lnm[:, 1:2], func=AF.Identity,
                      scale=-1.0, bias=epsb[:, 0:1])
        scg(sc)
        sc.activation(out=lnt[:, 0:1], in_=red[:, 1:2], func=AF.Identity,
                      scale=1.0 / D, bias=lnt[:, 3:4])
        sc.activation(out=lnt[:, 1:2], in_=lnt[:, 0:1], func=AF.Ln)
        sc.activation(out=lnt[:, 2:3], in_=lnt[:, 1:2], func=AF.Exp,
                      scale=-0.5)
        scg(sc)
        sc.mul(out=lnt[:, 3:4], in_=lnm[:, 0:1], mul=lnt[:, 2:3])
        sc.activation(out=lnt[:, 3:4], in_=lnt[:, 3:4], func=AF.Identity,
                      scale=-1.0)
        scg(sc)
        return sc.activation(out=dst, in_=src, func=AF.Identity,
                             scale=lnt[:, 2:3], bias=lnt[:, 3:4])

    for s in range(1 if dbg else 2):
        X0 = cur("sd_x"); XT0 = cur("sd_xt"); XS0 = cur("sd_xs")
        XC0 = cur("sv_xc")

        def a0_sync(sy, s=s, X0=X0, XT0=XT0, XS0=XS0, XC0=XC0):
            if s == 0:
                sy.wait_ge(g["sd_pt"], 16 * 9 * 32)
                sy.wait_ge(g["sd_cvout"], 16 * 128)
                sy.wait_ge(g["sd_w2"], 16 * 256)
                sy.wait_ge(g["sv_pb"], 15)
            else:
                sy.wait_ge(g["sv_out"], 16 * s)
                sy.wait_ge(g["sd_out"], 16 * 16 * s)
            for tc in range(TC):
                b = tc % 2
                if tc >= 2:
                    sy.wait_ge(g["sd_xs"], XS0 + 16 * (tc - 1))
                sy.dma_start(out=a_xstage[:, b, :],
                             in_=x_in[s, tc * 128:(tc + 1) * 128, :]).then_inc(
                    g["sd_x"], 16)
                sy.wait_ge(g["sv_xc"], XC0 + tc + 1)
                for kc in range(8):
                    sy.dma_start_transpose(
                        a_fmtile[:, b, kc, :],
                        a_xcast[:, b, kc * 128:(kc + 1) * 128]).then_inc(
                        g["sd_xt"], 16)
                sy.wait_ge(g["sd_xt"], XT0 + 16 * (8 * tc + 8))
                sy.dma_start(out=xfm_s[:, :, tc * 128:(tc + 1) * 128],
                             in_=a_fmtile[:, b, :, :]).then_inc(g["sd_xs"], 16)

        def a0_vec(ve, s=s, X0=X0, XT0=XT0):
            if s == 0:
                ve.wait_ge(g["sd_pt"], 16 * 9 * 32)
                ve.wait_ge(g["sv_pb"], 15)
            else:
                ve.wait_ge(g["sv_out"], 16 * s)
            for tc in range(TC):
                b = tc % 2
                if tc >= 2:
                    ve.wait_ge(g["sd_xt"], XT0 + 16 * 8 * (tc - 1))
                ve.wait_ge(g["sd_x"], X0 + 16 * (tc + 1))
                ve.tensor_copy(out=a_xcast[:, b, :],
                               in_=a_xstage[:, b, :]).then_inc(g["sv_xc"], 1)

        blk.sync(a0_sync)
        blk.vector(a0_vec)
        add("sd_x", 16 * TC)
        add("sd_xt", 16 * 8 * TC)
        add("sd_xs", 16 * TC)
        add("sv_xc", TC)

        WG0 = cur("sd_wg"); SLAB0 = cur("sd_slab"); PROJ0 = cur("st_proj")
        QKV0 = cur("sv_qkv"); SQ0 = cur("sv_sq"); NQ0 = cur("st_nq")
        NQE0 = cur("sv_nq"); NB0 = cur("sv_nb"); S0 = cur("st_s")
        EXP0 = cur("ss_exp"); PV0 = cur("st_pv"); REC0 = cur("sv_rec")
        BC0 = cur("st_bc"); ATT0 = cur("sv_attn"); SSNB0 = cur("ss_nb")

        def a_sync(sy, s=s, WG0=WG0, SLAB0=SLAB0, PROJ0=PROJ0, ATT0=ATT0):
            sy.wait_ge(g["sd_xs"], cur("sd_xs"))
            for gi in range(NG):
                bg = gi % 2
                if gi >= 2:
                    sy.wait_ge(g["st_proj"], PROJ0 + 8 * QS * (gi - 1))
                qo, ko, vo = 256 * gi, D + 256 * gi, 2 * D + 256 * gi
                for i, o0 in enumerate((qo, ko, vo)):
                    sy.dma_start(out=a_wg[:, bg, :, i * 256:(i + 1) * 256],
                                 in_=wqkvT_s[:, :, o0:o0 + 256]).then_inc(
                        g["sd_wg"], 16)
                for qs in range(QS):
                    u = gi * QS + qs
                    if u >= 2:
                        sy.wait_ge(g["st_proj"], PROJ0 + 8 * (u - 1))
                    sy.dma_start(out=a_xslab[:, u % 2, :, :],
                                 in_=xfm_s[:, :, qs * 512:(qs + 1) * 512]
                                 ).then_inc(g["sd_slab"], 16)
            sy.wait_ge(g["sv_attn"], ATT0 + 32)
            sy.dma_start(out=woT[:], in_=woT_s).then_inc(g["sd_wo"], 16)

        def a_te(te, s=s, WG0=WG0, SLAB0=SLAB0, PROJ0=PROJ0, QKV0=QKV0,
                 SQ0=SQ0, NQ0=NQ0, NQE0=NQE0, NB0=NB0, S0=S0, EXP0=EXP0,
                 PV0=PV0, REC0=REC0, BC0=BC0, ATT0=ATT0):
            nproj = 0
            nunit = 0
            for gi in range(NG):
                bg = gi % 2
                te.wait_ge(g["sd_wg"], WG0 + 16 * 3 * (gi + 1))
                for qs in range(QS):
                    u = gi * QS + qs
                    te.wait_ge(g["sd_slab"], SLAB0 + 16 * (u + 1))
                    for which in range(2):
                        for oc in range(2):
                            bank = nproj % 3
                            if nproj >= 3:
                                te.wait_ge(g["sv_qkv"], QKV0 + nproj - 2)
                            toff = which * 256 + oc * 128
                            for kc in range(8):
                                mmi = te.matmul(
                                    ps[:, bank, :],
                                    lhsT=a_wg[:, bg, kc, toff:toff + 128],
                                    rhs=a_xslab[:, u % 2, kc, :],
                                    start=(kc == 0), stop=(kc == 7))
                            mmi.then_inc(g["st_proj"], 1)
                            nproj += 1
                    for tcw in range(4):
                        bank = nproj % 3
                        if nproj >= 3:
                            te.wait_ge(g["sv_qkv"], QKV0 + nproj - 2)
                        for kc in range(8):
                            mmi = te.matmul(
                                ps[:, bank, 0:256],
                                lhsT=a_xslab[:, u % 2, kc,
                                             tcw * 128:(tcw + 1) * 128],
                                rhs=a_wg[:, bg, kc, 512:768],
                                start=(kc == 0), stop=(kc == 7))
                        mmi.then_inc(g["st_proj"], 1)
                        nproj += 1
                for pc in range(NPAIR):
                    pi = 2 * gi + pc
                    te.wait_ge(g["sv_attn"], ATT0 + 4 * pi)
                    te.wait_ge(g["sv_sq"], SQ0 + 2 * pi + 2)
                    for i in range(16):
                        qk, hp, q4 = i // 8, (i // 4) % 2, i % 4
                        if i >= 2:
                            te.wait_ge(g["sv_nq"], NQE0 + 16 * pi + i - 1)
                        te.matmul(ps[:, 4 + i % 2, :],
                                  lhsT=normf[:, hp, :],
                                  rhs=sqbuf[:, qk, q4 * 512:(q4 + 1) * 512],
                                  start=True, stop=True).then_inc(g["st_nq"], 1)
                    te.wait_ge(g["sv_nb"], NB0 + pi + 1)
                    for qs in range(QS):
                        u = nunit
                        up = u % 2
                        hh0 = 2 * pc

                        def s_mm(jc, u=u):
                            jcp = jc % 2
                            sb = 2 * jcp
                            if jc >= 2:
                                te.wait_ge(g["ss_exp"], EXP0 + 16 * u + jc - 1)
                            elif u >= 1:
                                te.wait_ge(g["ss_exp"], EXP0 + 16 * u)
                            te.matmul(ps[:, sb, :],
                                      lhsT=a_k[0:64, pc, jc * 128:(jc + 1) * 128],
                                      rhs=a_q[0:64, pc, qs * 512:(qs + 1) * 512],
                                      start=True, stop=True, tile_position=(0, 0))
                            te.matmul(ps[:, sb + 1, :],
                                      lhsT=a_k[64:128, pc,
                                               jc * 128:(jc + 1) * 128],
                                      rhs=a_q[64:128, pc,
                                              qs * 512:(qs + 1) * 512],
                                      start=True, stop=True,
                                      tile_position=(64, 0)).then_inc(g["st_s"], 1)

                        def pv_mm(jc, u=u, up=up, hh0=hh0):
                            jcp = jc % 4
                            te.wait_ge(g["ss_exp"], EXP0 + 16 * u + jc + 1)
                            if jc == 0:
                                if u >= 2:
                                    te.wait_ge(g["sv_attn"], ATT0 + u - 1)
                            te.matmul(ps[0:64, 4 + up, :],
                                      lhsT=a_v[:, jc, hh0, :],
                                      rhs=P_T[:, jcp, 0, :],
                                      start=(jc == 0), stop=(jc == JC - 1),
                                      tile_position=(0, 0))
                            te.matmul(ps[64:128, 4 + up, :],
                                      lhsT=a_v[:, jc, hh0 + 1, :],
                                      rhs=P_T[:, jcp, 1, :],
                                      start=(jc == 0), stop=(jc == JC - 1),
                                      tile_position=(0, 64))
                            te.matmul(ps[0:64, 6 + up, :],
                                      lhsT=ones_b[:, 0:64],
                                      rhs=P_T[:, jcp, 0, :],
                                      start=(jc == 0), stop=(jc == JC - 1),
                                      tile_position=(0, 0))
                            mmi = te.matmul(ps[64:128, 6 + up, :],
                                            lhsT=ones_b[:, 0:64],
                                            rhs=P_T[:, jcp, 1, :],
                                            start=(jc == 0), stop=(jc == JC - 1),
                                            tile_position=(0, 64))
                            mmi.then_inc(g["st_pv"], 1)

                        s_mm(0)
                        s_mm(1)
                        for jc in range(2, JC):
                            s_mm(jc)
                            pv_mm(jc - 2)
                        pv_mm(JC - 2)
                        pv_mm(JC - 1)
                        nunit += 1

        def a_vec(ve, s=s, PROJ0=PROJ0, QKV0=QKV0, SQ0=SQ0, NQ0=NQ0,
                  NQE0=NQE0, NB0=NB0, EXP0=EXP0, PV0=PV0, REC0=REC0,
                  BC0=BC0, ATT0=ATT0, SSNB0=SSNB0):
            nproj = 0
            nunit = 0
            for gi in range(NG):
                for qs in range(QS):
                    for which in range(2):
                        for oc in range(2):
                            ve.wait_ge(g["st_proj"], PROJ0 + nproj + 1)
                            dst = a_q if which == 0 else a_k
                            ve.tensor_copy(
                                out=dst[:, oc, qs * 512:(qs + 1) * 512],
                                in_=ps[:, nproj % 3, :]).then_inc(g["sv_qkv"], 1)
                            nproj += 1
                    for tcw in range(4):
                        ve.wait_ge(g["st_proj"], PROJ0 + nproj + 1)
                        ve.tensor_copy(
                            out=a_v[:, qs * 4 + tcw, :, :],
                            in_=ps[:, nproj % 3, 0:256].rearrange(
                                "p (h d) -> p h d", h=4)).then_inc(g["sv_qkv"], 1)
                        nproj += 1
                for pc in range(NPAIR):
                    pi = 2 * gi + pc
                    if pi >= 1:
                        ve.wait_ge(g["st_nq"], NQ0 + 16 * pi)
                        ve.wait_ge(g["ss_exp"], EXP0 + 64 * pi)
                        ve.wait_ge(g["st_pv"], PV0 + 16 * 4 * pi)
                    ve.tensor_tensor(out=sqbuf[:, 0, :], in0=a_q[:, pc, :],
                                     in1=a_q[:, pc, :], op=ALU.mult)
                    ve.tensor_tensor(out=sqbuf[:, 1, :], in0=a_k[:, pc, :],
                                     in1=a_k[:, pc, :],
                                     op=ALU.mult).then_inc(g["sv_sq"], 2)
                    for i in range(16):
                        ve.wait_ge(g["st_nq"], NQ0 + 16 * pi + i + 1)
                        ve.reduce_max(out=red[:, i:i + 1],
                                      in_=ps[:, 4 + i % 2, :],
                                      axis=AX.X).then_inc(g["sv_nq"], 1)
                    ve.wait_ge(g["ss_nb"], SSNB0 + 2 * pi + 1)
                    ve.tensor_copy(out=a_xcast[:, 0, :].bitcast(F32)
                                   [:, 0:256], in_=bias_b[:, 0:256])
                    ve.reduce_max(out=red[:, 16:17], in_=nqb128[:, 0:8],
                                  axis=AX.X)
                    ve.reduce_max(out=red[:, 17:18], in_=nqb128[:, 8:16],
                                  axis=AX.X).then_inc(g["sv_nb"], 1)
                    if dbg:
                        ve.wait_ge(g["ss_nb"], SSNB0 + 2 * pi + 2)
                        ve.tensor_copy(out=nq[0:1, 16 + pi:17 + pi],
                                       in_=mhatb[0:1, pc:pc + 1])
                    for qs in range(QS):
                        u = nunit
                        up = u % 2
                        ve.wait_ge(g["st_pv"], PV0 + 16 * u + 16)
                        ve.reciprocal(out=bc_sb[:], in_=ps[:, 6 + up, :])
                        ve.tensor_copy(out=a_xcast[:, 0, :].bitcast(F32)
                                       [:, 0:256], in_=bias_b[:, 0:256])
                        ve.tensor_copy(out=a_xcast[:, 1, :].bitcast(F32)
                                       [:, 0:256], in_=bias_b[:, 0:256])
                        ve.tensor_tensor(
                            out=a_attn[:, pi, qs * 512:(qs + 1) * 512],
                            in0=ps[:, 4 + up, :], in1=bc_sb[:],
                            op=ALU.mult).then_inc(g["sv_attn"], 1)
                        nunit += 1

        def a_sc(sc, s=s, S0=S0, EXP0=EXP0, PV0=PV0, NB0=NB0, NQE0=NQE0,
                 SSNB0=SSNB0):
            nunit = 0
            for gi in range(NG):
                for pc in range(NPAIR):
                    pi = 2 * gi + pc
                    sc.wait_ge(g["sv_nq"], NQE0 + 16 * pi + 16)
                    sguard(sc)
                    sc.activation(out=nqb128[:], in_=red[:, 0:16],
                                  func=AF.Copy).then_inc(g["ss_nb"], 1)
                    sc.wait_ge(g["sv_nb"], NB0 + pi + 1)
                    sguard(sc)
                    sc.activation(out=mt[:, 0:1], in_=red[:, 16:17],
                                  func=AF.Copy)
                    scg(sc)
                    sc.activation(out=mt[:, 1:2], in_=red[:, 17:18],
                                  func=AF.Identity, bias=mt[:, 0:1])
                    sc.mul(out=mt[:, 1:2], in_=mt[:, 1:2],
                           mul=SCALE * 1.01 * 0.5)
                    sc.activation(out=mt[:, 0:1], in_=mt[:, 1:2],
                                  func=AF.Relu)
                    sc.activation(out=mt[:, 1:2], in_=mt[:, 0:1],
                                  func=AF.Identity, scale=-1.0, bias=cl2[:, 0:1])
                    sc.activation(out=mt[:, 1:2], in_=mt[:, 1:2], func=AF.Relu)
                    sc.activation(out=mhatb[:, pc:pc + 1], in_=mt[:, 1:2],
                                  func=AF.Identity, scale=1.0,
                                  bias=cl2[:, 1:2])
                    scg(sc)
                    sc.engine_nop().then_inc(g["ss_nb"], 1) if False else \
                        sc.activation(out=scjunk[:, 0:64],
                                      in_=bias_b[:, 0:64],
                                      func=AF.Copy).then_inc(g["ss_nb"], 1)
                    for qs in range(QS):
                        u = nunit
                        for jc in range(JC):
                            jcp = jc % 2
                            sc.wait_ge(g["st_s"], S0 + 16 * u + jc + 1)
                            if jc >= 4:
                                sc.wait_ge(g["st_pv"], PV0 + 16 * u + jc - 3)
                            elif u >= 1:
                                sc.wait_ge(g["st_pv"], PV0 + 16 * u)
                            sc.activation(
                                out=P_T[:, jc % 4, :, :],
                                in_=ps[:, 2 * jcp:2 * jcp + 2, :],
                                func=AF.Exp, bias=mhatb[:, pc:pc + 1],
                                scale=SCALE).then_inc(g["ss_exp"], 1)
                        nunit += 1

        blk.sync(a_sync)
        blk.tensor(a_te)
        blk.vector(a_vec)
        blk.scalar(a_sc)

        add("sd_wg", 16 * 3 * NG)
        add("sd_slab", 16 * NG * QS)
        add("sd_wo", 16)
        add("st_proj", 8 * NG * QS)
        add("sv_qkv", 8 * NG * QS)
        add("sv_sq", 2 * NG * NPAIR)
        add("st_nq", 16 * NG * NPAIR)
        add("sv_nq", 16 * NG * NPAIR)
        add("sv_nb", NG * NPAIR)
        add("ss_nb", 2 * NG * NPAIR)
        add("st_s", 16 * 32)
        add("ss_exp", 16 * 32)
        add("st_pv", 16 * 32)
        add("sv_attn", 32)

        OP0 = cur("st_op"); R10 = cur("sv_r1"); RES0 = cur("sd_res")
        LNV0 = cur("sv_ln"); LNS0 = cur("ss_ln"); ATT1 = cur("sv_attn")

        def o_gp(gp, s=s, ATT1=ATT1, R10=R10):
            gp.wait_ge(g["sv_attn"], ATT1)
            for tc in range(TC):
                if tc >= 2:
                    gp.wait_ge(g["sv_r1"], R10 + tc - 1)
                gp.dma_start(out=a_xstage[:, tc % 2, :],
                             in_=x_in[s, tc * 128:(tc + 1) * 128, :]).then_inc(
                    g["sd_res"], 16)

        def o_te(te, s=s, R10=R10):
            te.wait_ge(g["sd_wo"], 16 * (s + 1))
            for tc in range(TC):
                bs = (tc % 2) * 4
                if tc >= 2:
                    te.wait_ge(g["sv_r1"], R10 + tc - 1)
                for p in range(8):
                    for dh in range(2):
                        te.matmul(ps[:, bs + dh, :],
                                  lhsT=a_attn[0:64, p, tc * 128:(tc + 1) * 128],
                                  rhs=woT[0:64, p, dh * 512:(dh + 1) * 512],
                                  start=(p == 0), stop=(p == 7),
                                  tile_position=(0, 0))
                        mmi = te.matmul(
                            ps[:, bs + 2 + dh, :],
                            lhsT=a_attn[64:128, p, tc * 128:(tc + 1) * 128],
                            rhs=woT[64:128, p, dh * 512:(dh + 1) * 512],
                            start=(p == 0), stop=(p == 7),
                            tile_position=(64, 0))
                mmi.then_inc(g["st_op"], 1)

        def o_vec(ve, s=s, OP0=OP0, R10=R10, RES0=RES0, LNS0=LNS0):
            for tc in range(TC):
                b = tc % 2
                bs = (tc % 2) * 4
                ve.wait_ge(g["st_op"], OP0 + tc + 1)
                ve.wait_ge(g["sd_res"], RES0 + 16 * (tc + 1))
                ve.scalar_tensor_tensor(
                    out=r_tmp[:, b, :],
                    in0=ps[:, bs:bs + 2, :].rearrange("p a b -> p (a b)"),
                    scalar=0.0, in1=bias_b[:], op0=ALU.bypass, op1=ALU.add)
                ve.scalar_tensor_tensor(
                    out=r_tmp[:, b, :],
                    in0=ps[:, bs + 2:bs + 4, :].rearrange("p a b -> p (a b)"),
                    scalar=0.0, in1=r_tmp[:, b, :], op0=ALU.bypass, op1=ALU.add)
                ve.tensor_tensor(out=r_tmp[:, b, :], in0=r_tmp[:, b, :],
                                 in1=a_xstage[:, b, :], op=ALU.add)
                if dbg == 4:
                    ve.tensor_copy(out=acc[:, tc, :], in_=r_tmp[:, b, :])
                    ve.engine_nop().then_inc(g["sv_r1"], 1)
                else:
                    ve.engine_nop().then_inc(g["sv_r1"], 1)
                    if tc >= 2:
                        ve.wait_ge(g["ss_ln"], LNS0 + tc - 1)
                    ln_stats(ve, r_tmp[:, b, :], o_lnscr)

        def o_sc(sc, s=s, LNV0=LNV0):
            if dbg == 4:
                return
            for tc in range(TC):
                ln_sc_apply(sc, LNV0 + tc + 1, r_tmp[:, tc % 2, :],
                            acc[:, tc, :]).then_inc(g["ss_ln"], 1)

        blk.gpsimd(o_gp)
        blk.tensor(o_te)
        blk.vector(o_vec)
        blk.scalar(o_sc)

        add("sd_res", 16 * TC)
        add("st_op", TC)
        add("sv_r1", TC)
        add("sv_ln", TC)
        add("ss_ln", TC)

        if dbg in (1, 4):
            def dbg_sync(sy, LNS0=LNS0):
                sy.wait_ge(g["sv_r1"], cur("sv_r1"))
                if dbg == 1:
                    sy.wait_ge(g["ss_ln"], LNS0 + TC)
                for nm, src_t in ((d_r1, acc), (d_attn, None), (d_q, None)):
                    pass
                sy.dma_start(out=d_r1, in_=acc[:]).then_inc(g["sd_out"], 16)
                sy.dma_start(out=d_mh, in_=mhatb[:]).then_inc(g["sd_out"], 16)
                sy.dma_start(out=d_bb[:, 0, :], in_=bias_b[:]).then_inc(g["sd_out"], 16)
                sy.dma_start(out=d_bb[:, 1, :], in_=gamma_b[:]).then_inc(g["sd_out"], 16)
                sy.dma_start(out=d_bb[:, 2, :], in_=beta_b[:]).then_inc(g["sd_out"], 16)
                sy.dma_start(out=d_ln[:, 0:2], in_=mv[:]).then_inc(g["sd_out"], 16)
                sy.dma_start(out=d_ln[:, 2:6], in_=lnt[:]).then_inc(g["sd_out"], 16)
                sy.dma_start(out=d_rt[:, 0, :], in_=r_tmp[:, 1, :]).then_inc(g["sd_out"], 16)
                sy.dma_start(out=d_rt[:, 1, :], in_=o_lnscr[:]).then_inc(g["sd_out"], 16)
                sy.dma_start(out=d_nq, in_=nq[0:1, :]).then_inc(g["sd_out"], 16)
                sy.wait_ge(g["sd_out"], cur("sd_out") + 16 * 10)
            def dbg_vec(ve):
                ve.wait_ge(g["sv_attn"], cur("sv_attn"))
                ve.wait_ge(g["sd_out"], cur("sd_out") + 16 * 9)
                ve.tensor_copy(out=bcf32[:], in_=a_attn[:, :, :]).then_inc(g["sv_out"], 1)
                ve.tensor_copy(out=qf32[:], in_=a_q[:]).then_inc(g["sv_out"], 1)
                ve.tensor_copy(out=vf32[:], in_=a_v[:]).then_inc(g["sv_out"], 1)
                ve.wait_ge(g["sd_out"], cur("sd_out") + 16 * 7)
                ve.tensor_copy(out=qf32[:], in_=a_k[:]).then_inc(g["sv_out"], 1)
            def dbg_sync2(sy):
                sy.wait_ge(g["sv_out"], 3)
                sy.dma_start(out=d_attn, in_=bcf32[:]).then_inc(g["sd_out"], 16)
                sy.dma_start(out=d_q, in_=qf32[:]).then_inc(g["sd_out"], 16)
                sy.dma_start(out=d_v, in_=vf32[:]).then_inc(g["sd_out"], 16)
                sy.wait_ge(g["sv_out"], 4)
                sy.dma_start(out=d_k, in_=qf32[:]).then_inc(g["sd_out"], 16)
                sy.wait_ge(g["sd_out"], cur("sd_out") + 16 * 9)
            bcf32 = at("bcf32", [128, 8, S], F32, 0)
            qf32 = at("qf32", [128, 2, S], F32, 65664)
            vf32 = at("vf32", [128, 16, 4, 64], F32, 82048)
            blk.vector(dbg_vec)
            blk.sync(dbg_sync)
            blk.sync(dbg_sync2)
            es.close()
            return nc

        XN0 = cur("sv_xn"); XNT0 = cur("sd_xnt"); TP0 = cur("st_tp")
        XNF0 = cur("sv_xnf"); LG0 = cur("st_lg"); WD0 = cur("sv_wd")
        SSWD0 = cur("ss_wd"); SG0 = cur("sg_wd")
        LNV1 = cur("sv_ln"); LNS1 = cur("ss_ln"); R11 = cur("sv_r1")

        def f1_vec(ve, s=s, XN0=XN0, XNT0=XNT0, TP0=TP0, XNF0=XNF0, LG0=LG0,
                   WD0=WD0, LNS1=LNS1, R11=R11, SSWD0=SSWD0, SG0=SG0,
                   LNSO=LNS0):
            ve.tensor_copy(out=rwhl[:, 0, :, :],
                           in_=router_T[:, :, :])
            ve.tensor_tensor(out=rwhl[:, 1, :, :], in0=router_T[:, :, :],
                             in1=rwhl[:, 0, :, :],
                             op=ALU.subtract).then_inc(g["st_rtp"], 1)
            for tc in range(TC):
                b = tc % 2
                ve.wait_ge(g["ss_ln"], LNSO + tc + 1)
                if tc >= 2:
                    ve.wait_ge(g["sd_xnt"], XNT0 + 16 * 16 * (tc - 1))
                ln_stats(ve, acc[:, tc, :], f_lnscr)
                ve.wait_ge(g["ss_ln"], LNS1 + tc + 1)
                ve.tensor_copy(out=f_lnscr[:, 0:256], in_=bias_b[:, 0:256])
                ve.tensor_copy(out=f_lnscr[:, 256:512], in_=bias_b[:, 0:256])
                ve.tensor_tensor(out=f_xn2tm[:, b, :], in0=f_xn2tm[:, b, :],
                                 in1=gamma_b[:], op=ALU.mult)
                ve.tensor_tensor(out=f_xn2tm[:, b, :], in0=f_xn2tm[:, b, :],
                                 in1=beta_b[:], op=ALU.add)
                ve.tensor_copy(out=f_xncast[:, b, :], in_=f_xn2tm[:, b, :])
                ve.tensor_tensor(out=f_xnlc[:, b, :], in0=f_xn2tm[:, b, :],
                                 in1=f_xncast[:, b, :],
                                 op=ALU.subtract).then_inc(g["sv_xn"], 1)
                ve.tensor_scalar(out=acc[:, tc, :], in0=acc[:, tc, :],
                                 scalar1=2.0, scalar2=None, op0=ALU.mult)
                ve.wait_ge(g["st_lg"], LG0 + tc + 1)
                lg = ps[:, 2, 0:E]
                if dbg:
                    ve.tensor_copy(out=lgd[:, tc, :], in_=lg)
                ve.reduce_max(out=red[:, 2:3], in_=lg,
                              axis=AX.X).then_inc(g["sv_wd"], 1)
                ve.wait_ge(g["ss_wd"], SSWD0 + 3 * tc + 1)
                ve.tensor_copy(out=f_lnscr[:, 0:256], in_=bias_b[:, 0:256])
                ve.tensor_copy(out=f_lnscr[:, 256:512], in_=bias_b[:, 0:256])
                ve.tensor_scalar(out=r_pk[:, 1, :], in0=lg, scalar1=r_m1[:, 0:1],
                                 scalar2=None, op0=ALU.is_lt)
                ve.tensor_scalar(out=r_pk[:, 2, :], in0=lg, scalar1=BIG,
                                 scalar2=None, op0=ALU.add).then_inc(
                    g["sv_wd"], 1)
                ve.wait_ge(g["sg_wd"], SG0 + 2 * tc + 1)
                ve.wait_ge(g["ss_wd"], SSWD0 + 3 * tc + 2)
                ve.tensor_copy(out=f_lnscr[:, 0:256], in_=bias_b[:, 0:256])
                ve.tensor_copy(out=f_lnscr[:, 256:512], in_=bias_b[:, 0:256])
                ve.reduce_max(out=red[:, 3:4], in_=r_pk[:, 3, :], axis=AX.X)
                ve.reduce_sum(out=red[:, 4:5],
                              in_=r_pk[:, 0, :].rearrange(
                                  "p (a b) -> p a b", a=1),
                              axis=AX.X).then_inc(g["sv_wd"], 1)
                ve.wait_ge(g["sg_wd"], SG0 + 2 * tc + 2)
                ve.wait_ge(g["ss_wd"], SSWD0 + 3 * tc + 3)
                ve.tensor_copy(out=f_lnscr[:, 0:256], in_=bias_b[:, 0:256])
                ve.tensor_copy(out=f_lnscr[:, 256:512], in_=bias_b[:, 0:256])
                ve.scalar_tensor_tensor(out=wdense[:, tc, :],
                                        in0=r_pk[:, 0, :],
                                        scalar=r_m1[:, 3:4],
                                        in1=r_pk[:, 1, :],
                                        op0=ALU.mult,
                                        op1=ALU.mult).then_inc(g["sv_wd"], 1)

        def f1_gp(gp, s=s, WD0=WD0, SSWD0=SSWD0, SG0=SG0):
            for tc in range(TC):
                gp.wait_ge(g["sv_wd"], WD0 + 4 * tc + 2)
                gp.tensor_copy(out=gp_j[:], in_=bias_b[:, 0:128])
                gp.tensor_copy(out=gp_j[:], in_=bias_b[:, 128:256])
                gp.tensor_tensor(out=r_pk[:, 3, :], in0=r_pk[:, 1, :],
                                 in1=r_pk[:, 2, :], op=ALU.mult).then_inc(
                    g["sg_wd"], 1)
                gp.wait_ge(g["ss_wd"], SSWD0 + 3 * tc + 3)
                gp.tensor_copy(out=gp_j[:], in_=bias_b[:, 0:128])
                gp.tensor_copy(out=gp_j[:], in_=bias_b[:, 128:256])
                gp.tensor_scalar(out=r_pk[:, 1, :], in0=r_pk[:, 2, :],
                                 scalar1=r_m1[:, 2:3], scalar2=None,
                                 op0=ALU.is_ge).then_inc(g["sg_wd"], 1)

        def f1_sync(sy, s=s, XN0=XN0, LG0=LG0):
            for tc in range(TC):
                b = tc % 2
                sy.wait_ge(g["sv_xn"], XN0 + tc + 1)
                if tc >= 2:
                    sy.wait_ge(g["st_lg"], LG0 + tc - 1)
                for kc in range(8):
                    sy.dma_start_transpose(
                        f_xn2fm[:, kc, tc * 128:(tc + 1) * 128],
                        f_xncast[:, b, kc * 128:(kc + 1) * 128]).then_inc(
                        g["sd_xnt"], 16)
                for kc in range(8):
                    sy.dma_start_transpose(
                        f_xnlo[:, b, kc, :],
                        f_xnlc[:, b, kc * 128:(kc + 1) * 128]).then_inc(
                        g["sd_xnt"], 16)

        def f1_te(te, s=s, XN0=XN0, XNT0=XNT0, WD0=WD0, R11=R11, LNS1=LNS1):
            te.wait_ge(g["sv_r1"], R11)
            te.wait_ge(g["st_rtp"], 14 + s + 1)
            for tc in range(TC):
                b = tc % 2
                te.wait_ge(g["sd_xnt"], XNT0 + 16 * 16 * (tc + 1))
                if tc >= 1:
                    te.wait_ge(g["sv_wd"], WD0 + 4 * tc)
                nmm = 0
                for dc in range(8):
                    for which in range(3):
                        lhs = f_xn2fm[:, dc, tc * 128:(tc + 1) * 128] \
                            if which != 1 else f_xnlo[:, b, dc, :]
                        rhs = rwhl[:, 0, dc, :] if which != 2 \
                            else rwhl[:, 1, dc, :]
                        nmm += 1
                        mmi = te.matmul(ps[:, 2, 0:E], lhsT=lhs, rhs=rhs,
                                        start=(nmm == 1), stop=(nmm == 24))
                mmi.then_inc(g["st_lg"], 1)

        def f1_sc(sc, s=s, WD0=WD0, LNV1=LNV1, SSWD0=SSWD0, XN0=XN0):
            for tc in range(TC):
                if tc >= 2:
                    sc.wait_ge(g["sv_xn"], XN0 + tc - 1)
                ln_sc_apply(sc, LNV1 + tc + 1, acc[:, tc, :],
                            f_xn2tm[:, tc % 2, :]).then_inc(g["ss_ln"], 1)
                sc.wait_ge(g["sv_wd"], WD0 + 4 * tc + 1)
                sguard(sc)
                sc.activation(out=r_m1[:, 0:1], in_=red[:, 2:3], func=AF.Copy)
                sc.activation(out=r_m1[:, 1:2], in_=red[:, 2:3], func=AF.Copy,
                              scale=-1.0 / TEMP).then_inc(g["ss_wd"], 1)
                sc.wait_ge(g["sv_wd"], WD0 + 4 * tc + 2)
                scg(sc)
                sc.activation(out=r_pk[:, 0, :], in_=ps[:, 2, 0:E], func=AF.Exp,
                              bias=r_m1[:, 1:2],
                              scale=1.0 / TEMP).then_inc(g["ss_wd"], 1)
                sc.wait_ge(g["sv_wd"], WD0 + 4 * tc + 3)
                sguard(sc)
                sc.activation(out=r_m1[:, 2:3], in_=red[:, 3:4], func=AF.Copy)
                sc.activation(out=rt_s[:, 0:1], in_=red[:, 4:5], func=AF.Ln)
                sc.activation(out=r_m1[:, 3:4], in_=rt_s[:, 0:1], func=AF.Exp,
                              scale=-1.0).then_inc(g["ss_wd"], 1)

        blk.vector(f1_vec)
        blk.sync(f1_sync)
        blk.tensor(f1_te)
        blk.scalar(f1_sc)
        blk.gpsimd(f1_gp)

        add("sv_xn", TC)
        add("sd_xnt", 16 * 16 * TC)
        add("st_lg", TC)
        add("st_rtp", 1)
        add("sv_wd", 4 * TC)
        add("ss_wd", 3 * TC)
        add("sg_wd", 2 * TC)
        add("sv_ln", TC)
        add("ss_ln", TC)

        if dbg == 2:
            def dbg_vec(ve):
                ve.wait_ge(g["sd_xnt"], cur("sd_xnt"))
                ve.tensor_copy(out=xnf32[:], in_=f_xn2fm[:]).then_inc(g["sv_out"], 1)
            def dbg_sync(sy):
                sy.wait_ge(g["sv_wd"], cur("sv_wd"))
                sy.wait_ge(g["sv_out"], 1)
                sy.dma_start(out=d_r1, in_=acc[:]).then_inc(g["sd_out"], 16)
                sy.dma_start(out=d_wd, in_=wdense[:]).then_inc(g["sd_out"], 16)
                sy.dma_start(out=d_xn, in_=xnf32[:]).then_inc(g["sd_out"], 16)
                sy.dma_start(out=d_lg, in_=lgd[:]).then_inc(g["sd_out"], 16)
                sy.wait_ge(g["sd_out"], cur("sd_out") + 16 * 4)
            xnf32 = at("xnf32", [128, 8, S], F32, 32768)
            blk.vector(dbg_vec)
            blk.sync(dbg_sync)
            es.close()
            return nc

        W10 = cur("sd_w1"); W20 = cur("sd_w2"); H0 = cur("st_h")
        SH0 = cur("ss_h"); Y0 = cur("st_y"); ACC0 = cur("sv_acc")
        WD1 = cur("sv_wd"); XNT1 = cur("sd_xnt")

        def f2_sync(sy, s=s, W10=W10, W20=W20, H0=H0, Y0=Y0, WD1=WD1,
                    XNT1=XNT1):
            sy.wait_ge(g["sv_wd"], WD1)
            sy.wait_ge(g["sd_xnt"], XNT1)
            nblk = 0
            nfc = 0
            for e in range(E):
                for ts in range(TS):
                    for fb in range(FBLK):
                        if nblk >= 2:
                            sy.wait_ge(g["st_h"], H0 + 4 * nblk - 4)
                        sy.dma_start(out=f_w1[:, nblk % 2, :, :],
                                     in_=w1s[e, fb]).then_inc(g["sd_w1"], 16)
                        nblk += 1
                    for fc in range(FC):
                        if nfc >= 4:
                            sy.wait_ge(g["st_y"], Y0 + nfc - 3)
                        sy.dma_start(out=f_w2[:, nfc % 4, :],
                                     in_=w2s[e, fc]).then_inc(g["sd_w2"], 16)
                        nfc += 1

        def f2_te(te, s=s, W10=W10, W20=W20, H0=H0, SH0=SH0, Y0=Y0,
                  ACC0=ACC0, WD1=WD1):
            te.wait_ge(g["sv_wd"], WD1)
            nblk = 0
            nfc = 0
            nsl = 0
            for e in range(E):
                for ts in range(TS):
                    for fb in range(FBLK):
                        te.wait_ge(g["sd_w1"], W10 + 16 * (nblk + 1))
                        bw = nblk % 2
                        for fc4 in range(4):
                            fc = fb * 4 + fc4
                            hb = fc % 2
                            nh = nsl * FC + fc
                            if nh >= 2:
                                te.wait_ge(g["ss_h"], SH0 + (nh - 2) // 2 + 1)
                            if fc < 2 and nsl >= 1:
                                te.wait_ge(g["sv_acc"], ACC0 + 4 * (nsl - 1) + 1)
                            for dc in range(8):
                                mmi = te.matmul(
                                    ps[:, hb, :],
                                    lhsT=f_w1[:, bw, dc,
                                              fc4 * 128:(fc4 + 1) * 128],
                                    rhs=f_xn2fm[:, dc, ts * 512:(ts + 1) * 512],
                                    start=(dc == 0), stop=(dc == 7))
                            mmi.then_inc(g["st_h"], 1)
                        nblk += 1
                    te.wait_ge(g["ss_h"], SH0 + 16 * (nsl + 1))
                    if nsl >= 1:
                        te.wait_ge(g["sv_acc"], ACC0 + 4 * nsl)
                    for fc in range(FC):
                        te.wait_ge(g["sd_w2"], W20 + 16 * (nfc + 1))
                        b4 = nfc % 4
                        for tc4 in range(4):
                            for dh in range(2):
                                mmi = te.matmul(
                                    ps[:, 2 * tc4 + dh, :],
                                    lhsT=f_h[:, fc, tc4 * 128:(tc4 + 1) * 128],
                                    rhs=f_w2[:, b4, dh * 512:(dh + 1) * 512],
                                    start=(fc == 0), stop=(fc == FC - 1))
                        mmi.then_inc(g["st_y"], 1)
                        nfc += 1
                    nsl += 1

        def f2_sc(sc, s=s, H0=H0, SH0=SH0):
            nsl = 0
            for e in range(E):
                for ts in range(TS):
                    for fcp in range(FC // 2):
                        sc.wait_ge(g["st_h"], H0 + nsl * FC + 2 * fcp + 2)
                        sc.activation(out=f_h[:, 2 * fcp:2 * fcp + 2, :],
                                      in_=ps[:, 0:2, :],
                                      func=AF.Silu).then_inc(g["ss_h"], 1)
                    nsl += 1

        def f2_vec(ve, s=s, Y0=Y0, ACC0=ACC0):
            nsl = 0
            for e in range(E):
                for ts in range(TS):
                    ve.wait_ge(g["st_y"], Y0 + FC * (nsl + 1))
                    for tc4 in range(4):
                        tc = ts * 4 + tc4
                        ve.scalar_tensor_tensor(
                            out=acc[:, tc, :],
                            in0=ps[:, 2 * tc4:2 * tc4 + 2, :].rearrange(
                                "p a b -> p (a b)"),
                            scalar=wdense[:, tc, e:e + 1], in1=acc[:, tc, :],
                            op0=ALU.mult, op1=ALU.add).then_inc(g["sv_acc"], 1)
                    nsl += 1

        blk.sync(f2_sync)
        blk.tensor(f2_te)
        blk.scalar(f2_sc)
        blk.vector(f2_vec)

        add("sd_w1", 16 * E * TS * FBLK)
        add("sd_w2", 16 * E * TS * FC)
        add("st_h", E * TS * FC)
        add("ss_h", E * TS * FC // 2)
        add("st_y", E * TS * FC)
        add("sv_acc", E * TS * 4)

        if dbg == 3:
            def dbg_sync(sy):
                sy.wait_ge(g["sv_acc"], cur("sv_acc"))
                sy.dma_start(out=d_acc2, in_=acc[:]).then_inc(g["sd_out"], 16)
                sy.wait_ge(g["sd_out"], cur("sd_out") + 16)
            blk.sync(dbg_sync)
            es.close()
            return nc

        OUT0 = cur("sv_out"); DOUT0 = cur("sd_out")
        LNV2 = cur("sv_ln"); LNS2 = cur("ss_ln"); ACC1 = cur("sv_acc")

        def f3_vec(ve, s=s, OUT0=OUT0, DOUT0=DOUT0, LNS2=LNS2, ACC1=ACC1):
            ve.wait_ge(g["sv_acc"], ACC1)
            for tc in range(TC):
                ln_stats(ve, acc[:, tc, :], f_lnscr)

        def f3_sc(sc, s=s, LNV2=LNV2, DOUT0=DOUT0):
            for tc in range(TC):
                if tc >= 2:
                    sc.wait_ge(g["sd_out"], DOUT0 + 16 * (tc - 1))
                ln_sc_apply(sc, LNV2 + tc + 1, acc[:, tc, :],
                            f_out[:, tc % 2, :]).then_inc(g["sv_out"], 1)

        def f3_sync(sy, s=s, OUT0=OUT0):
            for tc in range(TC):
                sy.wait_ge(g["sv_out"], OUT0 + tc + 1)
                sy.dma_start(out=out[s, tc * 128:(tc + 1) * 128, :],
                             in_=f_out[:, tc % 2, :]).then_inc(g["sd_out"], 16)

        blk.vector(f3_vec)
        blk.scalar(f3_sc)
        blk.sync(f3_sync)

        add("sv_out", TC)
        add("sd_out", 16 * TC)
        add("sv_ln", TC)

    def fin_sync(sy):
        sy.wait_ge(g["sd_out"], cur("sd_out"))

    blk.sync(fin_sync)
    es.close()
    return nc


_NC_CACHE = []


def kernel(L0_attn_x, L0_attn_qkv_weight, L0_attn_qkv_weight_mask,
           L0_attn_out_weight, L0_attn_out_weight_mask, L0_attn_out_bias,
           L0_attn_out_bias_mask, L0_ffn_gamma, L0_ffn_beta, L0_ffn_router_w,
           L0_ffn_W1, L0_ffn_W2):
    x = np.ascontiguousarray(np.asarray(L0_attn_x, dtype=np.float32))
    com = {
        "wqkv": np.ascontiguousarray(np.asarray(L0_attn_qkv_weight, np.float32)),
        "mqkv": np.ascontiguousarray(np.asarray(L0_attn_qkv_weight_mask, np.float32)),
        "wo": np.ascontiguousarray(np.asarray(L0_attn_out_weight, np.float32)),
        "mo": np.ascontiguousarray(np.asarray(L0_attn_out_weight_mask, np.float32)),
        "bo": np.ascontiguousarray(np.asarray(L0_attn_out_bias, np.float32).reshape(1, D)),
        "mbo": np.ascontiguousarray(np.asarray(L0_attn_out_bias_mask, np.float32).reshape(1, D)),
        "gamma": np.ascontiguousarray(np.asarray(L0_ffn_gamma, np.float32).reshape(1, D)),
        "beta": np.ascontiguousarray(np.asarray(L0_ffn_beta, np.float32).reshape(1, D)),
        "rw": np.ascontiguousarray(np.asarray(L0_ffn_router_w, np.float32)),
        "w1": np.ascontiguousarray(np.asarray(L0_ffn_W1, np.float32)),
        "w2": np.ascontiguousarray(np.asarray(L0_ffn_W2, np.float32)),
    }
    if not _NC_CACHE:
        _NC_CACHE.append(build_core())
    nc = _NC_CACHE[0]
    in_maps = [dict(com, x=np.ascontiguousarray(x[2 * c:2 * c + 2]))
               for c in range(8)]
    res = run_bass_kernel_spmd(nc, in_maps, core_ids=list(range(8)))
    return np.concatenate([np.asarray(r["out"]) for r in res.results], axis=0)


# version 06
# speedup vs baseline: 1.0755x; 1.0044x over previous
import contextlib
import numpy as np

import concourse.bass as bass
import concourse.mybir as mybir
from concourse.bass_utils import run_bass_kernel_spmd

F32 = mybir.dt.float32
BF16 = mybir.dt.bfloat16
AF = mybir.ActivationFunctionType
ALU = mybir.AluOpType
AX = mybir.AxisListType

S = 2048
D = 1024
E = 8
TEMP = 1.5
EPS = 1e-5
SCALE = 1.0 / 32.0
BIG = 30000.0

TC = 16
QS = 4
JC = 16
NG = 4
NPAIR = 2
TS = 4
FBLK = 8
FC = 32
NCV = 128 + 256


def build_core(dbg=0):
    nc = bass.Bass("TRN2", target_bir_lowering=False)

    x_in = nc.dram_tensor("x", [2, S, D], F32, kind="ExternalInput").ap()
    wqkv = nc.dram_tensor("wqkv", [3 * D, D], F32, kind="ExternalInput").ap()
    mqkv = nc.dram_tensor("mqkv", [3 * D, D], F32, kind="ExternalInput").ap()
    wo = nc.dram_tensor("wo", [D, D], F32, kind="ExternalInput").ap()
    mo = nc.dram_tensor("mo", [D, D], F32, kind="ExternalInput").ap()
    bo = nc.dram_tensor("bo", [1, D], F32, kind="ExternalInput").ap()
    mbo = nc.dram_tensor("mbo", [1, D], F32, kind="ExternalInput").ap()
    gamma = nc.dram_tensor("gamma", [1, D], F32, kind="ExternalInput").ap()
    beta = nc.dram_tensor("beta", [1, D], F32, kind="ExternalInput").ap()
    rw = nc.dram_tensor("rw", [E, D], F32, kind="ExternalInput").ap()
    w1 = nc.dram_tensor("w1", [E, D, 4096], F32, kind="ExternalInput").ap()
    w2 = nc.dram_tensor("w2", [E, 4096, D], F32, kind="ExternalInput").ap()
    out = nc.dram_tensor("out", [2, S, D], F32, kind="ExternalOutput").ap()
    if dbg:
        d_r1 = nc.dram_tensor("d_r1", [128, TC, D], F32, kind="ExternalOutput").ap()
        d_attn = nc.dram_tensor("d_attn", [128, 8, S], F32, kind="ExternalOutput").ap()
        d_q = nc.dram_tensor("d_q", [128, 2, S], F32, kind="ExternalOutput").ap()
        d_k = nc.dram_tensor("d_k", [128, 2, S], F32, kind="ExternalOutput").ap()
        d_v = nc.dram_tensor("d_v", [128, 16, 4, 64], F32, kind="ExternalOutput").ap()
        d_mh = nc.dram_tensor("d_mh", [128, 2], F32, kind="ExternalOutput").ap()
        d_xn = nc.dram_tensor("d_xn", [128, 8, S], F32, kind="ExternalOutput").ap()
        d_wd = nc.dram_tensor("d_wd", [128, TC, E], F32, kind="ExternalOutput").ap()
        d_acc2 = nc.dram_tensor("d_acc2", [128, TC, D], F32, kind="ExternalOutput").ap()
        d_bb = nc.dram_tensor("d_bb", [128, 3, D], F32, kind="ExternalOutput").ap()
        d_ln = nc.dram_tensor("d_ln", [128, 8], F32, kind="ExternalOutput").ap()
        d_rt = nc.dram_tensor("d_rt", [128, 2, D], F32, kind="ExternalOutput").ap()
        d_nq = nc.dram_tensor("d_nq", [1, 32], F32, kind="ExternalOutput").ap()
        d_lg = nc.dram_tensor("d_lg", [128, 16, 8], F32, kind="ExternalOutput").ap()

    wqkvT_s = nc.dram_tensor("wqkvT_s", [128, 8, 3 * D], BF16, kind="Internal").ap()
    woT_s = nc.dram_tensor("woT_s", [128, 8, D], BF16, kind="Internal").ap()
    xfm_s = nc.dram_tensor("xfm_s", [128, 8, S], BF16, kind="Internal").ap()
    w1s = nc.dram_tensor("w1s", [E, FBLK, 128, 8, 512], BF16, kind="Internal").ap()
    w2s = nc.dram_tensor("w2s", [E, FC, 128, D], BF16, kind="Internal").ap()

    def nbytes(shape, dtype):
        nb = 1
        for x in shape[1:]:
            nb *= x
        return nb * (4 if dtype == F32 else 2)

    cursor = [26624]

    def al(name, shape, dtype, align=64):
        cursor[0] = -(-cursor[0] // align) * align
        t = nc.alloc_sbuf_tensor_at(name, list(shape), dtype, offset=cursor[0])
        cursor[0] += nbytes(shape, dtype)
        return t

    acc = al("acc", [128, TC, D], F32)
    bias_b = al("bias_b", [128, D], F32)
    gamma_b = al("gamma_b", [128, D], F32)
    beta_b = al("beta_b", [128, D], F32)
    router_T = al("router_T", [128, 8, E], F32)
    ident_f = al("ident_f", [128, 128], F32)
    ones_b = al("ones_b", [128, 128], BF16)
    ones_f = al("ones_f", [33, 128], F32)
    normones = al("normones", [128, 2], BF16)
    normf = al("normf", [128, 2, 128], BF16)
    nqb128 = al("nqb128", [128, 16], F32)
    mt = al("mt", [128, 2], F32)
    epsb = al("epsb", [128, 1], F32)
    scjunk = al("scjunk", [128, 192], F32)
    cl2 = al("cl2", [128, 2], F32)
    rt_s = al("rt_s", [128, 2], F32)
    wdense = al("wdense", [128, TC, E], F32)
    mhatb = al("mhatb", [128, 2], F32)

    rsi = al("rsi", [33, 512], F32)
    bc_sb = al("bc_sb", [128, 512], F32)
    nq = al("nq", [1, 32], F32)
    sm1 = al("sm1", [1, 8], F32)
    smb = al("smb", [1, 8], BF16)
    red = al("red", [128, 24], F32)
    lnm = al("lnm", [128, 2], F32)
    nqb = al("nqb", [1, 16], F32)
    mv = al("mv", [128, 2], F32)
    lnt = al("lnt", [128, 4], F32)
    r_m1 = al("r_m1", [128, 4], F32)
    r_pk = al("r_pk", [128, 4, E], F32)
    assert cursor[0] <= 117248, cursor[0]

    PH = 117248
    TOP = 229344

    def at(name, shape, dtype, off):
        assert PH + off + nbytes(shape, dtype) <= TOP, name
        return nc.alloc_sbuf_tensor_at(name, list(shape), dtype, offset=PH + off)

    a_xslab = at("a_xslab", [128, 2, 8, 512], BF16, 0)
    a_wg = at("a_wg", [128, 2, 8, 768], BF16, 16384)
    a_q = at("a_q", [128, 2, S], BF16, 40960)
    a_k = at("a_k", [128, 2, S], BF16, 49152)
    a_v = at("a_v", [128, 16, 4, 64], BF16, 57344)
    a_attn = at("a_attn", [128, 8, S], BF16, 65664)
    a_xstage = at("a_xstage", [128, 2, D], F32, 98432)
    a_xcast = at("a_xcast", [128, 2, D], BF16, 106624)
    sqbuf = at("sqbuf", [128, 2, 2048], BF16, 98432)
    P_T = at("P_T", [128, 4, 2, 512], BF16, 98432)
    a_fmtile = at("a_fmtile", [128, 2, 8, 128], BF16, 16384)
    woT = at("woT", [128, 8, D], BF16, 40960)
    r_tmp = at("r_tmp", [128, 2, D], F32, 0)
    o_lnscr = at("o_lnscr", [128, D], F32, 8192)
    f_xn2fm = at("f_xn2fm", [128, 8, S], BF16, 0)
    f_h = at("f_h", [128, FC, 512], BF16, 32768)
    f_w1 = at("f_w1", [128, 2, 8, 512], BF16, 65664)
    f_w2 = at("f_w2", [128, 4, D], BF16, 82048)
    f_xn2tm = at("f_xn2tm", [128, 2, D], F32, 90240)
    f_out = at("f_out", [128, 2, D], F32, 90240)
    f_xnf = at("f_xnf", [128, 2, 128], F32, 107520)
    gp_j = at("gp_j", [128, 128], F32, 32768)
    lgd = at("lgd", [128, 16, 8], F32, 33280)
    f_xnlc = at("f_xnlc", [128, 2, D], BF16, 33792)
    f_xnlo = at("f_xnlo", [128, 2, 8, 128], BF16, 37888)
    rwhl = at("rwhl", [128, 2, 8, E], BF16, 42496)
    f_lnscr = at("f_lnscr", [128, D], F32, 98432)
    f_xncast = at("f_xncast", [128, 2, D], BF16, 103424)
    p_w = at("p_w", [128, 2, D], F32, 40960)
    p_m = at("p_m", [128, 2, D], F32, 49152)
    p_gt = at("p_gt", [128, 2, D], BF16, 57344)
    p_gated = at("p_gated", [128, 2, D], BF16, 61440)
    p_t = at("p_t", [128, 2, 8, 128], BF16, 65664)
    c_in = at("c_in", [128, 2, 2048], F32, 69760)
    c_out = at("c_out", [128, 2, 2048], BF16, 86144)
    p_vec = at("p_vec", [8, 6 * 1024], F32, 0)

    ps = nc.alloc_psum_tensor("ps", [128, 8, 512], F32)

    es = contextlib.ExitStack()
    SEMS = {}

    def sem(n):
        SEMS[n] = es.enter_context(nc.semaphore(n))
        return SEMS[n]

    for n in ("sd_misc sd_pw sd_pt sd_cvin sd_cvout sd_x sd_xt sd_xs sd_slab "
              "sd_wg sd_wo sd_res sd_xnt sd_w1 sd_w2 sd_out sv_init sv_gate "
              "sv_cv sv_pb sv_xc st_proj sv_qkv st_s ss_exp st_pv sv_rec "
              "st_bc sv_attn st_nq sv_nq sv_nb sv_sq st_op sv_r1 st_tp sv_xnf "
              "st_lg sv_wd st_h ss_h st_y sv_acc sv_out sv_xn st_rtp sv_ln "
              "ss_ln ss_wd ss_nb sg_wd sv_w2c").split():
        sem(n)
    g = SEMS

    CNT = {}

    def add(name, n):
        CNT[name] = CNT.get(name, 0) + n

    def cur(name):
        return CNT.get(name, 0)

    blk = es.enter_context(nc.Block())

    def p0_sync(sy):
        sy.dma_start(out=p_vec[0:1, 0:D], in_=bo).then_inc(g["sd_misc"], 16)
        sy.dma_start(out=p_vec[0:1, D:2 * D], in_=mbo).then_inc(g["sd_misc"], 16)
        sy.dma_start(out=p_vec[0:1, 2 * D:3 * D], in_=gamma).then_inc(g["sd_misc"], 16)
        sy.dma_start(out=p_vec[0:1, 3 * D:4 * D], in_=beta).then_inc(g["sd_misc"], 16)
        sy.dma_start(out=p_vec[0:8, 4 * D:5 * D], in_=rw).then_inc(g["sd_misc"], 16)
        for oc in range(32):
            b = oc % 2
            if oc >= 2:
                sy.wait_ge(g["sd_pt"], 16 * 9 * (oc - 1))
            wsrc = wqkv[oc * 128:(oc + 1) * 128, :] if oc < 24 else \
                wo[(oc - 24) * 128:(oc - 23) * 128, :]
            msrc = mqkv[oc * 128:(oc + 1) * 128, :] if oc < 24 else \
                mo[(oc - 24) * 128:(oc - 23) * 128, :]
            sy.dma_start(out=p_w[:, b, :], in_=wsrc).then_inc(g["sd_pw"], 16)
            sy.dma_start(out=p_m[:, b, :], in_=msrc).then_inc(g["sd_pw"], 16)
            sy.wait_ge(g["sv_gate"], oc + 1)
            for kc in range(8):
                sy.dma_start_transpose(
                    p_t[:, b, kc, :], p_gated[:, b, kc * 128:(kc + 1) * 128]
                ).then_inc(g["sd_pt"], 16)
            sy.wait_ge(g["sd_pt"], 16 * (9 * oc + 8))
            dst = wqkvT_s[:, :, oc * 128:(oc + 1) * 128] if oc < 24 else \
                woT_s[:, :, (oc - 24) * 128:(oc - 23) * 128]
            sy.dma_start(out=dst, in_=p_t[:, b, :, :]).then_inc(g["sd_pt"], 16)
        u2 = 0
        for e in range(E):
            for fc in range(FC):
                b2 = u2 % 2
                if u2 >= 2:
                    sy.wait_ge(g["sv_w2c"], u2 - 1)
                sy.dma_start(out=c_in[:, 1, b2 * 1024:(b2 + 1) * 1024],
                             in_=w2[e, fc * 128:(fc + 1) * 128, :]).then_inc(
                    g["sd_w1"], 16)
                sy.wait_ge(g["sv_w2c"], u2 + 1)
                sy.dma_start(out=w2s[e, fc],
                             in_=c_out[:, 1, b2 * 1024:(b2 + 1) * 1024]
                             ).then_inc(g["sd_w2"], 16)
                u2 += 1

    def p0_gp(gp):
        gp.memset(ident_f[:], 0.0)
        gp.affine_select(out=ident_f[:], in_=ident_f[:],
                         compare_op=ALU.not_equal, fill=1.0, base=0,
                         pattern=[[-1, 128]], channel_multiplier=1)
        gp.memset(ones_b[:], 1.0)
        gp.memset(ones_f[:], 1.0)
        gp.memset(normones[:], 0.0)
        gp.memset(normones[0:64, 0:1], 1.0)
        gp.memset(normones[64:128, 1:2], 1.0)
        gp.memset(epsb[:], EPS)
        gp.memset(mhatb[:], 0.0)
        gp.memset(red[:], 1.0)
        gp.memset(nqb128[:], 1.0)
        gp.memset(mt[:], 1.0)
        gp.memset(lnm[:], 0.0)
        gp.memset(lnt[:], 1.0)
        gp.memset(r_m1[:], 0.0)
        gp.memset(r_pk[:], 0.0)
        gp.memset(bc_sb[:], 0.0)
        gp.memset(cl2[:, 0:1], 60.0)
        gp.memset(cl2[:, 1:2], -60.0)
        gp.memset(normf[:], 0.0)
        gp.memset(normf[0:64, 0, :], 1.0)
        gp.memset(normf[64:128, 1, :], 1.0)
        gp.engine_nop().then_inc(g["sv_init"], 1)
        u = 0
        for e in range(E):
            for dc in range(8):
                for hf in range(2):
                    if u >= 1:
                        gp.wait_ge(g["sd_cvout"], 16 * u)
                    gp.dma_start(
                        out=c_in[:, 0, :],
                        in_=w1[e, dc * 128:(dc + 1) * 128,
                               hf * 2048:(hf + 1) * 2048]).then_inc(
                        g["sd_cvin"], 16)
                    gp.wait_ge(g["sv_cv"], u + 1)
                    gp.dma_start(
                        out=w1s[e, 4 * hf:4 * hf + 4, :, dc, :].rearrange(
                            "b p f -> p b f"),
                        in_=c_out[:, 0, :].rearrange("p (b f) -> p b f", b=4)
                    ).then_inc(g["sd_cvout"], 16)
                    u += 1

    def p0_vec(ve):
        for oc in range(32):
            b = oc % 2
            if oc >= 2:
                ve.wait_ge(g["sd_pt"], 16 * (9 * (oc - 2) + 8))
            ve.wait_ge(g["sd_pw"], 16 * 2 * (oc + 1))
            ve.tensor_scalar(out=p_gt[:, b, :], in0=p_m[:, b, :], scalar1=0.0,
                             scalar2=None, op0=ALU.is_gt)
            ve.tensor_tensor(out=p_gated[:, b, :], in0=p_w[:, b, :],
                             in1=p_gt[:, b, :], op=ALU.mult).then_inc(
                g["sv_gate"], 1)
        u2 = 0
        for u in range(128):
            ve.wait_ge(g["sd_cvin"], 16 * (u + 1))
            if u >= 1:
                ve.wait_ge(g["sd_cvout"], 16 * u)
            ve.tensor_copy(out=c_out[:, 0, :],
                           in_=c_in[:, 0, :]).then_inc(g["sv_cv"], 1)
            for _ in range(2):
                if u2 < 256:
                    b2 = u2 % 2
                    ve.wait_ge(g["sd_w1"], 16 * (u2 + 1))
                    if u2 >= 2:
                        ve.wait_ge(g["sd_w2"], 16 * (u2 - 1))
                    ve.tensor_copy(
                        out=c_out[:, 1, b2 * 1024:(b2 + 1) * 1024],
                        in_=c_in[:, 1, b2 * 1024:(b2 + 1) * 1024]).then_inc(
                        g["sv_w2c"], 1)
                    u2 += 1
        ve.wait_ge(g["sd_misc"], 16 * 5)
        ve.tensor_scalar(out=p_vec[0:1, 5 * D:6 * D], in0=p_vec[0:1, D:2 * D],
                         scalar1=0.0, scalar2=None, op0=ALU.is_gt)
        ve.tensor_tensor(out=p_vec[0:1, 0:D], in0=p_vec[0:1, 0:D],
                         in1=p_vec[0:1, 5 * D:6 * D], op=ALU.mult).then_inc(
            g["sv_pb"], 1)
        for i in range(6):
            ve.wait_ge(g["st_rtp"], i + 1)
            dstt = (bias_b, bias_b, gamma_b, gamma_b, beta_b, beta_b)[i]
            ve.tensor_copy(out=dstt[:, (i % 2) * 512:(i % 2 + 1) * 512],
                           in_=ps[:, i % 2, :]).then_inc(g["sv_pb"], 1)
        for dc in range(8):
            ve.wait_ge(g["st_rtp"], 7 + dc)
            ve.tensor_copy(out=router_T[:, dc, :],
                           in_=ps[:, 2 + dc % 2, 0:E]).then_inc(g["sv_pb"], 1)

    def p0_te(te):
        te.wait_ge(g["sv_init"], 1)
        te.wait_ge(g["sv_pb"], 1)
        for i in range(6):
            srcoff = (0, 0, 2 * D, 2 * D, 3 * D, 3 * D)[i] + (i % 2) * 512
            if i >= 2:
                te.wait_ge(g["sv_pb"], i)
            te.matmul(ps[:, i % 2, :], lhsT=ones_f[0:1, :],
                      rhs=p_vec[0:1, srcoff:srcoff + 512],
                      start=True, stop=True).then_inc(g["st_rtp"], 1)
        for dc in range(8):
            if dc >= 2:
                te.wait_ge(g["sv_pb"], 7 + dc)
            te.transpose(ps[:, 2 + dc % 2, 0:E],
                         p_vec[0:8, 4 * D + dc * 128:4 * D + (dc + 1) * 128],
                         ident_f[0:8, 0:8]).then_inc(g["st_rtp"], 1)

    blk.sync(p0_sync)
    blk.gpsimd(p0_gp)
    blk.vector(p0_vec)
    blk.tensor(p0_te)

    add("sd_misc", 16 * 5)
    add("sd_pw", 16 * 64)
    add("sd_pt", 16 * 9 * 32)
    add("sd_cvin", 16 * 128)
    add("sd_cvout", 16 * 128)
    add("sd_w1", 16 * 256)
    add("sd_w2", 16 * 256)
    add("sv_init", 1)
    add("sv_gate", 32)
    add("sv_cv", 128)
    add("sv_pb", 15)
    add("sv_w2c", 256)
    add("st_rtp", 14)

    def ln_stats(ve, src, scr):
        ve.reduce_sum(out=red[:, 0:1],
                      in_=src.rearrange("p (a b) -> p a b", a=1), axis=AX.X)
        ve.tensor_tensor(out=scr[:], in0=src, in1=src, op=ALU.mult)
        ve.reduce_sum(out=red[:, 1:2],
                      in_=scr[:].rearrange("p (a b) -> p a b", a=1),
                      axis=AX.X).then_inc(g["sv_ln"], 1)

    def sguard(sc):
        sc.activation(out=scjunk[:], in_=bias_b[:, 0:192], func=AF.Copy)
        sc.activation(out=scjunk[:], in_=bias_b[:, 192:384], func=AF.Copy)

    def scg(sc):
        sc.activation(out=scjunk[:, 0:96], in_=bias_b[:, 0:96], func=AF.Copy)
        sc.activation(out=scjunk[:, 96:192], in_=bias_b[:, 0:96], func=AF.Copy)

    def ln_sc_apply(sc, n, src, dst):
        sc.wait_ge(g["sv_ln"], n)
        sguard(sc)
        sc.activation(out=lnm[:, 0:1], in_=red[:, 0:1], func=AF.Copy,
                      scale=1.0 / D)
        sc.activation(out=lnm[:, 1:2], in_=lnm[:, 0:1], func=AF.Square)
        sc.activation(out=lnt[:, 3:4], in_=lnm[:, 1:2], func=AF.Identity,
                      scale=-1.0, bias=epsb[:, 0:1])
        scg(sc)
        sc.activation(out=lnt[:, 0:1], in_=red[:, 1:2], func=AF.Identity,
                      scale=1.0 / D, bias=lnt[:, 3:4])
        sc.activation(out=lnt[:, 1:2], in_=lnt[:, 0:1], func=AF.Ln)
        sc.activation(out=lnt[:, 2:3], in_=lnt[:, 1:2], func=AF.Exp,
                      scale=-0.5)
        scg(sc)
        sc.mul(out=lnt[:, 3:4], in_=lnm[:, 0:1], mul=lnt[:, 2:3])
        sc.activation(out=lnt[:, 3:4], in_=lnt[:, 3:4], func=AF.Identity,
                      scale=-1.0)
        scg(sc)
        return sc.activation(out=dst, in_=src, func=AF.Identity,
                             scale=lnt[:, 2:3], bias=lnt[:, 3:4])

    for s in range(1 if dbg else 2):
        X0 = cur("sd_x"); XT0 = cur("sd_xt"); XS0 = cur("sd_xs")
        XC0 = cur("sv_xc")

        def a0_sync(sy, s=s, X0=X0, XT0=XT0, XS0=XS0, XC0=XC0):
            if s == 0:
                sy.wait_ge(g["sd_pt"], 16 * 9 * 32)
                sy.wait_ge(g["sd_cvout"], 16 * 128)
                sy.wait_ge(g["sd_w2"], 16 * 256)
                sy.wait_ge(g["sv_pb"], 15)
            else:
                sy.wait_ge(g["sv_out"], 16 * s)
                sy.wait_ge(g["sd_out"], 16 * 16 * s)
            for tc in range(TC):
                b = tc % 2
                if tc >= 2:
                    sy.wait_ge(g["sd_xs"], XS0 + 16 * (tc - 1))
                sy.dma_start(out=a_xstage[:, b, :],
                             in_=x_in[s, tc * 128:(tc + 1) * 128, :]).then_inc(
                    g["sd_x"], 16)
                sy.wait_ge(g["sv_xc"], XC0 + tc + 1)
                for kc in range(8):
                    sy.dma_start_transpose(
                        a_fmtile[:, b, kc, :],
                        a_xcast[:, b, kc * 128:(kc + 1) * 128]).then_inc(
                        g["sd_xt"], 16)
                sy.wait_ge(g["sd_xt"], XT0 + 16 * (8 * tc + 8))
                sy.dma_start(out=xfm_s[:, :, tc * 128:(tc + 1) * 128],
                             in_=a_fmtile[:, b, :, :]).then_inc(g["sd_xs"], 16)

        def a0_vec(ve, s=s, X0=X0, XT0=XT0):
            if s == 0:
                ve.wait_ge(g["sd_pt"], 16 * 9 * 32)
                ve.wait_ge(g["sv_pb"], 15)
            else:
                ve.wait_ge(g["sv_out"], 16 * s)
            for tc in range(TC):
                b = tc % 2
                if tc >= 2:
                    ve.wait_ge(g["sd_xt"], XT0 + 16 * 8 * (tc - 1))
                ve.wait_ge(g["sd_x"], X0 + 16 * (tc + 1))
                ve.tensor_copy(out=a_xcast[:, b, :],
                               in_=a_xstage[:, b, :]).then_inc(g["sv_xc"], 1)

        blk.sync(a0_sync)
        blk.vector(a0_vec)
        add("sd_x", 16 * TC)
        add("sd_xt", 16 * 8 * TC)
        add("sd_xs", 16 * TC)
        add("sv_xc", TC)

        WG0 = cur("sd_wg"); SLAB0 = cur("sd_slab"); PROJ0 = cur("st_proj")
        QKV0 = cur("sv_qkv"); SQ0 = cur("sv_sq"); NQ0 = cur("st_nq")
        NQE0 = cur("sv_nq"); NB0 = cur("sv_nb"); S0 = cur("st_s")
        EXP0 = cur("ss_exp"); PV0 = cur("st_pv"); REC0 = cur("sv_rec")
        BC0 = cur("st_bc"); ATT0 = cur("sv_attn"); SSNB0 = cur("ss_nb")

        def a_sync(sy, s=s, WG0=WG0, SLAB0=SLAB0, PROJ0=PROJ0, ATT0=ATT0):
            sy.wait_ge(g["sd_xs"], cur("sd_xs"))
            for gi in range(NG):
                bg = gi % 2
                if gi >= 2:
                    sy.wait_ge(g["st_proj"], PROJ0 + 8 * QS * (gi - 1))
                qo, ko, vo = 256 * gi, D + 256 * gi, 2 * D + 256 * gi
                for i, o0 in enumerate((qo, ko, vo)):
                    sy.dma_start(out=a_wg[:, bg, :, i * 256:(i + 1) * 256],
                                 in_=wqkvT_s[:, :, o0:o0 + 256]).then_inc(
                        g["sd_wg"], 16)
                for qs in range(QS):
                    u = gi * QS + qs
                    if u >= 2:
                        sy.wait_ge(g["st_proj"], PROJ0 + 8 * (u - 1))
                    sy.dma_start(out=a_xslab[:, u % 2, :, :],
                                 in_=xfm_s[:, :, qs * 512:(qs + 1) * 512]
                                 ).then_inc(g["sd_slab"], 16)
            sy.wait_ge(g["sv_attn"], ATT0 + 32)
            sy.dma_start(out=woT[:], in_=woT_s).then_inc(g["sd_wo"], 16)

        def a_te(te, s=s, WG0=WG0, SLAB0=SLAB0, PROJ0=PROJ0, QKV0=QKV0,
                 SQ0=SQ0, NQ0=NQ0, NQE0=NQE0, NB0=NB0, S0=S0, EXP0=EXP0,
                 PV0=PV0, REC0=REC0, BC0=BC0, ATT0=ATT0):
            nproj = 0
            nunit = 0
            for gi in range(NG):
                bg = gi % 2
                te.wait_ge(g["sd_wg"], WG0 + 16 * 3 * (gi + 1))
                for qs in range(QS):
                    u = gi * QS + qs
                    te.wait_ge(g["sd_slab"], SLAB0 + 16 * (u + 1))
                    for which in range(2):
                        for oc in range(2):
                            bank = nproj % 3
                            if nproj >= 3:
                                te.wait_ge(g["sv_qkv"], QKV0 + nproj - 2)
                            toff = which * 256 + oc * 128
                            for kc in range(8):
                                mmi = te.matmul(
                                    ps[:, bank, :],
                                    lhsT=a_wg[:, bg, kc, toff:toff + 128],
                                    rhs=a_xslab[:, u % 2, kc, :],
                                    start=(kc == 0), stop=(kc == 7))
                            mmi.then_inc(g["st_proj"], 1)
                            nproj += 1
                    for tcw in range(4):
                        bank = nproj % 3
                        if nproj >= 3:
                            te.wait_ge(g["sv_qkv"], QKV0 + nproj - 2)
                        for kc in range(8):
                            mmi = te.matmul(
                                ps[:, bank, 0:256],
                                lhsT=a_xslab[:, u % 2, kc,
                                             tcw * 128:(tcw + 1) * 128],
                                rhs=a_wg[:, bg, kc, 512:768],
                                start=(kc == 0), stop=(kc == 7))
                        mmi.then_inc(g["st_proj"], 1)
                        nproj += 1
                for pc in range(NPAIR):
                    pi = 2 * gi + pc
                    te.wait_ge(g["sv_attn"], ATT0 + 4 * pi)
                    te.wait_ge(g["sv_sq"], SQ0 + 2 * pi + 2)
                    for i in range(16):
                        qk, hp, q4 = i // 8, (i // 4) % 2, i % 4
                        if i >= 2:
                            te.wait_ge(g["sv_nq"], NQE0 + 16 * pi + i - 1)
                        te.matmul(ps[:, 4 + i % 2, :],
                                  lhsT=normf[:, hp, :],
                                  rhs=sqbuf[:, qk, q4 * 512:(q4 + 1) * 512],
                                  start=True, stop=True).then_inc(g["st_nq"], 1)
                    te.wait_ge(g["sv_nb"], NB0 + pi + 1)
                    for qs in range(QS):
                        u = nunit
                        up = u % 2
                        hh0 = 2 * pc

                        def s_mm(jc, u=u):
                            jcp = jc % 2
                            sb = 2 * jcp
                            if jc >= 2:
                                te.wait_ge(g["ss_exp"], EXP0 + 16 * u + jc - 1)
                            elif u >= 1:
                                te.wait_ge(g["ss_exp"], EXP0 + 16 * u)
                            te.matmul(ps[:, sb, :],
                                      lhsT=a_k[0:64, pc, jc * 128:(jc + 1) * 128],
                                      rhs=a_q[0:64, pc, qs * 512:(qs + 1) * 512],
                                      start=True, stop=True, tile_position=(0, 0))
                            te.matmul(ps[:, sb + 1, :],
                                      lhsT=a_k[64:128, pc,
                                               jc * 128:(jc + 1) * 128],
                                      rhs=a_q[64:128, pc,
                                              qs * 512:(qs + 1) * 512],
                                      start=True, stop=True,
                                      tile_position=(64, 0)).then_inc(g["st_s"], 1)

                        def pv_mm(jc, u=u, up=up, hh0=hh0):
                            jcp = jc % 4
                            te.wait_ge(g["ss_exp"], EXP0 + 16 * u + jc + 1)
                            if jc == 0:
                                if u >= 2:
                                    te.wait_ge(g["sv_attn"], ATT0 + u - 1)
                            te.matmul(ps[0:64, 4 + up, :],
                                      lhsT=a_v[:, jc, hh0, :],
                                      rhs=P_T[:, jcp, 0, :],
                                      start=(jc == 0), stop=(jc == JC - 1),
                                      tile_position=(0, 0))
                            te.matmul(ps[64:128, 4 + up, :],
                                      lhsT=a_v[:, jc, hh0 + 1, :],
                                      rhs=P_T[:, jcp, 1, :],
                                      start=(jc == 0), stop=(jc == JC - 1),
                                      tile_position=(0, 64))
                            te.matmul(ps[0:64, 6 + up, :],
                                      lhsT=ones_b[:, 0:64],
                                      rhs=P_T[:, jcp, 0, :],
                                      start=(jc == 0), stop=(jc == JC - 1),
                                      tile_position=(0, 0))
                            mmi = te.matmul(ps[64:128, 6 + up, :],
                                            lhsT=ones_b[:, 0:64],
                                            rhs=P_T[:, jcp, 1, :],
                                            start=(jc == 0), stop=(jc == JC - 1),
                                            tile_position=(0, 64))
                            mmi.then_inc(g["st_pv"], 1)

                        s_mm(0)
                        s_mm(1)
                        for jc in range(2, JC):
                            s_mm(jc)
                            pv_mm(jc - 2)
                        pv_mm(JC - 2)
                        pv_mm(JC - 1)
                        nunit += 1

        def a_vec(ve, s=s, PROJ0=PROJ0, QKV0=QKV0, SQ0=SQ0, NQ0=NQ0,
                  NQE0=NQE0, NB0=NB0, EXP0=EXP0, PV0=PV0, REC0=REC0,
                  BC0=BC0, ATT0=ATT0, SSNB0=SSNB0):
            nproj = 0
            nunit = 0
            for gi in range(NG):
                for qs in range(QS):
                    for which in range(2):
                        for oc in range(2):
                            ve.wait_ge(g["st_proj"], PROJ0 + nproj + 1)
                            dst = a_q if which == 0 else a_k
                            ve.tensor_copy(
                                out=dst[:, oc, qs * 512:(qs + 1) * 512],
                                in_=ps[:, nproj % 3, :]).then_inc(g["sv_qkv"], 1)
                            nproj += 1
                    for tcw in range(4):
                        ve.wait_ge(g["st_proj"], PROJ0 + nproj + 1)
                        ve.tensor_copy(
                            out=a_v[:, qs * 4 + tcw, :, :],
                            in_=ps[:, nproj % 3, 0:256].rearrange(
                                "p (h d) -> p h d", h=4)).then_inc(g["sv_qkv"], 1)
                        nproj += 1
                for pc in range(NPAIR):
                    pi = 2 * gi + pc
                    if pi >= 1:
                        ve.wait_ge(g["st_nq"], NQ0 + 16 * pi)
                        ve.wait_ge(g["ss_exp"], EXP0 + 64 * pi)
                        ve.wait_ge(g["st_pv"], PV0 + 16 * 4 * pi)
                    ve.tensor_tensor(out=sqbuf[:, 0, :], in0=a_q[:, pc, :],
                                     in1=a_q[:, pc, :], op=ALU.mult)
                    ve.tensor_tensor(out=sqbuf[:, 1, :], in0=a_k[:, pc, :],
                                     in1=a_k[:, pc, :],
                                     op=ALU.mult).then_inc(g["sv_sq"], 2)
                    for i in range(16):
                        ve.wait_ge(g["st_nq"], NQ0 + 16 * pi + i + 1)
                        ve.reduce_max(out=red[:, i:i + 1],
                                      in_=ps[:, 4 + i % 2, :],
                                      axis=AX.X).then_inc(g["sv_nq"], 1)
                    ve.wait_ge(g["ss_nb"], SSNB0 + 2 * pi + 1)
                    ve.tensor_copy(out=a_xcast[:, 0, :].bitcast(F32)
                                   [:, 0:256], in_=bias_b[:, 0:256])
                    ve.reduce_max(out=red[:, 16:17], in_=nqb128[:, 0:8],
                                  axis=AX.X)
                    ve.reduce_max(out=red[:, 17:18], in_=nqb128[:, 8:16],
                                  axis=AX.X).then_inc(g["sv_nb"], 1)
                    if dbg:
                        ve.wait_ge(g["ss_nb"], SSNB0 + 2 * pi + 2)
                        ve.tensor_copy(out=nq[0:1, 16 + pi:17 + pi],
                                       in_=mhatb[0:1, pc:pc + 1])
                    for qs in range(QS):
                        u = nunit
                        up = u % 2
                        ve.wait_ge(g["st_pv"], PV0 + 16 * u + 16)
                        ve.reciprocal(out=bc_sb[:], in_=ps[:, 6 + up, :])
                        ve.tensor_copy(out=a_xcast[:, 0, :].bitcast(F32)
                                       [:, 0:256], in_=bias_b[:, 0:256])
                        ve.tensor_copy(out=a_xcast[:, 1, :].bitcast(F32)
                                       [:, 0:256], in_=bias_b[:, 0:256])
                        ve.tensor_tensor(
                            out=a_attn[:, pi, qs * 512:(qs + 1) * 512],
                            in0=ps[:, 4 + up, :], in1=bc_sb[:],
                            op=ALU.mult).then_inc(g["sv_attn"], 1)
                        nunit += 1

        def a_sc(sc, s=s, S0=S0, EXP0=EXP0, PV0=PV0, NB0=NB0, NQE0=NQE0,
                 SSNB0=SSNB0):
            nunit = 0
            for gi in range(NG):
                for pc in range(NPAIR):
                    pi = 2 * gi + pc
                    sc.wait_ge(g["sv_nq"], NQE0 + 16 * pi + 16)
                    sguard(sc)
                    sc.activation(out=nqb128[:], in_=red[:, 0:16],
                                  func=AF.Copy).then_inc(g["ss_nb"], 1)
                    sc.wait_ge(g["sv_nb"], NB0 + pi + 1)
                    sguard(sc)
                    sc.activation(out=mt[:, 0:1], in_=red[:, 16:17],
                                  func=AF.Copy)
                    scg(sc)
                    sc.activation(out=mt[:, 1:2], in_=red[:, 17:18],
                                  func=AF.Identity, bias=mt[:, 0:1])
                    sc.mul(out=mt[:, 1:2], in_=mt[:, 1:2],
                           mul=SCALE * 1.01 * 0.5)
                    sc.activation(out=mt[:, 0:1], in_=mt[:, 1:2],
                                  func=AF.Relu)
                    sc.activation(out=mt[:, 1:2], in_=mt[:, 0:1],
                                  func=AF.Identity, scale=-1.0, bias=cl2[:, 0:1])
                    sc.activation(out=mt[:, 1:2], in_=mt[:, 1:2], func=AF.Relu)
                    sc.activation(out=mhatb[:, pc:pc + 1], in_=mt[:, 1:2],
                                  func=AF.Identity, scale=1.0,
                                  bias=cl2[:, 1:2])
                    scg(sc)
                    sc.engine_nop().then_inc(g["ss_nb"], 1) if False else \
                        sc.activation(out=scjunk[:, 0:64],
                                      in_=bias_b[:, 0:64],
                                      func=AF.Copy).then_inc(g["ss_nb"], 1)
                    for qs in range(QS):
                        u = nunit
                        for jc in range(JC):
                            jcp = jc % 2
                            sc.wait_ge(g["st_s"], S0 + 16 * u + jc + 1)
                            if jc >= 4:
                                sc.wait_ge(g["st_pv"], PV0 + 16 * u + jc - 3)
                            elif u >= 1:
                                sc.wait_ge(g["st_pv"], PV0 + 16 * u)
                            sc.activation(
                                out=P_T[:, jc % 4, :, :],
                                in_=ps[:, 2 * jcp:2 * jcp + 2, :],
                                func=AF.Exp, bias=mhatb[:, pc:pc + 1],
                                scale=SCALE).then_inc(g["ss_exp"], 1)
                        nunit += 1

        blk.sync(a_sync)
        blk.tensor(a_te)
        blk.vector(a_vec)
        blk.scalar(a_sc)

        add("sd_wg", 16 * 3 * NG)
        add("sd_slab", 16 * NG * QS)
        add("sd_wo", 16)
        add("st_proj", 8 * NG * QS)
        add("sv_qkv", 8 * NG * QS)
        add("sv_sq", 2 * NG * NPAIR)
        add("st_nq", 16 * NG * NPAIR)
        add("sv_nq", 16 * NG * NPAIR)
        add("sv_nb", NG * NPAIR)
        add("ss_nb", 2 * NG * NPAIR)
        add("st_s", 16 * 32)
        add("ss_exp", 16 * 32)
        add("st_pv", 16 * 32)
        add("sv_attn", 32)

        OP0 = cur("st_op"); R10 = cur("sv_r1"); RES0 = cur("sd_res")
        LNV0 = cur("sv_ln"); LNS0 = cur("ss_ln"); ATT1 = cur("sv_attn")

        def o_gp(gp, s=s, ATT1=ATT1, R10=R10):
            gp.wait_ge(g["sv_attn"], ATT1)
            for tc in range(TC):
                if tc >= 2:
                    gp.wait_ge(g["sv_r1"], R10 + tc - 1)
                gp.dma_start(out=a_xstage[:, tc % 2, :],
                             in_=x_in[s, tc * 128:(tc + 1) * 128, :]).then_inc(
                    g["sd_res"], 16)

        def o_te(te, s=s, R10=R10):
            te.wait_ge(g["sd_wo"], 16 * (s + 1))
            for tc in range(TC):
                bs = (tc % 2) * 4
                if tc >= 2:
                    te.wait_ge(g["sv_r1"], R10 + tc - 1)
                for p in range(8):
                    for dh in range(2):
                        te.matmul(ps[:, bs + dh, :],
                                  lhsT=a_attn[0:64, p, tc * 128:(tc + 1) * 128],
                                  rhs=woT[0:64, p, dh * 512:(dh + 1) * 512],
                                  start=(p == 0), stop=(p == 7),
                                  tile_position=(0, 0))
                        mmi = te.matmul(
                            ps[:, bs + 2 + dh, :],
                            lhsT=a_attn[64:128, p, tc * 128:(tc + 1) * 128],
                            rhs=woT[64:128, p, dh * 512:(dh + 1) * 512],
                            start=(p == 0), stop=(p == 7),
                            tile_position=(64, 0))
                mmi.then_inc(g["st_op"], 1)

        def o_vec(ve, s=s, OP0=OP0, R10=R10, RES0=RES0, LNS0=LNS0):
            for tc in range(TC):
                b = tc % 2
                bs = (tc % 2) * 4
                ve.wait_ge(g["st_op"], OP0 + tc + 1)
                ve.wait_ge(g["sd_res"], RES0 + 16 * (tc + 1))
                ve.scalar_tensor_tensor(
                    out=r_tmp[:, b, :],
                    in0=ps[:, bs:bs + 2, :].rearrange("p a b -> p (a b)"),
                    scalar=0.0, in1=bias_b[:], op0=ALU.bypass, op1=ALU.add)
                ve.scalar_tensor_tensor(
                    out=r_tmp[:, b, :],
                    in0=ps[:, bs + 2:bs + 4, :].rearrange("p a b -> p (a b)"),
                    scalar=0.0, in1=r_tmp[:, b, :], op0=ALU.bypass, op1=ALU.add)
                ve.tensor_tensor(out=r_tmp[:, b, :], in0=r_tmp[:, b, :],
                                 in1=a_xstage[:, b, :], op=ALU.add)
                if dbg == 4:
                    ve.tensor_copy(out=acc[:, tc, :], in_=r_tmp[:, b, :])
                    ve.engine_nop().then_inc(g["sv_r1"], 1)
                else:
                    ve.engine_nop().then_inc(g["sv_r1"], 1)
                    if tc >= 2:
                        ve.wait_ge(g["ss_ln"], LNS0 + tc - 1)
                    ln_stats(ve, r_tmp[:, b, :], o_lnscr)

        def o_sc(sc, s=s, LNV0=LNV0):
            if dbg == 4:
                return
            for tc in range(TC):
                ln_sc_apply(sc, LNV0 + tc + 1, r_tmp[:, tc % 2, :],
                            acc[:, tc, :]).then_inc(g["ss_ln"], 1)

        blk.gpsimd(o_gp)
        blk.tensor(o_te)
        blk.vector(o_vec)
        blk.scalar(o_sc)

        add("sd_res", 16 * TC)
        add("st_op", TC)
        add("sv_r1", TC)
        add("sv_ln", TC)
        add("ss_ln", TC)

        if dbg in (1, 4):
            def dbg_sync(sy, LNS0=LNS0):
                sy.wait_ge(g["sv_r1"], cur("sv_r1"))
                if dbg == 1:
                    sy.wait_ge(g["ss_ln"], LNS0 + TC)
                for nm, src_t in ((d_r1, acc), (d_attn, None), (d_q, None)):
                    pass
                sy.dma_start(out=d_r1, in_=acc[:]).then_inc(g["sd_out"], 16)
                sy.dma_start(out=d_mh, in_=mhatb[:]).then_inc(g["sd_out"], 16)
                sy.dma_start(out=d_bb[:, 0, :], in_=bias_b[:]).then_inc(g["sd_out"], 16)
                sy.dma_start(out=d_bb[:, 1, :], in_=gamma_b[:]).then_inc(g["sd_out"], 16)
                sy.dma_start(out=d_bb[:, 2, :], in_=beta_b[:]).then_inc(g["sd_out"], 16)
                sy.dma_start(out=d_ln[:, 0:2], in_=mv[:]).then_inc(g["sd_out"], 16)
                sy.dma_start(out=d_ln[:, 2:6], in_=lnt[:]).then_inc(g["sd_out"], 16)
                sy.dma_start(out=d_rt[:, 0, :], in_=r_tmp[:, 1, :]).then_inc(g["sd_out"], 16)
                sy.dma_start(out=d_rt[:, 1, :], in_=o_lnscr[:]).then_inc(g["sd_out"], 16)
                sy.dma_start(out=d_nq, in_=nq[0:1, :]).then_inc(g["sd_out"], 16)
                sy.wait_ge(g["sd_out"], cur("sd_out") + 16 * 10)
            def dbg_vec(ve):
                ve.wait_ge(g["sv_attn"], cur("sv_attn"))
                ve.wait_ge(g["sd_out"], cur("sd_out") + 16 * 9)
                ve.tensor_copy(out=bcf32[:], in_=a_attn[:, :, :]).then_inc(g["sv_out"], 1)
                ve.tensor_copy(out=qf32[:], in_=a_q[:]).then_inc(g["sv_out"], 1)
                ve.tensor_copy(out=vf32[:], in_=a_v[:]).then_inc(g["sv_out"], 1)
                ve.wait_ge(g["sd_out"], cur("sd_out") + 16 * 7)
                ve.tensor_copy(out=qf32[:], in_=a_k[:]).then_inc(g["sv_out"], 1)
            def dbg_sync2(sy):
                sy.wait_ge(g["sv_out"], 3)
                sy.dma_start(out=d_attn, in_=bcf32[:]).then_inc(g["sd_out"], 16)
                sy.dma_start(out=d_q, in_=qf32[:]).then_inc(g["sd_out"], 16)
                sy.dma_start(out=d_v, in_=vf32[:]).then_inc(g["sd_out"], 16)
                sy.wait_ge(g["sv_out"], 4)
                sy.dma_start(out=d_k, in_=qf32[:]).then_inc(g["sd_out"], 16)
                sy.wait_ge(g["sd_out"], cur("sd_out") + 16 * 9)
            bcf32 = at("bcf32", [128, 8, S], F32, 0)
            qf32 = at("qf32", [128, 2, S], F32, 65664)
            vf32 = at("vf32", [128, 16, 4, 64], F32, 82048)
            blk.vector(dbg_vec)
            blk.sync(dbg_sync)
            blk.sync(dbg_sync2)
            es.close()
            return nc

        XN0 = cur("sv_xn"); XNT0 = cur("sd_xnt"); TP0 = cur("st_tp")
        XNF0 = cur("sv_xnf"); LG0 = cur("st_lg"); WD0 = cur("sv_wd")
        SSWD0 = cur("ss_wd"); SG0 = cur("sg_wd")
        LNV1 = cur("sv_ln"); LNS1 = cur("ss_ln"); R11 = cur("sv_r1")

        def f1_vec(ve, s=s, XN0=XN0, XNT0=XNT0, TP0=TP0, XNF0=XNF0, LG0=LG0,
                   WD0=WD0, LNS1=LNS1, R11=R11, SSWD0=SSWD0, SG0=SG0,
                   LNSO=LNS0):
            ve.tensor_copy(out=rwhl[:, 0, :, :],
                           in_=router_T[:, :, :])
            ve.tensor_tensor(out=rwhl[:, 1, :, :], in0=router_T[:, :, :],
                             in1=rwhl[:, 0, :, :],
                             op=ALU.subtract).then_inc(g["st_rtp"], 1)
            for tc in range(TC):
                b = tc % 2
                ve.wait_ge(g["ss_ln"], LNSO + tc + 1)
                if tc >= 2:
                    ve.wait_ge(g["sd_xnt"], XNT0 + 16 * 16 * (tc - 1))
                ln_stats(ve, acc[:, tc, :], f_lnscr)
                ve.wait_ge(g["ss_ln"], LNS1 + tc + 1)
                ve.tensor_copy(out=f_lnscr[:, 0:256], in_=bias_b[:, 0:256])
                ve.tensor_copy(out=f_lnscr[:, 256:512], in_=bias_b[:, 0:256])
                ve.tensor_tensor(out=f_xn2tm[:, b, :], in0=f_xn2tm[:, b, :],
                                 in1=gamma_b[:], op=ALU.mult)
                ve.tensor_tensor(out=f_xn2tm[:, b, :], in0=f_xn2tm[:, b, :],
                                 in1=beta_b[:], op=ALU.add)
                ve.tensor_copy(out=f_xncast[:, b, :], in_=f_xn2tm[:, b, :])
                ve.tensor_tensor(out=f_xnlc[:, b, :], in0=f_xn2tm[:, b, :],
                                 in1=f_xncast[:, b, :],
                                 op=ALU.subtract).then_inc(g["sv_xn"], 1)
                ve.tensor_scalar(out=acc[:, tc, :], in0=acc[:, tc, :],
                                 scalar1=2.0, scalar2=None, op0=ALU.mult)
                ve.wait_ge(g["st_lg"], LG0 + tc + 1)
                lg = ps[:, 2, 0:E]
                if dbg:
                    ve.tensor_copy(out=lgd[:, tc, :], in_=lg)
                ve.reduce_max(out=red[:, 2:3], in_=lg,
                              axis=AX.X).then_inc(g["sv_wd"], 1)
                ve.wait_ge(g["ss_wd"], SSWD0 + 3 * tc + 1)
                ve.tensor_copy(out=f_lnscr[:, 0:256], in_=bias_b[:, 0:256])
                ve.tensor_copy(out=f_lnscr[:, 256:512], in_=bias_b[:, 0:256])
                ve.tensor_scalar(out=r_pk[:, 1, :], in0=lg, scalar1=r_m1[:, 0:1],
                                 scalar2=None, op0=ALU.is_lt)
                ve.tensor_scalar(out=r_pk[:, 2, :], in0=lg, scalar1=BIG,
                                 scalar2=None, op0=ALU.add).then_inc(
                    g["sv_wd"], 1)
                ve.wait_ge(g["sg_wd"], SG0 + 2 * tc + 1)
                ve.wait_ge(g["ss_wd"], SSWD0 + 3 * tc + 2)
                ve.tensor_copy(out=f_lnscr[:, 0:256], in_=bias_b[:, 0:256])
                ve.tensor_copy(out=f_lnscr[:, 256:512], in_=bias_b[:, 0:256])
                ve.reduce_max(out=red[:, 3:4], in_=r_pk[:, 3, :], axis=AX.X)
                ve.reduce_sum(out=red[:, 4:5],
                              in_=r_pk[:, 0, :].rearrange(
                                  "p (a b) -> p a b", a=1),
                              axis=AX.X).then_inc(g["sv_wd"], 1)
                ve.wait_ge(g["sg_wd"], SG0 + 2 * tc + 2)
                ve.wait_ge(g["ss_wd"], SSWD0 + 3 * tc + 3)
                ve.tensor_copy(out=f_lnscr[:, 0:256], in_=bias_b[:, 0:256])
                ve.tensor_copy(out=f_lnscr[:, 256:512], in_=bias_b[:, 0:256])
                ve.scalar_tensor_tensor(out=wdense[:, tc, :],
                                        in0=r_pk[:, 0, :],
                                        scalar=r_m1[:, 3:4],
                                        in1=r_pk[:, 1, :],
                                        op0=ALU.mult,
                                        op1=ALU.mult).then_inc(g["sv_wd"], 1)

        def f1_gp(gp, s=s, WD0=WD0, SSWD0=SSWD0, SG0=SG0):
            for tc in range(TC):
                gp.wait_ge(g["sv_wd"], WD0 + 4 * tc + 2)
                gp.tensor_copy(out=gp_j[:], in_=bias_b[:, 0:128])
                gp.tensor_copy(out=gp_j[:], in_=bias_b[:, 128:256])
                gp.tensor_tensor(out=r_pk[:, 3, :], in0=r_pk[:, 1, :],
                                 in1=r_pk[:, 2, :], op=ALU.mult).then_inc(
                    g["sg_wd"], 1)
                gp.wait_ge(g["ss_wd"], SSWD0 + 3 * tc + 3)
                gp.tensor_copy(out=gp_j[:], in_=bias_b[:, 0:128])
                gp.tensor_copy(out=gp_j[:], in_=bias_b[:, 128:256])
                gp.tensor_scalar(out=r_pk[:, 1, :], in0=r_pk[:, 2, :],
                                 scalar1=r_m1[:, 2:3], scalar2=None,
                                 op0=ALU.is_ge).then_inc(g["sg_wd"], 1)

        def f1_sync(sy, s=s, XN0=XN0, LG0=LG0):
            for tc in range(TC):
                b = tc % 2
                sy.wait_ge(g["sv_xn"], XN0 + tc + 1)
                if tc >= 2:
                    sy.wait_ge(g["st_lg"], LG0 + tc - 1)
                for kc in range(8):
                    sy.dma_start_transpose(
                        f_xn2fm[:, kc, tc * 128:(tc + 1) * 128],
                        f_xncast[:, b, kc * 128:(kc + 1) * 128]).then_inc(
                        g["sd_xnt"], 16)
                for kc in range(8):
                    sy.dma_start_transpose(
                        f_xnlo[:, b, kc, :],
                        f_xnlc[:, b, kc * 128:(kc + 1) * 128]).then_inc(
                        g["sd_xnt"], 16)

        def f1_te(te, s=s, XN0=XN0, XNT0=XNT0, WD0=WD0, R11=R11, LNS1=LNS1):
            te.wait_ge(g["sv_r1"], R11)
            te.wait_ge(g["st_rtp"], 14 + s + 1)
            for tc in range(TC):
                b = tc % 2
                te.wait_ge(g["sd_xnt"], XNT0 + 16 * 16 * (tc + 1))
                if tc >= 1:
                    te.wait_ge(g["sv_wd"], WD0 + 4 * tc)
                nmm = 0
                for dc in range(8):
                    for which in range(3):
                        lhs = f_xn2fm[:, dc, tc * 128:(tc + 1) * 128] \
                            if which != 1 else f_xnlo[:, b, dc, :]
                        rhs = rwhl[:, 0, dc, :] if which != 2 \
                            else rwhl[:, 1, dc, :]
                        nmm += 1
                        mmi = te.matmul(ps[:, 2, 0:E], lhsT=lhs, rhs=rhs,
                                        start=(nmm == 1), stop=(nmm == 24))
                mmi.then_inc(g["st_lg"], 1)

        def f1_sc(sc, s=s, WD0=WD0, LNV1=LNV1, SSWD0=SSWD0, XN0=XN0):
            for tc in range(TC):
                if tc >= 2:
                    sc.wait_ge(g["sv_xn"], XN0 + tc - 1)
                ln_sc_apply(sc, LNV1 + tc + 1, acc[:, tc, :],
                            f_xn2tm[:, tc % 2, :]).then_inc(g["ss_ln"], 1)
                sc.wait_ge(g["sv_wd"], WD0 + 4 * tc + 1)
                sguard(sc)
                sc.activation(out=r_m1[:, 0:1], in_=red[:, 2:3], func=AF.Copy)
                sc.activation(out=r_m1[:, 1:2], in_=red[:, 2:3], func=AF.Copy,
                              scale=-1.0 / TEMP).then_inc(g["ss_wd"], 1)
                sc.wait_ge(g["sv_wd"], WD0 + 4 * tc + 2)
                scg(sc)
                sc.activation(out=r_pk[:, 0, :], in_=ps[:, 2, 0:E], func=AF.Exp,
                              bias=r_m1[:, 1:2],
                              scale=1.0 / TEMP).then_inc(g["ss_wd"], 1)
                sc.wait_ge(g["sv_wd"], WD0 + 4 * tc + 3)
                sguard(sc)
                sc.activation(out=r_m1[:, 2:3], in_=red[:, 3:4], func=AF.Copy)
                sc.activation(out=rt_s[:, 0:1], in_=red[:, 4:5], func=AF.Ln)
                sc.activation(out=r_m1[:, 3:4], in_=rt_s[:, 0:1], func=AF.Exp,
                              scale=-1.0).then_inc(g["ss_wd"], 1)

        blk.vector(f1_vec)
        blk.sync(f1_sync)
        blk.tensor(f1_te)
        blk.scalar(f1_sc)
        blk.gpsimd(f1_gp)

        add("sv_xn", TC)
        add("sd_xnt", 16 * 16 * TC)
        add("st_lg", TC)
        add("st_rtp", 1)
        add("sv_wd", 4 * TC)
        add("ss_wd", 3 * TC)
        add("sg_wd", 2 * TC)
        add("sv_ln", TC)
        add("ss_ln", TC)

        if dbg == 2:
            def dbg_vec(ve):
                ve.wait_ge(g["sd_xnt"], cur("sd_xnt"))
                ve.tensor_copy(out=xnf32[:], in_=f_xn2fm[:]).then_inc(g["sv_out"], 1)
            def dbg_sync(sy):
                sy.wait_ge(g["sv_wd"], cur("sv_wd"))
                sy.wait_ge(g["sv_out"], 1)
                sy.dma_start(out=d_r1, in_=acc[:]).then_inc(g["sd_out"], 16)
                sy.dma_start(out=d_wd, in_=wdense[:]).then_inc(g["sd_out"], 16)
                sy.dma_start(out=d_xn, in_=xnf32[:]).then_inc(g["sd_out"], 16)
                sy.dma_start(out=d_lg, in_=lgd[:]).then_inc(g["sd_out"], 16)
                sy.wait_ge(g["sd_out"], cur("sd_out") + 16 * 4)
            xnf32 = at("xnf32", [128, 8, S], F32, 32768)
            blk.vector(dbg_vec)
            blk.sync(dbg_sync)
            es.close()
            return nc

        W10 = cur("sd_w1"); W20 = cur("sd_w2"); H0 = cur("st_h")
        SH0 = cur("ss_h"); Y0 = cur("st_y"); ACC0 = cur("sv_acc")
        WD1 = cur("sv_wd"); XNT1 = cur("sd_xnt")

        def f2_sync(sy, s=s, W10=W10, H0=H0, WD1=WD1, XNT1=XNT1):
            sy.wait_ge(g["sv_wd"], WD1)
            sy.wait_ge(g["sd_xnt"], XNT1)
            nblk = 0
            for e in range(E):
                for ts in range(TS):
                    for fb in range(FBLK):
                        if nblk >= 2:
                            sy.wait_ge(g["st_h"], H0 + 4 * nblk - 4)
                        sy.dma_start(out=f_w1[:, nblk % 2, :, :],
                                     in_=w1s[e, fb]).then_inc(g["sd_w1"], 16)
                        nblk += 1

        def f2_gp(gp, s=s, Y0=Y0, WD1=WD1, XNT1=XNT1):
            gp.wait_ge(g["sv_wd"], WD1)
            gp.wait_ge(g["sd_xnt"], XNT1)
            nfc = 0
            for e in range(E):
                for ts in range(TS):
                    for fc in range(FC):
                        if nfc >= 4:
                            gp.wait_ge(g["st_y"], Y0 + nfc - 3)
                        gp.dma_start(out=f_w2[:, nfc % 4, :],
                                     in_=w2s[e, fc]).then_inc(g["sd_w2"], 16)
                        nfc += 1

        def f2_te(te, s=s, W10=W10, W20=W20, H0=H0, SH0=SH0, Y0=Y0,
                  ACC0=ACC0, WD1=WD1):
            te.wait_ge(g["sv_wd"], WD1)
            nblk = 0
            nfc = 0
            nsl = 0
            for e in range(E):
                for ts in range(TS):
                    for fb in range(FBLK):
                        te.wait_ge(g["sd_w1"], W10 + 16 * (nblk + 1))
                        bw = nblk % 2
                        for fc4 in range(4):
                            fc = fb * 4 + fc4
                            hb = fc % 2
                            nh = nsl * FC + fc
                            if nh >= 2:
                                te.wait_ge(g["ss_h"], SH0 + (nh - 2) // 2 + 1)
                            if fc < 2 and nsl >= 1:
                                te.wait_ge(g["sv_acc"], ACC0 + 4 * (nsl - 1) + 1)
                            for dc in range(8):
                                mmi = te.matmul(
                                    ps[:, hb, :],
                                    lhsT=f_w1[:, bw, dc,
                                              fc4 * 128:(fc4 + 1) * 128],
                                    rhs=f_xn2fm[:, dc, ts * 512:(ts + 1) * 512],
                                    start=(dc == 0), stop=(dc == 7))
                            mmi.then_inc(g["st_h"], 1)
                        nblk += 1
                    te.wait_ge(g["ss_h"], SH0 + 16 * (nsl + 1))
                    if nsl >= 1:
                        te.wait_ge(g["sv_acc"], ACC0 + 4 * nsl)
                    for fc in range(FC):
                        te.wait_ge(g["sd_w2"], W20 + 16 * (nfc + 1))
                        b4 = nfc % 4
                        for tc4 in range(4):
                            for dh in range(2):
                                mmi = te.matmul(
                                    ps[:, 2 * tc4 + dh, :],
                                    lhsT=f_h[:, fc, tc4 * 128:(tc4 + 1) * 128],
                                    rhs=f_w2[:, b4, dh * 512:(dh + 1) * 512],
                                    start=(fc == 0), stop=(fc == FC - 1))
                        mmi.then_inc(g["st_y"], 1)
                        nfc += 1
                    nsl += 1

        def f2_sc(sc, s=s, H0=H0, SH0=SH0):
            nsl = 0
            for e in range(E):
                for ts in range(TS):
                    for fcp in range(FC // 2):
                        sc.wait_ge(g["st_h"], H0 + nsl * FC + 2 * fcp + 2)
                        sc.activation(out=f_h[:, 2 * fcp:2 * fcp + 2, :],
                                      in_=ps[:, 0:2, :],
                                      func=AF.Silu).then_inc(g["ss_h"], 1)
                    nsl += 1

        def f2_vec(ve, s=s, Y0=Y0, ACC0=ACC0):
            nsl = 0
            for e in range(E):
                for ts in range(TS):
                    ve.wait_ge(g["st_y"], Y0 + FC * (nsl + 1))
                    for tc4 in range(4):
                        tc = ts * 4 + tc4
                        ve.scalar_tensor_tensor(
                            out=acc[:, tc, :],
                            in0=ps[:, 2 * tc4:2 * tc4 + 2, :].rearrange(
                                "p a b -> p (a b)"),
                            scalar=wdense[:, tc, e:e + 1], in1=acc[:, tc, :],
                            op0=ALU.mult, op1=ALU.add).then_inc(g["sv_acc"], 1)
                    nsl += 1

        blk.sync(f2_sync)
        blk.gpsimd(f2_gp)
        blk.tensor(f2_te)
        blk.scalar(f2_sc)
        blk.vector(f2_vec)

        add("sd_w1", 16 * E * TS * FBLK)
        add("sd_w2", 16 * E * TS * FC)
        add("st_h", E * TS * FC)
        add("ss_h", E * TS * FC // 2)
        add("st_y", E * TS * FC)
        add("sv_acc", E * TS * 4)

        if dbg == 3:
            def dbg_sync(sy):
                sy.wait_ge(g["sv_acc"], cur("sv_acc"))
                sy.dma_start(out=d_acc2, in_=acc[:]).then_inc(g["sd_out"], 16)
                sy.wait_ge(g["sd_out"], cur("sd_out") + 16)
            blk.sync(dbg_sync)
            es.close()
            return nc

        OUT0 = cur("sv_out"); DOUT0 = cur("sd_out")
        LNV2 = cur("sv_ln"); LNS2 = cur("ss_ln"); ACC1 = cur("sv_acc")

        def f3_vec(ve, s=s, OUT0=OUT0, DOUT0=DOUT0, LNS2=LNS2, ACC1=ACC1):
            ve.wait_ge(g["sv_acc"], ACC1)
            for tc in range(TC):
                ln_stats(ve, acc[:, tc, :], f_lnscr)

        def f3_sc(sc, s=s, LNV2=LNV2, DOUT0=DOUT0):
            for tc in range(TC):
                if tc >= 2:
                    sc.wait_ge(g["sd_out"], DOUT0 + 16 * (tc - 1))
                ln_sc_apply(sc, LNV2 + tc + 1, acc[:, tc, :],
                            f_out[:, tc % 2, :]).then_inc(g["sv_out"], 1)

        def f3_sync(sy, s=s, OUT0=OUT0):
            for tc in range(TC):
                sy.wait_ge(g["sv_out"], OUT0 + tc + 1)
                sy.dma_start(out=out[s, tc * 128:(tc + 1) * 128, :],
                             in_=f_out[:, tc % 2, :]).then_inc(g["sd_out"], 16)

        blk.vector(f3_vec)
        blk.scalar(f3_sc)
        blk.sync(f3_sync)

        add("sv_out", TC)
        add("sd_out", 16 * TC)
        add("sv_ln", TC)

    def fin_sync(sy):
        sy.wait_ge(g["sd_out"], cur("sd_out"))

    blk.sync(fin_sync)
    es.close()
    return nc


_NC_CACHE = []


def kernel(L0_attn_x, L0_attn_qkv_weight, L0_attn_qkv_weight_mask,
           L0_attn_out_weight, L0_attn_out_weight_mask, L0_attn_out_bias,
           L0_attn_out_bias_mask, L0_ffn_gamma, L0_ffn_beta, L0_ffn_router_w,
           L0_ffn_W1, L0_ffn_W2):
    x = np.ascontiguousarray(np.asarray(L0_attn_x, dtype=np.float32))
    com = {
        "wqkv": np.ascontiguousarray(np.asarray(L0_attn_qkv_weight, np.float32)),
        "mqkv": np.ascontiguousarray(np.asarray(L0_attn_qkv_weight_mask, np.float32)),
        "wo": np.ascontiguousarray(np.asarray(L0_attn_out_weight, np.float32)),
        "mo": np.ascontiguousarray(np.asarray(L0_attn_out_weight_mask, np.float32)),
        "bo": np.ascontiguousarray(np.asarray(L0_attn_out_bias, np.float32).reshape(1, D)),
        "mbo": np.ascontiguousarray(np.asarray(L0_attn_out_bias_mask, np.float32).reshape(1, D)),
        "gamma": np.ascontiguousarray(np.asarray(L0_ffn_gamma, np.float32).reshape(1, D)),
        "beta": np.ascontiguousarray(np.asarray(L0_ffn_beta, np.float32).reshape(1, D)),
        "rw": np.ascontiguousarray(np.asarray(L0_ffn_router_w, np.float32)),
        "w1": np.ascontiguousarray(np.asarray(L0_ffn_W1, np.float32)),
        "w2": np.ascontiguousarray(np.asarray(L0_ffn_W2, np.float32)),
    }
    if not _NC_CACHE:
        _NC_CACHE.append(build_core())
    nc = _NC_CACHE[0]
    in_maps = [dict(com, x=np.ascontiguousarray(x[2 * c:2 * c + 2]))
               for c in range(8)]
    res = run_bass_kernel_spmd(nc, in_maps, core_ids=list(range(8)))
    return np.concatenate([np.asarray(r["out"]) for r in res.results], axis=0)


# version 07 (final)
# speedup vs baseline: 1.0762x; 1.0007x over previous
import contextlib
import numpy as np

import concourse.bass as bass
import concourse.mybir as mybir
from concourse.bass_utils import run_bass_kernel_spmd

F32 = mybir.dt.float32
BF16 = mybir.dt.bfloat16
AF = mybir.ActivationFunctionType
ALU = mybir.AluOpType
AX = mybir.AxisListType

S = 2048
D = 1024
E = 8
TEMP = 1.5
EPS = 1e-5
SCALE = 1.0 / 32.0
BIG = 30000.0

TC = 16
QS = 4
JC = 16
NG = 4
NPAIR = 2
TS = 4
FBLK = 8
FC = 32
NCV = 128 + 256


def build_core(dbg=0):
    nc = bass.Bass("TRN2", target_bir_lowering=False)

    x_in = nc.dram_tensor("x", [2, S, D], F32, kind="ExternalInput").ap()
    wqkv = nc.dram_tensor("wqkv", [3 * D, D], F32, kind="ExternalInput").ap()
    mqkv = nc.dram_tensor("mqkv", [3 * D, D], F32, kind="ExternalInput").ap()
    wo = nc.dram_tensor("wo", [D, D], F32, kind="ExternalInput").ap()
    mo = nc.dram_tensor("mo", [D, D], F32, kind="ExternalInput").ap()
    bo = nc.dram_tensor("bo", [1, D], F32, kind="ExternalInput").ap()
    mbo = nc.dram_tensor("mbo", [1, D], F32, kind="ExternalInput").ap()
    gamma = nc.dram_tensor("gamma", [1, D], F32, kind="ExternalInput").ap()
    beta = nc.dram_tensor("beta", [1, D], F32, kind="ExternalInput").ap()
    rw = nc.dram_tensor("rw", [E, D], F32, kind="ExternalInput").ap()
    w1 = nc.dram_tensor("w1", [E, D, 4096], F32, kind="ExternalInput").ap()
    w2 = nc.dram_tensor("w2", [E, 4096, D], F32, kind="ExternalInput").ap()
    out = nc.dram_tensor("out", [2, S, D], F32, kind="ExternalOutput").ap()
    if dbg:
        d_r1 = nc.dram_tensor("d_r1", [128, TC, D], F32, kind="ExternalOutput").ap()
        d_attn = nc.dram_tensor("d_attn", [128, 8, S], F32, kind="ExternalOutput").ap()
        d_q = nc.dram_tensor("d_q", [128, 2, S], F32, kind="ExternalOutput").ap()
        d_k = nc.dram_tensor("d_k", [128, 2, S], F32, kind="ExternalOutput").ap()
        d_v = nc.dram_tensor("d_v", [128, 16, 4, 64], F32, kind="ExternalOutput").ap()
        d_mh = nc.dram_tensor("d_mh", [128, 2], F32, kind="ExternalOutput").ap()
        d_xn = nc.dram_tensor("d_xn", [128, 8, S], F32, kind="ExternalOutput").ap()
        d_wd = nc.dram_tensor("d_wd", [128, TC, E], F32, kind="ExternalOutput").ap()
        d_acc2 = nc.dram_tensor("d_acc2", [128, TC, D], F32, kind="ExternalOutput").ap()
        d_bb = nc.dram_tensor("d_bb", [128, 3, D], F32, kind="ExternalOutput").ap()
        d_ln = nc.dram_tensor("d_ln", [128, 8], F32, kind="ExternalOutput").ap()
        d_rt = nc.dram_tensor("d_rt", [128, 2, D], F32, kind="ExternalOutput").ap()
        d_nq = nc.dram_tensor("d_nq", [1, 32], F32, kind="ExternalOutput").ap()
        d_lg = nc.dram_tensor("d_lg", [128, 16, 8], F32, kind="ExternalOutput").ap()

    wqkvT_s = nc.dram_tensor("wqkvT_s", [128, 8, 3 * D], BF16, kind="Internal").ap()
    woT_s = nc.dram_tensor("woT_s", [128, 8, D], BF16, kind="Internal").ap()
    xfm_s = nc.dram_tensor("xfm_s", [128, 8, S], BF16, kind="Internal").ap()
    w1s = nc.dram_tensor("w1s", [E, FBLK, 128, 8, 512], BF16, kind="Internal").ap()
    w2s = nc.dram_tensor("w2s", [E, FC, 128, D], BF16, kind="Internal").ap()

    def nbytes(shape, dtype):
        nb = 1
        for x in shape[1:]:
            nb *= x
        return nb * (4 if dtype == F32 else 2)

    cursor = [26624]

    def al(name, shape, dtype, align=64):
        cursor[0] = -(-cursor[0] // align) * align
        t = nc.alloc_sbuf_tensor_at(name, list(shape), dtype, offset=cursor[0])
        cursor[0] += nbytes(shape, dtype)
        return t

    acc = al("acc", [128, TC, D], F32)
    bias_b = al("bias_b", [128, D], F32)
    gamma_b = al("gamma_b", [128, D], F32)
    beta_b = al("beta_b", [128, D], F32)
    router_T = al("router_T", [128, 8, E], F32)
    ident_f = al("ident_f", [128, 128], F32)
    ones_b = al("ones_b", [128, 128], BF16)
    ones_f = al("ones_f", [33, 128], F32)
    normones = al("normones", [128, 2], BF16)
    normf = al("normf", [128, 2, 128], BF16)
    nqb128 = al("nqb128", [128, 16], F32)
    mt = al("mt", [128, 2], F32)
    epsb = al("epsb", [128, 1], F32)
    scjunk = al("scjunk", [128, 192], F32)
    cl2 = al("cl2", [128, 2], F32)
    rt_s = al("rt_s", [128, 2], F32)
    wdense = al("wdense", [128, TC, E], F32)
    mhatb = al("mhatb", [128, 2], F32)

    rsi = al("rsi", [33, 512], F32)
    bc_sb = al("bc_sb", [128, 512], F32)
    nq = al("nq", [1, 32], F32)
    sm1 = al("sm1", [1, 8], F32)
    smb = al("smb", [1, 8], BF16)
    red = al("red", [128, 24], F32)
    lnm = al("lnm", [128, 2], F32)
    nqb = al("nqb", [1, 16], F32)
    mv = al("mv", [128, 2], F32)
    lnt = al("lnt", [128, 4], F32)
    r_m1 = al("r_m1", [128, 4], F32)
    r_pk = al("r_pk", [128, 4, E], F32)
    assert cursor[0] <= 117248, cursor[0]

    PH = 117248
    TOP = 229344

    def at(name, shape, dtype, off):
        assert PH + off + nbytes(shape, dtype) <= TOP, name
        return nc.alloc_sbuf_tensor_at(name, list(shape), dtype, offset=PH + off)

    a_xslab = at("a_xslab", [128, 2, 8, 512], BF16, 0)
    a_wg = at("a_wg", [128, 2, 8, 768], BF16, 16384)
    a_q = at("a_q", [128, 2, S], BF16, 40960)
    a_k = at("a_k", [128, 2, S], BF16, 49152)
    a_v = at("a_v", [128, 16, 4, 64], BF16, 57344)
    a_attn = at("a_attn", [128, 8, S], BF16, 65664)
    a_xstage = at("a_xstage", [128, 2, D], F32, 98432)
    a_xcast = at("a_xcast", [128, 2, D], BF16, 106624)
    sqbuf = at("sqbuf", [128, 2, 2048], BF16, 98432)
    P_T = at("P_T", [128, 4, 2, 512], BF16, 98432)
    a_fmtile = at("a_fmtile", [128, 2, 8, 128], BF16, 16384)
    woT = at("woT", [128, 8, D], BF16, 40960)
    r_tmp = at("r_tmp", [128, 2, D], F32, 0)
    o_lnscr = at("o_lnscr", [128, D], F32, 8192)
    f_xn2fm = at("f_xn2fm", [128, 8, S], BF16, 0)
    f_h = at("f_h", [128, FC, 512], BF16, 32768)
    f_w1 = at("f_w1", [128, 2, 8, 512], BF16, 65664)
    f_w2 = at("f_w2", [128, 4, D], BF16, 82048)
    f_xn2tm = at("f_xn2tm", [128, 2, D], F32, 90240)
    f_out = at("f_out", [128, 2, D], F32, 90240)
    f_xnf = at("f_xnf", [128, 2, 128], F32, 107520)
    gp_j = at("gp_j", [128, 128], F32, 32768)
    lgd = at("lgd", [128, 16, 8], F32, 33280)
    f_xnlc = at("f_xnlc", [128, 2, D], BF16, 33792)
    f_xnlo = at("f_xnlo", [128, 2, 8, 128], BF16, 37888)
    rwhl = at("rwhl", [128, 2, 8, E], BF16, 42496)
    f_lnscr = at("f_lnscr", [128, D], F32, 98432)
    f_xncast = at("f_xncast", [128, 2, D], BF16, 103424)
    p_w = at("p_w", [128, 2, D], F32, 40960)
    p_m = at("p_m", [128, 2, D], F32, 49152)
    p_gt = at("p_gt", [128, 2, D], BF16, 57344)
    p_gated = at("p_gated", [128, 2, D], BF16, 61440)
    p_t = at("p_t", [128, 2, 8, 128], BF16, 65664)
    c_in = at("c_in", [128, 2, 2048], F32, 69760)
    c_out = at("c_out", [128, 2, 2048], BF16, 86144)
    p_vec = at("p_vec", [8, 6 * 1024], F32, 0)

    ps = nc.alloc_psum_tensor("ps", [128, 8, 512], F32)

    es = contextlib.ExitStack()
    SEMS = {}

    def sem(n):
        SEMS[n] = es.enter_context(nc.semaphore(n))
        return SEMS[n]

    for n in ("sd_misc sd_pw sd_pt sd_cvin sd_cvout sd_x sd_xt sd_xs sd_slab "
              "sd_wg sd_wo sd_res sd_xnt sd_w1 sd_w2 sd_out sv_init sv_gate "
              "sv_cv sv_pb sv_xc st_proj sv_qkv st_s ss_exp st_pv sv_rec "
              "st_bc sv_attn st_nq sv_nq sv_nb sv_sq st_op sv_r1 st_tp sv_xnf "
              "st_lg sv_wd st_h ss_h st_y sv_acc sv_out sv_xn st_rtp sv_ln "
              "ss_ln ss_wd ss_nb sg_wd sv_w2c").split():
        sem(n)
    g = SEMS

    CNT = {}

    def add(name, n):
        CNT[name] = CNT.get(name, 0) + n

    def cur(name):
        return CNT.get(name, 0)

    blk = es.enter_context(nc.Block())

    def p0_sync(sy):
        sy.dma_start(out=p_vec[0:1, 0:D], in_=bo).then_inc(g["sd_misc"], 16)
        sy.dma_start(out=p_vec[0:1, D:2 * D], in_=mbo).then_inc(g["sd_misc"], 16)
        sy.dma_start(out=p_vec[0:1, 2 * D:3 * D], in_=gamma).then_inc(g["sd_misc"], 16)
        sy.dma_start(out=p_vec[0:1, 3 * D:4 * D], in_=beta).then_inc(g["sd_misc"], 16)
        sy.dma_start(out=p_vec[0:8, 4 * D:5 * D], in_=rw).then_inc(g["sd_misc"], 16)
        for oc in range(32):
            b = oc % 2
            if oc >= 2:
                sy.wait_ge(g["sd_pt"], 16 * 9 * (oc - 1))
            wsrc = wqkv[oc * 128:(oc + 1) * 128, :] if oc < 24 else \
                wo[(oc - 24) * 128:(oc - 23) * 128, :]
            msrc = mqkv[oc * 128:(oc + 1) * 128, :] if oc < 24 else \
                mo[(oc - 24) * 128:(oc - 23) * 128, :]
            sy.dma_start(out=p_w[:, b, :], in_=wsrc).then_inc(g["sd_pw"], 16)
            sy.dma_start(out=p_m[:, b, :], in_=msrc).then_inc(g["sd_pw"], 16)
            sy.wait_ge(g["sv_gate"], oc + 1)
            for kc in range(8):
                sy.dma_start_transpose(
                    p_t[:, b, kc, :], p_gated[:, b, kc * 128:(kc + 1) * 128]
                ).then_inc(g["sd_pt"], 16)
            sy.wait_ge(g["sd_pt"], 16 * (9 * oc + 8))
            dst = wqkvT_s[:, :, oc * 128:(oc + 1) * 128] if oc < 24 else \
                woT_s[:, :, (oc - 24) * 128:(oc - 23) * 128]
            sy.dma_start(out=dst, in_=p_t[:, b, :, :]).then_inc(g["sd_pt"], 16)
        u2 = 0
        for e in range(E):
            for fc in range(FC):
                b2 = u2 % 2
                if u2 >= 2:
                    sy.wait_ge(g["sv_w2c"], u2 - 1)
                sy.dma_start(out=c_in[:, 1, b2 * 1024:(b2 + 1) * 1024],
                             in_=w2[e, fc * 128:(fc + 1) * 128, :]).then_inc(
                    g["sd_w1"], 16)
                sy.wait_ge(g["sv_w2c"], u2 + 1)
                sy.dma_start(out=w2s[e, fc],
                             in_=c_out[:, 1, b2 * 1024:(b2 + 1) * 1024]
                             ).then_inc(g["sd_w2"], 16)
                u2 += 1

    def p0_gp(gp):
        gp.memset(ident_f[:], 0.0)
        gp.affine_select(out=ident_f[:], in_=ident_f[:],
                         compare_op=ALU.not_equal, fill=1.0, base=0,
                         pattern=[[-1, 128]], channel_multiplier=1)
        gp.memset(ones_b[:], 1.0)
        gp.memset(ones_f[:], 1.0)
        gp.memset(normones[:], 0.0)
        gp.memset(normones[0:64, 0:1], 1.0)
        gp.memset(normones[64:128, 1:2], 1.0)
        gp.memset(epsb[:], EPS)
        gp.memset(mhatb[:], 0.0)
        gp.memset(red[:], 1.0)
        gp.memset(nqb128[:], 1.0)
        gp.memset(mt[:], 1.0)
        gp.memset(lnm[:], 0.0)
        gp.memset(lnt[:], 1.0)
        gp.memset(r_m1[:], 0.0)
        gp.memset(r_pk[:], 0.0)
        gp.memset(bc_sb[:], 0.0)
        gp.memset(cl2[:, 0:1], 60.0)
        gp.memset(cl2[:, 1:2], -60.0)
        gp.memset(normf[:], 0.0)
        gp.memset(normf[0:64, 0, :], 1.0)
        gp.memset(normf[64:128, 1, :], 1.0)
        gp.engine_nop().then_inc(g["sv_init"], 1)
        u = 0
        for e in range(E):
            for dc in range(8):
                for hf in range(2):
                    if u >= 1:
                        gp.wait_ge(g["sd_cvout"], 16 * u)
                    gp.dma_start(
                        out=c_in[:, 0, :],
                        in_=w1[e, dc * 128:(dc + 1) * 128,
                               hf * 2048:(hf + 1) * 2048]).then_inc(
                        g["sd_cvin"], 16)
                    gp.wait_ge(g["sv_cv"], u + 1)
                    gp.dma_start(
                        out=w1s[e, 4 * hf:4 * hf + 4, :, dc, :].rearrange(
                            "b p f -> p b f"),
                        in_=c_out[:, 0, :].rearrange("p (b f) -> p b f", b=4)
                    ).then_inc(g["sd_cvout"], 16)
                    u += 1

    def p0_vec(ve):
        for oc in range(32):
            b = oc % 2
            if oc >= 2:
                ve.wait_ge(g["sd_pt"], 16 * (9 * (oc - 2) + 8))
            ve.wait_ge(g["sd_pw"], 16 * 2 * (oc + 1))
            ve.tensor_scalar(out=p_gt[:, b, :], in0=p_m[:, b, :], scalar1=0.0,
                             scalar2=None, op0=ALU.is_gt)
            ve.tensor_tensor(out=p_gated[:, b, :], in0=p_w[:, b, :],
                             in1=p_gt[:, b, :], op=ALU.mult).then_inc(
                g["sv_gate"], 1)
        u2 = 0
        for u in range(128):
            ve.wait_ge(g["sd_cvin"], 16 * (u + 1))
            if u >= 1:
                ve.wait_ge(g["sd_cvout"], 16 * u)
            ve.tensor_copy(out=c_out[:, 0, :],
                           in_=c_in[:, 0, :]).then_inc(g["sv_cv"], 1)
            for _ in range(2):
                if u2 < 256:
                    b2 = u2 % 2
                    ve.wait_ge(g["sd_w1"], 16 * (u2 + 1))
                    if u2 >= 2:
                        ve.wait_ge(g["sd_w2"], 16 * (u2 - 1))
                    ve.tensor_copy(
                        out=c_out[:, 1, b2 * 1024:(b2 + 1) * 1024],
                        in_=c_in[:, 1, b2 * 1024:(b2 + 1) * 1024]).then_inc(
                        g["sv_w2c"], 1)
                    u2 += 1
        ve.wait_ge(g["sd_misc"], 16 * 5)
        ve.tensor_scalar(out=p_vec[0:1, 5 * D:6 * D], in0=p_vec[0:1, D:2 * D],
                         scalar1=0.0, scalar2=None, op0=ALU.is_gt)
        ve.tensor_tensor(out=p_vec[0:1, 0:D], in0=p_vec[0:1, 0:D],
                         in1=p_vec[0:1, 5 * D:6 * D], op=ALU.mult).then_inc(
            g["sv_pb"], 1)
        for i in range(6):
            ve.wait_ge(g["st_rtp"], i + 1)
            dstt = (bias_b, bias_b, gamma_b, gamma_b, beta_b, beta_b)[i]
            ve.tensor_copy(out=dstt[:, (i % 2) * 512:(i % 2 + 1) * 512],
                           in_=ps[:, i % 2, :]).then_inc(g["sv_pb"], 1)
        for dc in range(8):
            ve.wait_ge(g["st_rtp"], 7 + dc)
            ve.tensor_copy(out=router_T[:, dc, :],
                           in_=ps[:, 2 + dc % 2, 0:E]).then_inc(g["sv_pb"], 1)

    def p0_te(te):
        te.wait_ge(g["sv_init"], 1)
        te.wait_ge(g["sv_pb"], 1)
        for i in range(6):
            srcoff = (0, 0, 2 * D, 2 * D, 3 * D, 3 * D)[i] + (i % 2) * 512
            if i >= 2:
                te.wait_ge(g["sv_pb"], i)
            te.matmul(ps[:, i % 2, :], lhsT=ones_f[0:1, :],
                      rhs=p_vec[0:1, srcoff:srcoff + 512],
                      start=True, stop=True).then_inc(g["st_rtp"], 1)
        for dc in range(8):
            if dc >= 2:
                te.wait_ge(g["sv_pb"], 7 + dc)
            te.transpose(ps[:, 2 + dc % 2, 0:E],
                         p_vec[0:8, 4 * D + dc * 128:4 * D + (dc + 1) * 128],
                         ident_f[0:8, 0:8]).then_inc(g["st_rtp"], 1)

    blk.sync(p0_sync)
    blk.gpsimd(p0_gp)
    blk.vector(p0_vec)
    blk.tensor(p0_te)

    add("sd_misc", 16 * 5)
    add("sd_pw", 16 * 64)
    add("sd_pt", 16 * 9 * 32)
    add("sd_cvin", 16 * 128)
    add("sd_cvout", 16 * 128)
    add("sd_w1", 16 * 256)
    add("sd_w2", 16 * 256)
    add("sv_init", 1)
    add("sv_gate", 32)
    add("sv_cv", 128)
    add("sv_pb", 15)
    add("sv_w2c", 256)
    add("st_rtp", 14)

    def ln_stats(ve, src, scr):
        ve.reduce_sum(out=red[:, 0:1],
                      in_=src.rearrange("p (a b) -> p a b", a=1), axis=AX.X)
        ve.tensor_tensor(out=scr[:], in0=src, in1=src, op=ALU.mult)
        ve.reduce_sum(out=red[:, 1:2],
                      in_=scr[:].rearrange("p (a b) -> p a b", a=1),
                      axis=AX.X).then_inc(g["sv_ln"], 1)

    def sguard(sc):
        sc.activation(out=scjunk[:], in_=bias_b[:, 0:192], func=AF.Copy)
        sc.activation(out=scjunk[:], in_=bias_b[:, 192:384], func=AF.Copy)

    def scg(sc):
        sc.activation(out=scjunk[:, 0:96], in_=bias_b[:, 0:96], func=AF.Copy)
        sc.activation(out=scjunk[:, 96:192], in_=bias_b[:, 0:96], func=AF.Copy)

    def ln_sc_apply(sc, n, src, dst):
        sc.wait_ge(g["sv_ln"], n)
        sguard(sc)
        sc.activation(out=lnm[:, 0:1], in_=red[:, 0:1], func=AF.Copy,
                      scale=1.0 / D)
        sc.activation(out=lnm[:, 1:2], in_=lnm[:, 0:1], func=AF.Square)
        sc.activation(out=lnt[:, 3:4], in_=lnm[:, 1:2], func=AF.Identity,
                      scale=-1.0, bias=epsb[:, 0:1])
        scg(sc)
        sc.activation(out=lnt[:, 0:1], in_=red[:, 1:2], func=AF.Identity,
                      scale=1.0 / D, bias=lnt[:, 3:4])
        sc.activation(out=lnt[:, 1:2], in_=lnt[:, 0:1], func=AF.Ln)
        sc.activation(out=lnt[:, 2:3], in_=lnt[:, 1:2], func=AF.Exp,
                      scale=-0.5)
        scg(sc)
        sc.mul(out=lnt[:, 3:4], in_=lnm[:, 0:1], mul=lnt[:, 2:3])
        sc.activation(out=lnt[:, 3:4], in_=lnt[:, 3:4], func=AF.Identity,
                      scale=-1.0)
        scg(sc)
        return sc.activation(out=dst, in_=src, func=AF.Identity,
                             scale=lnt[:, 2:3], bias=lnt[:, 3:4])

    for s in range(1 if dbg else 2):
        X0 = cur("sd_x"); XT0 = cur("sd_xt"); XS0 = cur("sd_xs")
        XC0 = cur("sv_xc")

        def a0_sync(sy, s=s, X0=X0, XT0=XT0, XS0=XS0, XC0=XC0):
            if s == 0:
                sy.wait_ge(g["sd_pt"], 16 * 9 * 32)
                sy.wait_ge(g["sd_cvout"], 16 * 128)
                sy.wait_ge(g["sd_w2"], 16 * 256)
                sy.wait_ge(g["sv_pb"], 15)
            else:
                sy.wait_ge(g["sv_out"], 16 * s)
                sy.wait_ge(g["sd_out"], 16 * 16 * s)
            for tc in range(TC):
                b = tc % 2
                if tc >= 2:
                    sy.wait_ge(g["sd_xs"], XS0 + 16 * (tc - 1))
                sy.dma_start(out=a_xstage[:, b, :],
                             in_=x_in[s, tc * 128:(tc + 1) * 128, :]).then_inc(
                    g["sd_x"], 16)
                sy.wait_ge(g["sv_xc"], XC0 + tc + 1)
                for kc in range(8):
                    sy.dma_start_transpose(
                        a_fmtile[:, b, kc, :],
                        a_xcast[:, b, kc * 128:(kc + 1) * 128]).then_inc(
                        g["sd_xt"], 16)
                sy.wait_ge(g["sd_xt"], XT0 + 16 * (8 * tc + 8))
                sy.dma_start(out=xfm_s[:, :, tc * 128:(tc + 1) * 128],
                             in_=a_fmtile[:, b, :, :]).then_inc(g["sd_xs"], 16)

        def a0_vec(ve, s=s, X0=X0, XT0=XT0):
            if s == 0:
                ve.wait_ge(g["sd_pt"], 16 * 9 * 32)
                ve.wait_ge(g["sv_pb"], 15)
            else:
                ve.wait_ge(g["sv_out"], 16 * s)
            for tc in range(TC):
                b = tc % 2
                if tc >= 2:
                    ve.wait_ge(g["sd_xt"], XT0 + 16 * 8 * (tc - 1))
                ve.wait_ge(g["sd_x"], X0 + 16 * (tc + 1))
                ve.tensor_copy(out=a_xcast[:, b, :],
                               in_=a_xstage[:, b, :]).then_inc(g["sv_xc"], 1)

        blk.sync(a0_sync)
        blk.vector(a0_vec)
        add("sd_x", 16 * TC)
        add("sd_xt", 16 * 8 * TC)
        add("sd_xs", 16 * TC)
        add("sv_xc", TC)

        WG0 = cur("sd_wg"); SLAB0 = cur("sd_slab"); PROJ0 = cur("st_proj")
        QKV0 = cur("sv_qkv"); SQ0 = cur("sv_sq"); NQ0 = cur("st_nq")
        NQE0 = cur("sv_nq"); NB0 = cur("sv_nb"); S0 = cur("st_s")
        EXP0 = cur("ss_exp"); PV0 = cur("st_pv"); REC0 = cur("sv_rec")
        BC0 = cur("st_bc"); ATT0 = cur("sv_attn"); SSNB0 = cur("ss_nb")

        def a_sync(sy, s=s, WG0=WG0, SLAB0=SLAB0, PROJ0=PROJ0, ATT0=ATT0):
            sy.wait_ge(g["sd_xs"], cur("sd_xs"))
            for gi in range(NG):
                bg = gi % 2
                if gi >= 2:
                    sy.wait_ge(g["st_proj"], PROJ0 + 8 * QS * (gi - 1))
                qo, ko, vo = 256 * gi, D + 256 * gi, 2 * D + 256 * gi
                for i, o0 in enumerate((qo, ko, vo)):
                    sy.dma_start(out=a_wg[:, bg, :, i * 256:(i + 1) * 256],
                                 in_=wqkvT_s[:, :, o0:o0 + 256]).then_inc(
                        g["sd_wg"], 16)
                for qs in range(QS):
                    u = gi * QS + qs
                    if u >= 2:
                        sy.wait_ge(g["st_proj"], PROJ0 + 8 * (u - 1))
                    sy.dma_start(out=a_xslab[:, u % 2, :, :],
                                 in_=xfm_s[:, :, qs * 512:(qs + 1) * 512]
                                 ).then_inc(g["sd_slab"], 16)
            sy.wait_ge(g["sv_attn"], ATT0 + 32)
            sy.dma_start(out=woT[:], in_=woT_s).then_inc(g["sd_wo"], 16)

        def a_te(te, s=s, WG0=WG0, SLAB0=SLAB0, PROJ0=PROJ0, QKV0=QKV0,
                 SQ0=SQ0, NQ0=NQ0, NQE0=NQE0, NB0=NB0, S0=S0, EXP0=EXP0,
                 PV0=PV0, REC0=REC0, BC0=BC0, ATT0=ATT0):
            nproj = 0
            nunit = 0
            for gi in range(NG):
                bg = gi % 2
                te.wait_ge(g["sd_wg"], WG0 + 16 * 3 * (gi + 1))
                for qs in range(QS):
                    u = gi * QS + qs
                    te.wait_ge(g["sd_slab"], SLAB0 + 16 * (u + 1))
                    for which in range(2):
                        for oc in range(2):
                            bank = nproj % 3
                            if nproj >= 3:
                                te.wait_ge(g["sv_qkv"], QKV0 + nproj - 2)
                            toff = which * 256 + oc * 128
                            for kc in range(8):
                                mmi = te.matmul(
                                    ps[:, bank, :],
                                    lhsT=a_wg[:, bg, kc, toff:toff + 128],
                                    rhs=a_xslab[:, u % 2, kc, :],
                                    start=(kc == 0), stop=(kc == 7))
                            mmi.then_inc(g["st_proj"], 1)
                            nproj += 1
                    for tcw in range(4):
                        bank = nproj % 3
                        if nproj >= 3:
                            te.wait_ge(g["sv_qkv"], QKV0 + nproj - 2)
                        for kc in range(8):
                            mmi = te.matmul(
                                ps[:, bank, 0:256],
                                lhsT=a_xslab[:, u % 2, kc,
                                             tcw * 128:(tcw + 1) * 128],
                                rhs=a_wg[:, bg, kc, 512:768],
                                start=(kc == 0), stop=(kc == 7))
                        mmi.then_inc(g["st_proj"], 1)
                        nproj += 1
                for pc in range(NPAIR):
                    pi = 2 * gi + pc
                    te.wait_ge(g["sv_attn"], ATT0 + 4 * pi)
                    te.wait_ge(g["sv_sq"], SQ0 + 2 * pi + 2)
                    for i in range(16):
                        qk, hp, q4 = i // 8, (i // 4) % 2, i % 4
                        if i >= 2:
                            te.wait_ge(g["sv_nq"], NQE0 + 16 * pi + i - 1)
                        te.matmul(ps[:, 4 + i % 2, :],
                                  lhsT=normf[:, hp, :],
                                  rhs=sqbuf[:, qk, q4 * 512:(q4 + 1) * 512],
                                  start=True, stop=True).then_inc(g["st_nq"], 1)
                    te.wait_ge(g["sv_nb"], NB0 + pi + 1)
                    for qs in range(QS):
                        u = nunit
                        up = u % 2
                        hh0 = 2 * pc

                        def s_mm(jc, u=u):
                            jcp = jc % 2
                            sb = 2 * jcp
                            if jc >= 2:
                                te.wait_ge(g["ss_exp"], EXP0 + 16 * u + jc - 1)
                            elif u >= 1:
                                te.wait_ge(g["ss_exp"], EXP0 + 16 * u)
                            te.matmul(ps[:, sb, :],
                                      lhsT=a_k[0:64, pc, jc * 128:(jc + 1) * 128],
                                      rhs=a_q[0:64, pc, qs * 512:(qs + 1) * 512],
                                      start=True, stop=True, tile_position=(0, 0))
                            te.matmul(ps[:, sb + 1, :],
                                      lhsT=a_k[64:128, pc,
                                               jc * 128:(jc + 1) * 128],
                                      rhs=a_q[64:128, pc,
                                              qs * 512:(qs + 1) * 512],
                                      start=True, stop=True,
                                      tile_position=(64, 0)).then_inc(g["st_s"], 1)

                        def pv_mm(jc, u=u, up=up, hh0=hh0):
                            jcp = jc % 4
                            te.wait_ge(g["ss_exp"], EXP0 + 16 * u + jc + 1)
                            if jc == 0:
                                if u >= 2:
                                    te.wait_ge(g["sv_attn"], ATT0 + u - 1)
                            te.matmul(ps[0:64, 4 + up, :],
                                      lhsT=a_v[:, jc, hh0, :],
                                      rhs=P_T[:, jcp, 0, :],
                                      start=(jc == 0), stop=(jc == JC - 1),
                                      tile_position=(0, 0))
                            te.matmul(ps[64:128, 4 + up, :],
                                      lhsT=a_v[:, jc, hh0 + 1, :],
                                      rhs=P_T[:, jcp, 1, :],
                                      start=(jc == 0), stop=(jc == JC - 1),
                                      tile_position=(0, 64))
                            te.matmul(ps[0:64, 6 + up, :],
                                      lhsT=ones_b[:, 0:64],
                                      rhs=P_T[:, jcp, 0, :],
                                      start=(jc == 0), stop=(jc == JC - 1),
                                      tile_position=(0, 0))
                            mmi = te.matmul(ps[64:128, 6 + up, :],
                                            lhsT=ones_b[:, 0:64],
                                            rhs=P_T[:, jcp, 1, :],
                                            start=(jc == 0), stop=(jc == JC - 1),
                                            tile_position=(0, 64))
                            mmi.then_inc(g["st_pv"], 1)

                        s_mm(0)
                        s_mm(1)
                        for jc in range(2, JC):
                            s_mm(jc)
                            pv_mm(jc - 2)
                        pv_mm(JC - 2)
                        pv_mm(JC - 1)
                        nunit += 1

        def a_vec(ve, s=s, PROJ0=PROJ0, QKV0=QKV0, SQ0=SQ0, NQ0=NQ0,
                  NQE0=NQE0, NB0=NB0, EXP0=EXP0, PV0=PV0, REC0=REC0,
                  BC0=BC0, ATT0=ATT0, SSNB0=SSNB0):
            nproj = 0
            nunit = 0
            for gi in range(NG):
                for qs in range(QS):
                    for which in range(2):
                        for oc in range(2):
                            ve.wait_ge(g["st_proj"], PROJ0 + nproj + 1)
                            dst = a_q if which == 0 else a_k
                            ve.tensor_copy(
                                out=dst[:, oc, qs * 512:(qs + 1) * 512],
                                in_=ps[:, nproj % 3, :]).then_inc(g["sv_qkv"], 1)
                            nproj += 1
                    for tcw in range(4):
                        ve.wait_ge(g["st_proj"], PROJ0 + nproj + 1)
                        ve.tensor_copy(
                            out=a_v[:, qs * 4 + tcw, :, :],
                            in_=ps[:, nproj % 3, 0:256].rearrange(
                                "p (h d) -> p h d", h=4)).then_inc(g["sv_qkv"], 1)
                        nproj += 1
                for pc in range(NPAIR):
                    pi = 2 * gi + pc
                    if pi >= 1:
                        ve.wait_ge(g["st_nq"], NQ0 + 16 * pi)
                        ve.wait_ge(g["ss_exp"], EXP0 + 64 * pi)
                        ve.wait_ge(g["st_pv"], PV0 + 16 * 4 * pi)
                    ve.tensor_tensor(out=sqbuf[:, 0, :], in0=a_q[:, pc, :],
                                     in1=a_q[:, pc, :], op=ALU.mult)
                    ve.tensor_tensor(out=sqbuf[:, 1, :], in0=a_k[:, pc, :],
                                     in1=a_k[:, pc, :],
                                     op=ALU.mult).then_inc(g["sv_sq"], 2)
                    for i in range(16):
                        ve.wait_ge(g["st_nq"], NQ0 + 16 * pi + i + 1)
                        ve.reduce_max(out=red[:, i:i + 1],
                                      in_=ps[:, 4 + i % 2, :],
                                      axis=AX.X).then_inc(g["sv_nq"], 1)
                    ve.wait_ge(g["ss_nb"], SSNB0 + 2 * pi + 1)
                    ve.tensor_copy(out=a_xcast[:, 0, :].bitcast(F32)
                                   [:, 0:256], in_=bias_b[:, 0:256])
                    ve.reduce_max(out=red[:, 16:17], in_=nqb128[:, 0:8],
                                  axis=AX.X)
                    ve.reduce_max(out=red[:, 17:18], in_=nqb128[:, 8:16],
                                  axis=AX.X).then_inc(g["sv_nb"], 1)
                    if dbg:
                        ve.wait_ge(g["ss_nb"], SSNB0 + 2 * pi + 2)
                        ve.tensor_copy(out=nq[0:1, 16 + pi:17 + pi],
                                       in_=mhatb[0:1, pc:pc + 1])
                    for qs in range(QS):
                        u = nunit
                        up = u % 2
                        ve.wait_ge(g["st_pv"], PV0 + 16 * u + 16)
                        ve.reciprocal(out=bc_sb[:], in_=ps[:, 6 + up, :])
                        ve.tensor_copy(out=a_xcast[:, 0, :].bitcast(F32)
                                       [:, 0:256], in_=bias_b[:, 0:256])
                        ve.tensor_copy(out=a_xcast[:, 1, :].bitcast(F32)
                                       [:, 0:256], in_=bias_b[:, 0:256])
                        ve.tensor_tensor(
                            out=a_attn[:, pi, qs * 512:(qs + 1) * 512],
                            in0=ps[:, 4 + up, :], in1=bc_sb[:],
                            op=ALU.mult).then_inc(g["sv_attn"], 1)
                        nunit += 1

        def a_sc(sc, s=s, S0=S0, EXP0=EXP0, PV0=PV0, NB0=NB0, NQE0=NQE0,
                 SSNB0=SSNB0):
            nunit = 0
            for gi in range(NG):
                for pc in range(NPAIR):
                    pi = 2 * gi + pc
                    sc.wait_ge(g["sv_nq"], NQE0 + 16 * pi + 16)
                    sguard(sc)
                    sc.activation(out=nqb128[:], in_=red[:, 0:16],
                                  func=AF.Copy).then_inc(g["ss_nb"], 1)
                    sc.wait_ge(g["sv_nb"], NB0 + pi + 1)
                    sguard(sc)
                    sc.activation(out=mt[:, 0:1], in_=red[:, 16:17],
                                  func=AF.Copy)
                    scg(sc)
                    sc.activation(out=mt[:, 1:2], in_=red[:, 17:18],
                                  func=AF.Identity, bias=mt[:, 0:1])
                    sc.mul(out=mt[:, 1:2], in_=mt[:, 1:2],
                           mul=SCALE * 1.01 * 0.5)
                    sc.activation(out=mt[:, 0:1], in_=mt[:, 1:2],
                                  func=AF.Relu)
                    sc.activation(out=mt[:, 1:2], in_=mt[:, 0:1],
                                  func=AF.Identity, scale=-1.0, bias=cl2[:, 0:1])
                    sc.activation(out=mt[:, 1:2], in_=mt[:, 1:2], func=AF.Relu)
                    sc.activation(out=mhatb[:, pc:pc + 1], in_=mt[:, 1:2],
                                  func=AF.Identity, scale=1.0,
                                  bias=cl2[:, 1:2])
                    scg(sc)
                    sc.engine_nop().then_inc(g["ss_nb"], 1) if False else \
                        sc.activation(out=scjunk[:, 0:64],
                                      in_=bias_b[:, 0:64],
                                      func=AF.Copy).then_inc(g["ss_nb"], 1)
                    for qs in range(QS):
                        u = nunit
                        for jc in range(JC):
                            jcp = jc % 2
                            sc.wait_ge(g["st_s"], S0 + 16 * u + jc + 1)
                            if jc >= 4:
                                sc.wait_ge(g["st_pv"], PV0 + 16 * u + jc - 3)
                            elif u >= 1:
                                sc.wait_ge(g["st_pv"], PV0 + 16 * u)
                            sc.activation(
                                out=P_T[:, jc % 4, :, :],
                                in_=ps[:, 2 * jcp:2 * jcp + 2, :],
                                func=AF.Exp, bias=mhatb[:, pc:pc + 1],
                                scale=SCALE).then_inc(g["ss_exp"], 1)
                        nunit += 1

        blk.sync(a_sync)
        blk.tensor(a_te)
        blk.vector(a_vec)
        blk.scalar(a_sc)

        add("sd_wg", 16 * 3 * NG)
        add("sd_slab", 16 * NG * QS)
        add("sd_wo", 16)
        add("st_proj", 8 * NG * QS)
        add("sv_qkv", 8 * NG * QS)
        add("sv_sq", 2 * NG * NPAIR)
        add("st_nq", 16 * NG * NPAIR)
        add("sv_nq", 16 * NG * NPAIR)
        add("sv_nb", NG * NPAIR)
        add("ss_nb", 2 * NG * NPAIR)
        add("st_s", 16 * 32)
        add("ss_exp", 16 * 32)
        add("st_pv", 16 * 32)
        add("sv_attn", 32)

        OP0 = cur("st_op"); R10 = cur("sv_r1"); RES0 = cur("sd_res")
        LNV0 = cur("sv_ln"); LNS0 = cur("ss_ln"); ATT1 = cur("sv_attn")

        def o_gp(gp, s=s, ATT1=ATT1, R10=R10):
            gp.wait_ge(g["sv_attn"], ATT1)
            for tc in range(TC):
                if tc >= 2:
                    gp.wait_ge(g["sv_r1"], R10 + tc - 1)
                gp.dma_start(out=a_xstage[:, tc % 2, :],
                             in_=x_in[s, tc * 128:(tc + 1) * 128, :]).then_inc(
                    g["sd_res"], 16)

        def o_te(te, s=s, R10=R10):
            te.wait_ge(g["sd_wo"], 16 * (s + 1))
            for tc in range(TC):
                bs = (tc % 2) * 4
                if tc >= 2:
                    te.wait_ge(g["sv_r1"], R10 + tc - 1)
                for p in range(8):
                    for dh in range(2):
                        te.matmul(ps[:, bs + dh, :],
                                  lhsT=a_attn[0:64, p, tc * 128:(tc + 1) * 128],
                                  rhs=woT[0:64, p, dh * 512:(dh + 1) * 512],
                                  start=(p == 0), stop=(p == 7),
                                  tile_position=(0, 0))
                        mmi = te.matmul(
                            ps[:, bs + 2 + dh, :],
                            lhsT=a_attn[64:128, p, tc * 128:(tc + 1) * 128],
                            rhs=woT[64:128, p, dh * 512:(dh + 1) * 512],
                            start=(p == 0), stop=(p == 7),
                            tile_position=(64, 0))
                mmi.then_inc(g["st_op"], 1)

        def o_vec(ve, s=s, OP0=OP0, R10=R10, RES0=RES0, LNS0=LNS0):
            for tc in range(TC):
                b = tc % 2
                bs = (tc % 2) * 4
                ve.wait_ge(g["st_op"], OP0 + tc + 1)
                ve.wait_ge(g["sd_res"], RES0 + 16 * (tc + 1))
                ve.scalar_tensor_tensor(
                    out=r_tmp[:, b, :],
                    in0=ps[:, bs:bs + 2, :].rearrange("p a b -> p (a b)"),
                    scalar=0.0, in1=bias_b[:], op0=ALU.bypass, op1=ALU.add)
                ve.scalar_tensor_tensor(
                    out=r_tmp[:, b, :],
                    in0=ps[:, bs + 2:bs + 4, :].rearrange("p a b -> p (a b)"),
                    scalar=0.0, in1=r_tmp[:, b, :], op0=ALU.bypass, op1=ALU.add)
                ve.tensor_tensor(out=r_tmp[:, b, :], in0=r_tmp[:, b, :],
                                 in1=a_xstage[:, b, :], op=ALU.add)
                if dbg == 4:
                    ve.tensor_copy(out=acc[:, tc, :], in_=r_tmp[:, b, :])
                    ve.engine_nop().then_inc(g["sv_r1"], 1)
                else:
                    ve.engine_nop().then_inc(g["sv_r1"], 1)
                    if tc >= 2:
                        ve.wait_ge(g["ss_ln"], LNS0 + tc - 1)
                    ln_stats(ve, r_tmp[:, b, :], o_lnscr)

        def o_sc(sc, s=s, LNV0=LNV0):
            if dbg == 4:
                return
            for tc in range(TC):
                ln_sc_apply(sc, LNV0 + tc + 1, r_tmp[:, tc % 2, :],
                            acc[:, tc, :]).then_inc(g["ss_ln"], 1)

        blk.gpsimd(o_gp)
        blk.tensor(o_te)
        blk.vector(o_vec)
        blk.scalar(o_sc)

        add("sd_res", 16 * TC)
        add("st_op", TC)
        add("sv_r1", TC)
        add("sv_ln", TC)
        add("ss_ln", TC)

        if dbg in (1, 4):
            def dbg_sync(sy, LNS0=LNS0):
                sy.wait_ge(g["sv_r1"], cur("sv_r1"))
                if dbg == 1:
                    sy.wait_ge(g["ss_ln"], LNS0 + TC)
                for nm, src_t in ((d_r1, acc), (d_attn, None), (d_q, None)):
                    pass
                sy.dma_start(out=d_r1, in_=acc[:]).then_inc(g["sd_out"], 16)
                sy.dma_start(out=d_mh, in_=mhatb[:]).then_inc(g["sd_out"], 16)
                sy.dma_start(out=d_bb[:, 0, :], in_=bias_b[:]).then_inc(g["sd_out"], 16)
                sy.dma_start(out=d_bb[:, 1, :], in_=gamma_b[:]).then_inc(g["sd_out"], 16)
                sy.dma_start(out=d_bb[:, 2, :], in_=beta_b[:]).then_inc(g["sd_out"], 16)
                sy.dma_start(out=d_ln[:, 0:2], in_=mv[:]).then_inc(g["sd_out"], 16)
                sy.dma_start(out=d_ln[:, 2:6], in_=lnt[:]).then_inc(g["sd_out"], 16)
                sy.dma_start(out=d_rt[:, 0, :], in_=r_tmp[:, 1, :]).then_inc(g["sd_out"], 16)
                sy.dma_start(out=d_rt[:, 1, :], in_=o_lnscr[:]).then_inc(g["sd_out"], 16)
                sy.dma_start(out=d_nq, in_=nq[0:1, :]).then_inc(g["sd_out"], 16)
                sy.wait_ge(g["sd_out"], cur("sd_out") + 16 * 10)
            def dbg_vec(ve):
                ve.wait_ge(g["sv_attn"], cur("sv_attn"))
                ve.wait_ge(g["sd_out"], cur("sd_out") + 16 * 9)
                ve.tensor_copy(out=bcf32[:], in_=a_attn[:, :, :]).then_inc(g["sv_out"], 1)
                ve.tensor_copy(out=qf32[:], in_=a_q[:]).then_inc(g["sv_out"], 1)
                ve.tensor_copy(out=vf32[:], in_=a_v[:]).then_inc(g["sv_out"], 1)
                ve.wait_ge(g["sd_out"], cur("sd_out") + 16 * 7)
                ve.tensor_copy(out=qf32[:], in_=a_k[:]).then_inc(g["sv_out"], 1)
            def dbg_sync2(sy):
                sy.wait_ge(g["sv_out"], 3)
                sy.dma_start(out=d_attn, in_=bcf32[:]).then_inc(g["sd_out"], 16)
                sy.dma_start(out=d_q, in_=qf32[:]).then_inc(g["sd_out"], 16)
                sy.dma_start(out=d_v, in_=vf32[:]).then_inc(g["sd_out"], 16)
                sy.wait_ge(g["sv_out"], 4)
                sy.dma_start(out=d_k, in_=qf32[:]).then_inc(g["sd_out"], 16)
                sy.wait_ge(g["sd_out"], cur("sd_out") + 16 * 9)
            bcf32 = at("bcf32", [128, 8, S], F32, 0)
            qf32 = at("qf32", [128, 2, S], F32, 65664)
            vf32 = at("vf32", [128, 16, 4, 64], F32, 82048)
            blk.vector(dbg_vec)
            blk.sync(dbg_sync)
            blk.sync(dbg_sync2)
            es.close()
            return nc

        XN0 = cur("sv_xn"); XNT0 = cur("sd_xnt"); TP0 = cur("st_tp")
        XNF0 = cur("sv_xnf"); LG0 = cur("st_lg"); WD0 = cur("sv_wd")
        SSWD0 = cur("ss_wd"); SG0 = cur("sg_wd")
        LNV1 = cur("sv_ln"); LNS1 = cur("ss_ln"); R11 = cur("sv_r1")

        def f1_vec(ve, s=s, XN0=XN0, XNT0=XNT0, TP0=TP0, XNF0=XNF0, LG0=LG0,
                   WD0=WD0, LNS1=LNS1, R11=R11, SSWD0=SSWD0, SG0=SG0,
                   LNSO=LNS0):
            ve.tensor_copy(out=rwhl[:, 0, :, :],
                           in_=router_T[:, :, :])
            ve.tensor_tensor(out=rwhl[:, 1, :, :], in0=router_T[:, :, :],
                             in1=rwhl[:, 0, :, :],
                             op=ALU.subtract).then_inc(g["st_rtp"], 1)
            for tc in range(TC):
                b = tc % 2
                ve.wait_ge(g["ss_ln"], LNSO + tc + 1)
                if tc >= 2:
                    ve.wait_ge(g["sd_xnt"], XNT0 + 16 * 16 * (tc - 1))
                ln_stats(ve, acc[:, tc, :], f_lnscr)
                ve.wait_ge(g["ss_ln"], LNS1 + tc + 1)
                ve.tensor_copy(out=f_lnscr[:, 0:256], in_=bias_b[:, 0:256])
                ve.tensor_copy(out=f_lnscr[:, 256:512], in_=bias_b[:, 0:256])
                ve.tensor_tensor(out=f_xn2tm[:, b, :], in0=f_xn2tm[:, b, :],
                                 in1=gamma_b[:], op=ALU.mult)
                ve.tensor_tensor(out=f_xn2tm[:, b, :], in0=f_xn2tm[:, b, :],
                                 in1=beta_b[:], op=ALU.add)
                ve.tensor_copy(out=f_xncast[:, b, :], in_=f_xn2tm[:, b, :])
                ve.tensor_tensor(out=f_xnlc[:, b, :], in0=f_xn2tm[:, b, :],
                                 in1=f_xncast[:, b, :],
                                 op=ALU.subtract).then_inc(g["sv_xn"], 1)
                ve.tensor_scalar(out=acc[:, tc, :], in0=acc[:, tc, :],
                                 scalar1=2.0, scalar2=None, op0=ALU.mult)
                ve.wait_ge(g["st_lg"], LG0 + tc + 1)
                lg = ps[:, 2, 0:E]
                if dbg:
                    ve.tensor_copy(out=lgd[:, tc, :], in_=lg)
                ve.reduce_max(out=red[:, 2:3], in_=lg,
                              axis=AX.X).then_inc(g["sv_wd"], 1)
                ve.wait_ge(g["ss_wd"], SSWD0 + 3 * tc + 1)
                ve.tensor_copy(out=f_lnscr[:, 0:256], in_=bias_b[:, 0:256])
                ve.tensor_copy(out=f_lnscr[:, 256:512], in_=bias_b[:, 0:256])
                ve.tensor_scalar(out=r_pk[:, 1, :], in0=lg, scalar1=r_m1[:, 0:1],
                                 scalar2=None, op0=ALU.is_lt)
                ve.tensor_scalar(out=r_pk[:, 2, :], in0=lg, scalar1=BIG,
                                 scalar2=None, op0=ALU.add).then_inc(
                    g["sv_wd"], 1)
                ve.wait_ge(g["sg_wd"], SG0 + 2 * tc + 1)
                ve.wait_ge(g["ss_wd"], SSWD0 + 3 * tc + 2)
                ve.tensor_copy(out=f_lnscr[:, 0:256], in_=bias_b[:, 0:256])
                ve.tensor_copy(out=f_lnscr[:, 256:512], in_=bias_b[:, 0:256])
                ve.reduce_max(out=red[:, 3:4], in_=r_pk[:, 3, :], axis=AX.X)
                ve.reduce_sum(out=red[:, 4:5],
                              in_=r_pk[:, 0, :].rearrange(
                                  "p (a b) -> p a b", a=1),
                              axis=AX.X).then_inc(g["sv_wd"], 1)
                ve.wait_ge(g["sg_wd"], SG0 + 2 * tc + 2)
                ve.wait_ge(g["ss_wd"], SSWD0 + 3 * tc + 3)
                ve.tensor_copy(out=f_lnscr[:, 0:256], in_=bias_b[:, 0:256])
                ve.tensor_copy(out=f_lnscr[:, 256:512], in_=bias_b[:, 0:256])
                ve.scalar_tensor_tensor(out=wdense[:, tc, :],
                                        in0=r_pk[:, 0, :],
                                        scalar=r_m1[:, 3:4],
                                        in1=r_pk[:, 1, :],
                                        op0=ALU.mult,
                                        op1=ALU.mult).then_inc(g["sv_wd"], 1)

        def f1_gp(gp, s=s, WD0=WD0, SSWD0=SSWD0, SG0=SG0):
            for tc in range(TC):
                gp.wait_ge(g["sv_wd"], WD0 + 4 * tc + 2)
                gp.tensor_copy(out=gp_j[:], in_=bias_b[:, 0:128])
                gp.tensor_copy(out=gp_j[:], in_=bias_b[:, 128:256])
                gp.tensor_tensor(out=r_pk[:, 3, :], in0=r_pk[:, 1, :],
                                 in1=r_pk[:, 2, :], op=ALU.mult).then_inc(
                    g["sg_wd"], 1)
                gp.wait_ge(g["ss_wd"], SSWD0 + 3 * tc + 3)
                gp.tensor_copy(out=gp_j[:], in_=bias_b[:, 0:128])
                gp.tensor_copy(out=gp_j[:], in_=bias_b[:, 128:256])
                gp.tensor_scalar(out=r_pk[:, 1, :], in0=r_pk[:, 2, :],
                                 scalar1=r_m1[:, 2:3], scalar2=None,
                                 op0=ALU.is_ge).then_inc(g["sg_wd"], 1)

        def f1_sync(sy, s=s, XN0=XN0, LG0=LG0):
            for tc in range(TC):
                b = tc % 2
                sy.wait_ge(g["sv_xn"], XN0 + tc + 1)
                if tc >= 2:
                    sy.wait_ge(g["st_lg"], LG0 + tc - 1)
                for kc in range(8):
                    sy.dma_start_transpose(
                        f_xn2fm[:, kc, tc * 128:(tc + 1) * 128],
                        f_xncast[:, b, kc * 128:(kc + 1) * 128]).then_inc(
                        g["sd_xnt"], 16)
                for kc in range(8):
                    sy.dma_start_transpose(
                        f_xnlo[:, b, kc, :],
                        f_xnlc[:, b, kc * 128:(kc + 1) * 128]).then_inc(
                        g["sd_xnt"], 16)

        def f1_te(te, s=s, XN0=XN0, XNT0=XNT0, WD0=WD0, R11=R11, LNS1=LNS1):
            te.wait_ge(g["sv_r1"], R11)
            te.wait_ge(g["st_rtp"], 14 + s + 1)
            for tc in range(TC):
                b = tc % 2
                te.wait_ge(g["sd_xnt"], XNT0 + 16 * 16 * (tc + 1))
                if tc >= 1:
                    te.wait_ge(g["sv_wd"], WD0 + 4 * tc)
                nmm = 0
                for dc in range(8):
                    for which in range(3):
                        lhs = f_xn2fm[:, dc, tc * 128:(tc + 1) * 128] \
                            if which != 1 else f_xnlo[:, b, dc, :]
                        rhs = rwhl[:, 0, dc, :] if which != 2 \
                            else rwhl[:, 1, dc, :]
                        nmm += 1
                        mmi = te.matmul(ps[:, 2, 0:E], lhsT=lhs, rhs=rhs,
                                        start=(nmm == 1), stop=(nmm == 24))
                mmi.then_inc(g["st_lg"], 1)

        def f1_sc(sc, s=s, WD0=WD0, LNV1=LNV1, SSWD0=SSWD0, XN0=XN0):
            for tc in range(TC):
                if tc >= 2:
                    sc.wait_ge(g["sv_xn"], XN0 + tc - 1)
                ln_sc_apply(sc, LNV1 + tc + 1, acc[:, tc, :],
                            f_xn2tm[:, tc % 2, :]).then_inc(g["ss_ln"], 1)
                sc.wait_ge(g["sv_wd"], WD0 + 4 * tc + 1)
                sguard(sc)
                sc.activation(out=r_m1[:, 0:1], in_=red[:, 2:3], func=AF.Copy)
                sc.activation(out=r_m1[:, 1:2], in_=red[:, 2:3], func=AF.Copy,
                              scale=-1.0 / TEMP).then_inc(g["ss_wd"], 1)
                sc.wait_ge(g["sv_wd"], WD0 + 4 * tc + 2)
                scg(sc)
                sc.activation(out=r_pk[:, 0, :], in_=ps[:, 2, 0:E], func=AF.Exp,
                              bias=r_m1[:, 1:2],
                              scale=1.0 / TEMP).then_inc(g["ss_wd"], 1)
                sc.wait_ge(g["sv_wd"], WD0 + 4 * tc + 3)
                sguard(sc)
                sc.activation(out=r_m1[:, 2:3], in_=red[:, 3:4], func=AF.Copy)
                sc.activation(out=rt_s[:, 0:1], in_=red[:, 4:5], func=AF.Ln)
                sc.activation(out=r_m1[:, 3:4], in_=rt_s[:, 0:1], func=AF.Exp,
                              scale=-1.0).then_inc(g["ss_wd"], 1)

        blk.vector(f1_vec)
        blk.sync(f1_sync)
        blk.tensor(f1_te)
        blk.scalar(f1_sc)
        blk.gpsimd(f1_gp)

        add("sv_xn", TC)
        add("sd_xnt", 16 * 16 * TC)
        add("st_lg", TC)
        add("st_rtp", 1)
        add("sv_wd", 4 * TC)
        add("ss_wd", 3 * TC)
        add("sg_wd", 2 * TC)
        add("sv_ln", TC)
        add("ss_ln", TC)

        if dbg == 2:
            def dbg_vec(ve):
                ve.wait_ge(g["sd_xnt"], cur("sd_xnt"))
                ve.tensor_copy(out=xnf32[:], in_=f_xn2fm[:]).then_inc(g["sv_out"], 1)
            def dbg_sync(sy):
                sy.wait_ge(g["sv_wd"], cur("sv_wd"))
                sy.wait_ge(g["sv_out"], 1)
                sy.dma_start(out=d_r1, in_=acc[:]).then_inc(g["sd_out"], 16)
                sy.dma_start(out=d_wd, in_=wdense[:]).then_inc(g["sd_out"], 16)
                sy.dma_start(out=d_xn, in_=xnf32[:]).then_inc(g["sd_out"], 16)
                sy.dma_start(out=d_lg, in_=lgd[:]).then_inc(g["sd_out"], 16)
                sy.wait_ge(g["sd_out"], cur("sd_out") + 16 * 4)
            xnf32 = at("xnf32", [128, 8, S], F32, 32768)
            blk.vector(dbg_vec)
            blk.sync(dbg_sync)
            es.close()
            return nc

        W10 = cur("sd_w1"); W20 = cur("sd_w2"); H0 = cur("st_h")
        SH0 = cur("ss_h"); Y0 = cur("st_y"); ACC0 = cur("sv_acc")
        WD1 = cur("sv_wd"); XNT1 = cur("sd_xnt"); OP1 = cur("st_op")

        def f2_sync(sy, s=s, W10=W10, H0=H0, OP1=OP1):
            sy.wait_ge(g["st_op"], OP1)
            nblk = 0
            for e in range(E):
                for ts in range(TS):
                    for fb in range(FBLK):
                        if nblk >= 2:
                            sy.wait_ge(g["st_h"], H0 + 4 * nblk - 4)
                        sy.dma_start(out=f_w1[:, nblk % 2, :, :],
                                     in_=w1s[e, fb]).then_inc(g["sd_w1"], 16)
                        nblk += 1

        def f2_gp(gp, s=s, Y0=Y0, WD1=WD1, XNT1=XNT1):
            gp.wait_ge(g["sv_wd"], WD1)
            gp.wait_ge(g["sd_xnt"], XNT1)
            nfc = 0
            for e in range(E):
                for ts in range(TS):
                    for fc in range(FC):
                        if nfc >= 4:
                            gp.wait_ge(g["st_y"], Y0 + nfc - 3)
                        gp.dma_start(out=f_w2[:, nfc % 4, :],
                                     in_=w2s[e, fc]).then_inc(g["sd_w2"], 16)
                        nfc += 1

        def f2_te(te, s=s, W10=W10, W20=W20, H0=H0, SH0=SH0, Y0=Y0,
                  ACC0=ACC0, WD1=WD1, XNT1=XNT1):
            te.wait_ge(g["sv_wd"], WD1)
            te.wait_ge(g["sd_xnt"], XNT1)
            nblk = 0
            nfc = 0
            nsl = 0
            for e in range(E):
                for ts in range(TS):
                    for fb in range(FBLK):
                        te.wait_ge(g["sd_w1"], W10 + 16 * (nblk + 1))
                        bw = nblk % 2
                        for fc4 in range(4):
                            fc = fb * 4 + fc4
                            hb = fc % 2
                            nh = nsl * FC + fc
                            if nh >= 2:
                                te.wait_ge(g["ss_h"], SH0 + (nh - 2) // 2 + 1)
                            if fc < 2 and nsl >= 1:
                                te.wait_ge(g["sv_acc"], ACC0 + 4 * (nsl - 1) + 1)
                            for dc in range(8):
                                mmi = te.matmul(
                                    ps[:, hb, :],
                                    lhsT=f_w1[:, bw, dc,
                                              fc4 * 128:(fc4 + 1) * 128],
                                    rhs=f_xn2fm[:, dc, ts * 512:(ts + 1) * 512],
                                    start=(dc == 0), stop=(dc == 7))
                            mmi.then_inc(g["st_h"], 1)
                        nblk += 1
                    te.wait_ge(g["ss_h"], SH0 + 16 * (nsl + 1))
                    if nsl >= 1:
                        te.wait_ge(g["sv_acc"], ACC0 + 4 * nsl)
                    for fc in range(FC):
                        te.wait_ge(g["sd_w2"], W20 + 16 * (nfc + 1))
                        b4 = nfc % 4
                        for tc4 in range(4):
                            for dh in range(2):
                                mmi = te.matmul(
                                    ps[:, 2 * tc4 + dh, :],
                                    lhsT=f_h[:, fc, tc4 * 128:(tc4 + 1) * 128],
                                    rhs=f_w2[:, b4, dh * 512:(dh + 1) * 512],
                                    start=(fc == 0), stop=(fc == FC - 1))
                        mmi.then_inc(g["st_y"], 1)
                        nfc += 1
                    nsl += 1

        def f2_sc(sc, s=s, H0=H0, SH0=SH0):
            nsl = 0
            for e in range(E):
                for ts in range(TS):
                    for fcp in range(FC // 2):
                        sc.wait_ge(g["st_h"], H0 + nsl * FC + 2 * fcp + 2)
                        sc.activation(out=f_h[:, 2 * fcp:2 * fcp + 2, :],
                                      in_=ps[:, 0:2, :],
                                      func=AF.Silu).then_inc(g["ss_h"], 1)
                    nsl += 1

        def f2_vec(ve, s=s, Y0=Y0, ACC0=ACC0):
            nsl = 0
            for e in range(E):
                for ts in range(TS):
                    ve.wait_ge(g["st_y"], Y0 + FC * (nsl + 1))
                    for tc4 in range(4):
                        tc = ts * 4 + tc4
                        ve.scalar_tensor_tensor(
                            out=acc[:, tc, :],
                            in0=ps[:, 2 * tc4:2 * tc4 + 2, :].rearrange(
                                "p a b -> p (a b)"),
                            scalar=wdense[:, tc, e:e + 1], in1=acc[:, tc, :],
                            op0=ALU.mult, op1=ALU.add).then_inc(g["sv_acc"], 1)
                    nsl += 1

        blk.sync(f2_sync)
        blk.gpsimd(f2_gp)
        blk.tensor(f2_te)
        blk.scalar(f2_sc)
        blk.vector(f2_vec)

        add("sd_w1", 16 * E * TS * FBLK)
        add("sd_w2", 16 * E * TS * FC)
        add("st_h", E * TS * FC)
        add("ss_h", E * TS * FC // 2)
        add("st_y", E * TS * FC)
        add("sv_acc", E * TS * 4)

        if dbg == 3:
            def dbg_sync(sy):
                sy.wait_ge(g["sv_acc"], cur("sv_acc"))
                sy.dma_start(out=d_acc2, in_=acc[:]).then_inc(g["sd_out"], 16)
                sy.wait_ge(g["sd_out"], cur("sd_out") + 16)
            blk.sync(dbg_sync)
            es.close()
            return nc

        OUT0 = cur("sv_out"); DOUT0 = cur("sd_out")
        LNV2 = cur("sv_ln"); LNS2 = cur("ss_ln"); ACC1 = cur("sv_acc")

        def f3_vec(ve, s=s, OUT0=OUT0, DOUT0=DOUT0, LNS2=LNS2, ACC1=ACC1):
            ve.wait_ge(g["sv_acc"], ACC1)
            for tc in range(TC):
                ln_stats(ve, acc[:, tc, :], f_lnscr)

        def f3_sc(sc, s=s, LNV2=LNV2, DOUT0=DOUT0):
            for tc in range(TC):
                if tc >= 2:
                    sc.wait_ge(g["sd_out"], DOUT0 + 16 * (tc - 1))
                ln_sc_apply(sc, LNV2 + tc + 1, acc[:, tc, :],
                            f_out[:, tc % 2, :]).then_inc(g["sv_out"], 1)

        def f3_sync(sy, s=s, OUT0=OUT0):
            for tc in range(TC):
                sy.wait_ge(g["sv_out"], OUT0 + tc + 1)
                sy.dma_start(out=out[s, tc * 128:(tc + 1) * 128, :],
                             in_=f_out[:, tc % 2, :]).then_inc(g["sd_out"], 16)

        blk.vector(f3_vec)
        blk.scalar(f3_sc)
        blk.sync(f3_sync)

        add("sv_out", TC)
        add("sd_out", 16 * TC)
        add("sv_ln", TC)

    def fin_sync(sy):
        sy.wait_ge(g["sd_out"], cur("sd_out"))

    blk.sync(fin_sync)
    es.close()
    return nc


_NC_CACHE = []


def kernel(L0_attn_x, L0_attn_qkv_weight, L0_attn_qkv_weight_mask,
           L0_attn_out_weight, L0_attn_out_weight_mask, L0_attn_out_bias,
           L0_attn_out_bias_mask, L0_ffn_gamma, L0_ffn_beta, L0_ffn_router_w,
           L0_ffn_W1, L0_ffn_W2):
    x = np.ascontiguousarray(np.asarray(L0_attn_x, dtype=np.float32))
    com = {
        "wqkv": np.ascontiguousarray(np.asarray(L0_attn_qkv_weight, np.float32)),
        "mqkv": np.ascontiguousarray(np.asarray(L0_attn_qkv_weight_mask, np.float32)),
        "wo": np.ascontiguousarray(np.asarray(L0_attn_out_weight, np.float32)),
        "mo": np.ascontiguousarray(np.asarray(L0_attn_out_weight_mask, np.float32)),
        "bo": np.ascontiguousarray(np.asarray(L0_attn_out_bias, np.float32).reshape(1, D)),
        "mbo": np.ascontiguousarray(np.asarray(L0_attn_out_bias_mask, np.float32).reshape(1, D)),
        "gamma": np.ascontiguousarray(np.asarray(L0_ffn_gamma, np.float32).reshape(1, D)),
        "beta": np.ascontiguousarray(np.asarray(L0_ffn_beta, np.float32).reshape(1, D)),
        "rw": np.ascontiguousarray(np.asarray(L0_ffn_router_w, np.float32)),
        "w1": np.ascontiguousarray(np.asarray(L0_ffn_W1, np.float32)),
        "w2": np.ascontiguousarray(np.asarray(L0_ffn_W2, np.float32)),
    }
    if not _NC_CACHE:
        _NC_CACHE.append(build_core())
    nc = _NC_CACHE[0]
    in_maps = [dict(com, x=np.ascontiguousarray(x[2 * c:2 * c + 2]))
               for c in range(8)]
    res = run_bass_kernel_spmd(nc, in_maps, core_ids=list(range(8)))
    return np.concatenate([np.asarray(r["out"]) for r in res.results], axis=0)
